# version 02
# baseline (speedup 1.0000x reference)
import contextlib
import numpy as np
import concourse.bass as bass
import concourse.mybir as mybir
from concourse.bass_utils import run_bass_kernel_spmd
from concourse.bass_types import AP

F32 = mybir.dt.float32
F16 = mybir.dt.float16
I32 = mybir.dt.int32
AF = mybir.ActivationFunctionType
ALU = mybir.AluOpType
AX = mybir.AxisListType

C = 1024
NH = 16
HID = 2048
FF = 3072
INV_QMAX = 1.0 / 32767.0
RMS_K = float(1024 ** 0.5 + 1e-06)
DEBUG = False

_ENG_OPS = {
    "tensor_copy", "tensor_tensor", "tensor_add", "tensor_sub", "tensor_mul",
    "tensor_max", "tensor_scalar", "tensor_single_scalar", "tensor_scalar_add",
    "tensor_scalar_sub", "tensor_scalar_mul", "tensor_scalar_max",
    "tensor_scalar_min", "tensor_reduce", "reduce_max", "reduce_sum",
    "scalar_tensor_tensor", "select", "transpose", "bn_stats", "bn_aggr",
    "reciprocal", "memset", "memzero", "activation", "copy", "add", "mul",
    "sqrt", "square", "sign", "iota", "matmul", "collective_compute",
}


def _ap_names(args, kwargs):
    ins, outs = [], []

    def walk(v, into):
        if isinstance(v, AP):
            into.append(v.tensor.name)
        elif isinstance(v, (list, tuple)):
            for x in v:
                walk(x, into)

    for i, v in enumerate(args):
        walk(v, ins if i > 0 else outs)
    for k, v in kwargs.items():
        walk(v, outs if k in ("out", "accum_out", "outs") else ins)
    return ins, outs


class Sem:
    def __init__(self, h, name):
        self.h = h
        self.name = name
        self.val = 0


class Eng:

    def __init__(self, eng, ctx):
        self._eng = eng
        self._ctx = ctx
        self._touched = set()
        self._written = set()

    @property
    def counting(self):
        return self._ctx["counting"]

    def drain(self):
        if not self.counting:
            self._eng.drain()
        self._touched = set()
        self._written = set()

    def fence(self):
        self._touched = set()
        self._written = set()

    def mark(self, sem, key):
        if self.counting:
            assert key not in self._ctx["marks"], f"duplicate mark {key}"
            self._ctx["marks"][key] = (sem, sem.val)
        else:
            rec = self._ctx["marks"].get(key)
            assert rec is not None and rec[0] is sem and rec[1] == sem.val, \
                f"mark drift at {key}: recorded {rec and rec[1]} vs emit {sem.val}"

    def inc(self, inst, sem, n=1):
        if not self.counting:
            inst.then_inc(sem.h, n)
        sem.val += n

    def wait(self, key):
        if self.counting:
            return
        sem, v = self._ctx["marks"][key]
        self._eng.wait_ge(sem.h, v)

    def wait_sem_now(self, sem):
        if self.counting:
            return
        self._eng.wait_ge(sem.h, sem.val)

    def dma(self, out, in_, sem):
        if self.counting:
            sem.val += 16
            return None
        self._eng.wait_ge(sem.h, sem.val)
        inst = self._eng.dma_start(out=out, in_=in_)
        inst.then_inc(sem.h, 16)
        sem.val += 16
        return inst

    def cc(self, in_ap, out_ap, sem):
        if self.counting:
            sem.val += 1
            return None
        inst = self._eng.collective_compute(
            "AllReduce", ALU.max, replica_groups=[list(range(8))],
            ins=[in_ap], outs=[out_ap])
        inst.then_inc(sem.h, 1)
        sem.val += 1
        return inst

    def __getattr__(self, name):
        if name.startswith("_"):
            raise AttributeError(name)
        if name not in _ENG_OPS:
            fn = getattr(self._eng, name)
            if self.counting:
                return lambda *a, **k: None
            return fn

        def wrapped(*args, **kwargs):
            if self.counting:
                return None
            ins, outs = _ap_names(args, kwargs)
            if any(n in self._written for n in ins) or any(n in self._touched for n in outs):
                self.drain()
            r = getattr(self._eng, name)(*args, **kwargs)
            self._touched.update(ins + outs)
            self._written.update(outs)
            return r

        return wrapped


R_SQ, R_RC, R_RK, R_M, R_EX = 0, 512, 1024, 1536, 2048
R_VAR, R_SD, R_R, R_MR = R_SQ, R_RC, R_RK, R_EX


def build_program(T):
    NJ = T // 512
    NJ2 = T // 256
    TJ = 256
    nc = bass.Bass("TRN2", target_bir_lowering=False, debug=False)
    scratch_kind = "ExternalOutput" if DEBUG else "Internal"

    x_in = nc.dram_tensor("x", [C, T], F32, kind="ExternalInput").ap()
    wqkv_d = nc.dram_tensor("wqkv", [C, 3 * C], F32, kind="ExternalInput").ap()
    wproj_d = nc.dram_tensor("wproj", [C, C], F32, kind="ExternalInput").ap()
    wc1_d = nc.dram_tensor("wc1", [C, HID], F32, kind="ExternalInput").ap()
    wc2_d = nc.dram_tensor("wc2", [HID, C], F32, kind="ExternalInput").ap()
    wup_d = nc.dram_tensor("wup", [FF, C], F32, kind="ExternalInput").ap()
    wgate_d = nc.dram_tensor("wgate", [FF, C], F32, kind="ExternalInput").ap()
    wdown_d = nc.dram_tensor("wdown", [C, FF], F32, kind="ExternalInput").ap()
    vecs_d = nc.dram_tensor("vecs", [128, 48], F32, kind="ExternalInput").ap()
    c1b_d = nc.dram_tensor("c1b", [128, 16], F32, kind="ExternalInput").ap()
    y_out = nc.dram_tensor("y", [C, T], F32, kind="ExternalOutput").ap()

    def scratch(name, shape, dt):
        return nc.dram_tensor(name, shape, dt, kind=scratch_kind).ap()

    d_phiq = scratch("d_phiq", [C, T], F32)
    d_x2 = scratch("d_x2", [C, T], F32)
    d_y1 = scratch("d_y1", [C, T], F32)
    d_U = scratch("d_U", [FF, T], F32)
    d_G = scratch("d_G", [FF, T], F32)
    d_mu = scratch("d_mu", [FF, T], I32)
    d_P = scratch("d_P", [FF, T], F32)
    d_Q = scratch("d_Q", [FF, T], F32)
    d_wqh = scratch("d_wqh", [C, FF], F16)
    d_wql = scratch("d_wql", [C, FF], F16)
    d_mx = scratch("d_mx", [C, T], I32)
    cc_in = [nc.dram_tensor(f"cc_in{i}", [1, 2], F32).ap() for i in range(3)]
    cc_out = [nc.dram_tensor(f"cc_out{i}", [1, 2], F32).ap() for i in range(3)]

    eye32_c = nc.inline_tensor(np.eye(128, dtype=np.float32), name="eye32c").ap()
    eye16_c = nc.inline_tensor(np.eye(128, dtype=np.float16), name="eye16c").ap()
    onesr_c = nc.inline_tensor(np.ones((1, 128), np.float32), name="onesrc").ap()
    onesc_c = nc.inline_tensor(np.ones((128, 1), np.float32), name="onescc").ap()
    cbias_c = nc.inline_tensor(np.full((128, 1), 1e-5, np.float32), name="cbiasc").ap()
    eA = np.zeros((32, 8 * 128), np.float32)
    for ci in range(8):
        for p in range(128):
            eA[4 * ci + p // 32, ci * 128 + p] = 1.0
    eexpA_c = nc.inline_tensor(eA, name="eexpAc").ap()
    eB = np.zeros((96, 24 * 128), np.float32)
    for ci in range(24):
        for p in range(128):
            eB[4 * ci + p // 32, ci * 128 + p] = 1.0
    eexpB_c = nc.inline_tensor(eB, name="eexpBc").ap()

    es = contextlib.ExitStack()

    def sbuf(name, shape, dt):
        return es.enter_context(nc.sbuf_tensor(name, shape, dt))

    def mksem(name):
        return Sem(es.enter_context(nc.semaphore(name)), name)

    with es, nc.Block() as block:
        vec_sb = sbuf("vec_sb", [128, 48], F32)
        gpb_sb = sbuf("gpb_sb", [128, 16], F32)
        c1b_sb = sbuf("c1b_sb", [128, 16], F32)
        eye32_sb = sbuf("eye32_sb", [128, 128], F32)
        eye16_sb = sbuf("eye16_sb", [128, 128], F16)
        onesr_sb = sbuf("onesr_sb", [1, 128], F32)
        onesc_sb = sbuf("onesc_sb", [128, 1], F32)
        eexpA_sb = sbuf("eexpA_sb", [32, 8 * 128], F32)
        cbias_sb = sbuf("cbias_sb", [128, 1], F32)
        kvext = sbuf("kvext", [128, 8 * 65], F32)
        scal = sbuf("scal", [1, 32], F32)
        scbc = sbuf("scbc", [128, 8], F32)
        rowb = sbuf("rowb", [1, 5 * 512], F32)
        amaxc = sbuf("amaxc", [128, 8], F32)

        sLD = mksem("sLD")
        sSP = mksem("sSP")
        sMM = mksem("sMM")
        sEV = mksem("sEV")
        sVE = mksem("sVE")
        sGP = mksem("sGP")
        sPB = mksem("sPB")
        ALL_SEMS = [sLD, sSP, sMM, sEV, sVE, sGP, sPB]

        ctx = {"counting": False, "marks": {}}
        EV = Eng(nc.vector, ctx)
        EC = Eng(nc.scalar, ctx)
        EG = Eng(nc.gpsimd, ctx)
        ET = Eng(nc.tensor, ctx)
        EY = Eng(nc.sync, ctx)

        def vsl(row, ci):
            return vec_sb[:, row * 8 + ci: row * 8 + ci + 1]

        def row(slot, n=512):
            return rowb[0:1, slot:slot + n]

        def sc1(i):
            return scal[0:1, i:i + 1]

        def run_phase(fns):
            snap = {s.name: s.val for s in ALL_SEMS}
            ctx["counting"] = True
            for _, fn in fns:
                fn()
                for e in (EV, EC, EG, ET, EY):
                    e.fence()
            for s in ALL_SEMS:
                s.val = snap[s.name]
            ctx["counting"] = False
            deco = {"sync": block.sync, "vector": block.vector, "scalar": block.scalar,
                    "gpsimd": block.gpsimd, "tensor": block.tensor}
            for name, fn in fns:
                deco[name](lambda e, fn=fn: fn())
            tgt = sPB.val + 5
            for dec, eng, qsem in ((block.sync, nc.sync, sLD),
                                   (block.vector, nc.vector, None),
                                   (block.scalar, nc.scalar, None),
                                   (block.gpsimd, nc.gpsimd, sSP),
                                   (block.tensor, nc.tensor, None)):
                def bfn(e, eng=eng, tgt=tgt, qsem=qsem):
                    if qsem is not None:
                        eng.wait_ge(qsem.h, qsem.val)
                    if eng is not nc.sync:
                        eng.drain()
                    eng.nop().then_inc(sPB.h, 1)
                    eng.wait_ge(sPB.h, tgt)
                dec(bfn)
            sPB.val = tgt
            for e in (EV, EC, EG, ET, EY):
                e.fence()
            ctx["marks"] = {}

        sc_qkv = contextlib.ExitStack()
        wqkv_h = sc_qkv.enter_context(nc.sbuf_tensor("wqkv_h", [128, 8 * 3 * C], F16))
        wqkv_l = sc_qkv.enter_context(nc.sbuf_tensor("wqkv_l", [128, 8 * 3 * C], F16))
        def emit_wprep(tag, plan, wstg, wtmp, stg_cols, after_consts):
            items = []
            for (src, nch, ncols, fold, dh, dl) in plan:
                for ci in range(nch):
                    items.append((src, ci, ncols, fold, dh, dl))

            def w_sync():
                e = EY
                for nb, (src, ci, ncols, fold, dh, dl) in enumerate(items):
                    if nb >= 2:
                        e.wait((tag, "lo", nb - 2))
                    e.dma(wstg[:, (nb % 2) * stg_cols:(nb % 2) * stg_cols + ncols],
                          src[ci * 128:(ci + 1) * 128, :], sLD)
                    e.mark(sLD, (tag, "w", nb))

            def w_scalar():
                e = EC
                if after_consts:
                    e.wait("p0consts")
                for nb, (src, ci, ncols, fold, dh, dl) in enumerate(items):
                    e.wait((tag, "w", nb))
                    if nb >= 1:
                        e.wait((tag, "lo", nb - 1))
                    stg = wstg[:, (nb % 2) * stg_cols:(nb % 2) * stg_cols + ncols]
                    sc = vsl(fold, ci) if fold is not None else 1.0
                    e.activation(dh[:, ci * ncols:(ci + 1) * ncols], stg,
                                 AF.Identity, scale=sc)
                    i = e.activation(wtmp[:, 0:ncols], stg, AF.Identity, scale=sc)
                    e.inc(i, sEV, 1)
                    e.mark(sEV, (tag, "hi", nb))

            def w_vector():
                e = EV
                for nb, (src, ci, ncols, fold, dh, dl) in enumerate(items):
                    e.wait((tag, "hi", nb))
                    i = e.tensor_tensor(dl[:, ci * ncols:(ci + 1) * ncols],
                                        wtmp[:, 0:ncols],
                                        dh[:, ci * ncols:(ci + 1) * ncols],
                                        ALU.subtract)
                    e.inc(i, sVE, 1)
                    e.mark(sVE, (tag, "lo", nb))

            return w_sync, w_scalar, w_vector

        with nc.sbuf_tensor("wstg", [128, 2 * 3 * C], F32) as wstg, \
             nc.sbuf_tensor("wtmp", [128, 3 * C], F32) as wtmp:

            ws0, wc0, wv0 = emit_wprep(
                "p0", [(wqkv_d, 8, 3 * C, 0, wqkv_h, wqkv_l)],
                wstg, wtmp, 3 * C, True)

            def p0_sync():
                e = EY
                e.dma(eye32_sb[:, :], eye32_c[:, :], sLD)
                e.dma(eye16_sb[:, :], eye16_c[:, :], sLD)
                e.dma(onesr_sb[:, :], onesr_c[:, :], sLD)
                e.dma(onesc_sb[:, :], onesc_c[:, :], sLD)
                e.dma(eexpA_sb[:, :], eexpA_c[:, :], sLD)
                e.dma(cbias_sb[:, :], cbias_c[:, :], sLD)
                e.dma(vec_sb[:, :], vecs_d[:, :], sLD)
                e.dma(c1b_sb[:, :], c1b_d[:, :], sLD)
                e.mark(sLD, "p0consts")
                ws0()

            def p0_vector():
                e = EV
                e.wait("p0consts")
                e.tensor_tensor(gpb_sb[:, 0:8], vec_sb[:, 16:24], vec_sb[:, 24:32], ALU.mult)
                e.tensor_tensor(gpb_sb[:, 8:16], vec_sb[:, 16:24], vec_sb[:, 32:40], ALU.mult)
                wv0()

            run_phase([("sync", p0_sync), ("scalar", wc0), ("vector", p0_vector)])

        with nc.sbuf_tensor("xbuf", [128, 8 * 512], F32) as xbuf, \
             nc.sbuf_tensor("sh1", [128, 8 * 512], F32) as sh1, \
             nc.sbuf_tensor("r1bc", [128, 512], F32) as r1bc, \
             nc.sbuf_tensor("xnh", [128, 8 * 512], F16) as xnh, \
             nc.sbuf_tensor("xnl", [128, 8 * 512], F16) as xnl, \
             nc.sbuf_tensor("phib", [128, 2 * 512], F32) as phib, \
             nc.sbuf_tensor("k_sb", [128, 1024], F32) as k_sb, \
             nc.sbuf_tensor("v_sb", [128, 1024], F32) as v_sb, \
             nc.sbuf_tensor("qtmp", [128, 512], F32) as qtmp, \
             nc.psum_tensor("ps_misc", [128, 512], F32) as ps_misc, \
             nc.psum_tensor("ps_q", [128, 1024], F32) as ps_q, \
             nc.psum_tensor("ps_kv", [128, 1024], F32) as ps_kv, \
             nc.psum_tensor("ps_acc", [128, 1024], F32) as ps_acc, \
             nc.psum_tensor("ps_ks", [128, 16], F32) as ps_ks:

            def wq_sl(hi, k, a, b):
                w = wqkv_h if hi else wqkv_l
                return w[:, k * 3 * C + a:k * 3 * C + b]

            def p1_sync():
                e = EY
                x3 = x_in.rearrange("(c p) t -> p c t", p=128)
                for j in range(NJ):
                    if j >= 1:
                        e.wait(("p1xn", j - 1))
                    e.dma(xbuf[:, :].rearrange("p (c t) -> p c t", t=512),
                          x3[:, :, 512 * j:512 * (j + 1)], sLD)
                    e.mark(sLD, ("p1x", j))

            def p1_scalar():
                e = EC
                for j in range(NJ):
                    e.wait(("p1x", j))
                    if j >= 1:
                        e.wait(("p1xn", j - 1))
                    i = e.activation(sh1[:, :], xbuf[:, :], AF.Square)
                    e.inc(i, sEV, 1)
                    e.mark(sEV, ("p1sq", j))
                    e.wait(("p1ss", j))
                    i = e.activation(row(R_SQ), ps_misc[0:1, 0:512], AF.Sqrt)
                    e.inc(i, sEV, 1)
                    e.mark(sEV, ("p1sqrt", j))
                    for co in range(8):
                        e.wait(("p1mmq", j, co))
                        gg = j * 8 + co
                        if gg >= 2:
                            e.wait(("p1phisp", (gg - 2) // 8, (gg - 2) % 8))
                        pq = ps_q[:, (co % 2) * 512:(co % 2) * 512 + 512]
                        e.activation(qtmp[:, :], pq, AF.Square)
                        i = e.activation(phib[:, (co % 2) * 512:(co % 2) * 512 + 512],
                                         qtmp[:, :], AF.Exp, scale=-0.5)
                        e.inc(i, sEV, 1)
                        e.mark(sEV, ("p1phi", j, co))
                    for tt in range(4):
                        for fo in range(4):
                            e.wait(("p1mmkv", j, tt, fo))
                            if fo == 0 and (tt > 0 or j > 0):
                                e.wait(("p1kvb", j if tt > 0 else j - 1,
                                        tt - 1 if tt > 0 else 3))
                            pkv = ps_kv[:, (fo % 2) * 512:(fo % 2) * 512 + 512]
                            if fo < 2:
                                e.activation(qtmp[:, :], pkv, AF.Square)
                                i = e.activation(k_sb[:, fo * 512:(fo + 1) * 512],
                                                 qtmp[:, :], AF.Exp, scale=-0.5)
                            else:
                                i = e.copy(v_sb[:, (fo - 2) * 512:(fo - 1) * 512], pkv)
                            e.inc(i, sEV, 1)
                            e.mark(sEV, ("p1kvev", j, tt, fo))

            def p1_vector():
                e = EV
                for j in range(NJ):
                    e.wait(("p1sqrt", j))
                    e.reciprocal(row(R_RC), row(R_SQ))
                    i = e.tensor_scalar(row(R_RK), row(R_RC), RMS_K, None, ALU.mult)
                    e.inc(i, sVE, 1)
                    e.mark(sVE, ("p1r1row", j))
                    e.wait(("p1r1bcmm", j))
                    i = e.tensor_copy(r1bc[:, :], ps_misc[:, :])
                    e.inc(i, sVE, 1)
                    e.mark(sVE, ("p1r1bc", j))
                    r1b3 = r1bc[:, :].unsqueeze(1).broadcast_to([128, 8, 512])
                    x4 = xbuf[:, :].rearrange("p (c t) -> p c t", t=512)
                    e.tensor_tensor(xnh[:, :].rearrange("p (c t) -> p c t", t=512),
                                    x4, r1b3, ALU.mult)
                    e.tensor_tensor(sh1[:, :].rearrange("p (c t) -> p c t", t=512),
                                    x4, r1b3, ALU.mult)
                    i = e.tensor_tensor(xnl[:, :], sh1[:, :], xnh[:, :], ALU.subtract)
                    e.inc(i, sVE, 1)
                    e.mark(sVE, ("p1xn", j))

            def p1_tensor():
                e = ET
                for j in range(NJ):
                    e.wait(("p1sq", j))
                    if j >= 1:
                        e.wait(("p1r1bc", j - 1))
                    for k in range(8):
                        i = e.matmul(ps_misc[0:1, 0:512], onesc_sb[:, :],
                                     sh1[:, k * 512:(k + 1) * 512],
                                     start=(k == 0), stop=(k == 7))
                    e.inc(i, sMM, 1)
                    e.mark(sMM, ("p1ss", j))
                    e.wait(("p1sqrt", j))
                    e.wait(("p1r1row", j))
                    i = e.matmul(ps_misc[:, :], onesr_sb[0:1, :], row(R_RK),
                                 start=True, stop=True)
                    e.inc(i, sMM, 1)
                    e.mark(sMM, ("p1r1bcmm", j))
                    e.wait(("p1xn", j))
                    for co in range(8):
                        gg = j * 8 + co
                        if gg >= 2:
                            e.wait(("p1phi", (gg - 2) // 8, (gg - 2) % 8))
                        pq = ps_q[:, (co % 2) * 512:(co % 2) * 512 + 512]
                        n = 0
                        for (ah, bh) in ((1, 1), (1, 0), (0, 1)):
                            for k in range(8):
                                xs = (xnh if bh else xnl)[:, k * 512:(k + 1) * 512]
                                i = e.matmul(pq, wq_sl(ah, k, co * 128, (co + 1) * 128),
                                             xs, start=(n == 0), stop=(n == 23))
                                n += 1
                        e.inc(i, sMM, 1)
                        e.mark(sMM, ("p1mmq", j, co))
                    for tt in range(4):
                        for fo in range(4):
                            ff = j * 16 + tt * 4 + fo
                            if ff >= 2:
                                pf = ff - 2
                                e.wait(("p1kvev", pf // 16, (pf % 16) // 4, (pf % 16) % 4))
                            pkv = ps_kv[:, (fo % 2) * 512:(fo % 2) * 512 + 512]
                            n = 0
                            for (ah, bh) in ((1, 1), (1, 0), (0, 1)):
                                for k in range(8):
                                    xs = (xnh if ah else xnl)[:, k * 512 + tt * 128:
                                                              k * 512 + tt * 128 + 128]
                                    ws_ = wq_sl(bh, k, C + fo * 512, C + (fo + 1) * 512)
                                    i = e.matmul(pkv, xs, ws_, start=(n == 0), stop=(n == 23))
                                    n += 1
                            e.inc(i, sMM, 1)
                            e.mark(sMM, ("p1mmkv", j, tt, fo))
                        for fo in range(4):
                            e.wait(("p1kvev", j, tt, fo))
                        first = (j == 0 and tt == 0)
                        last = (j == NJ - 1 and tt == 3)
                        for p in range(8):
                            e.matmul(ps_acc[:, p * 128:(p + 1) * 128],
                                     k_sb[:, p * 128:(p + 1) * 128],
                                     v_sb[:, p * 128:(p + 1) * 128],
                                     start=(first and p % 4 == 0),
                                     stop=(last and p % 4 == 3))
                            i = e.matmul(ps_ks[:, p:p + 1],
                                         k_sb[:, p * 128:(p + 1) * 128],
                                         onesc_sb[:, :], start=(first and p == 0),
                                         stop=(last and p == 7))
                        e.inc(i, sMM, 1)
                        e.mark(sMM, ("p1kvb", j, tt))

            def p1_gpsimd():
                e = EG
                for j in range(NJ):
                    for co in range(8):
                        e.wait(("p1phi", j, co))
                        e.dma(d_phiq[co * 128:(co + 1) * 128, 512 * j:512 * (j + 1)],
                              phib[:, (co % 2) * 512:(co % 2) * 512 + 512], sSP)
                        e.mark(sSP, ("p1phisp", j, co))

            run_phase([("sync", p1_sync), ("scalar", p1_scalar), ("vector", p1_vector),
                       ("tensor", p1_tensor), ("gpsimd", p1_gpsimd)])

            def p15_vector():
                e = EV
                for p in range(8):
                    e.tensor_copy(kvext[0:64, p * 65:p * 65 + 64],
                                  ps_acc[0:64, p * 128:p * 128 + 64])
                    e.tensor_copy(kvext[64:128, p * 65:p * 65 + 64],
                                  ps_acc[64:128, p * 128 + 64:p * 128 + 128])
                    e.tensor_copy(kvext[:, p * 65 + 64:p * 65 + 65], ps_ks[:, p:p + 1])

            run_phase([("vector", p15_vector)])

        sc_qkv.close()

        sc_proj = contextlib.ExitStack()
        wproj_h = sc_proj.enter_context(nc.sbuf_tensor("wproj_h", [128, 8 * C], F16))
        wproj_l = sc_proj.enter_context(nc.sbuf_tensor("wproj_l", [128, 8 * C], F16))
        with nc.sbuf_tensor("wstgp", [128, 2 * C], F32) as wstgp, \
             nc.sbuf_tensor("wtmpp", [128, C], F32) as wtmpp:
            pws, pwc, pwv = emit_wprep(
                "p2p", [(wproj_d, 8, C, None, wproj_h, wproj_l)],
                wstgp, wtmpp, C, False)
            run_phase([("sync", pws), ("scalar", pwc), ("vector", pwv)])

        with nc.sbuf_tensor("qbuf", [128, 8 * 512], F32) as qbuf, \
             nc.sbuf_tensor("atile", [128, 1024], F32) as atile, \
             nc.sbuf_tensor("zt", [128, 16], F32) as zt, \
             nc.sbuf_tensor("athi", [128, 8 * 512], F16) as athi, \
             nc.sbuf_tensor("atlo", [128, 8 * 512], F16) as atlo, \
             nc.sbuf_tensor("xck", [128, 2 * 512], F32) as xck, \
             nc.sbuf_tensor("ptmp", [128, 2 * 512], F32) as ptmp, \
             nc.sbuf_tensor("x2b", [128, 2 * 512], F32) as x2b, \
             nc.psum_tensor("ps_ap", [128, 1024], F32) as ps_ap, \
             nc.psum_tensor("ps_tr", [128, 1024], F32) as ps_tr, \
             nc.psum_tensor("ps_pr", [128, 2 * 512], F32) as ps_pr:

            def p2_sync():
                e = EY
                ph3 = d_phiq.rearrange("(c p) t -> p c t", p=128)
                x3 = x_in.rearrange("(c p) t -> p c t", p=128)
                for j in range(NJ):
                    if j >= 1:
                        e.wait(("p2apdone", j - 1))
                    e.dma(qbuf[:, :].rearrange("p (c t) -> p c t", t=512),
                          ph3[:, :, 512 * j:512 * (j + 1)], sLD)
                    e.mark(sLD, ("p2q", j))
                    for oc in range(8):
                        qq = j * 8 + oc
                        if qq >= 2:
                            e.wait(("p2x2v", (qq - 2) // 8, (qq - 2) % 8))
                        e.dma(xck[:, (oc % 2) * 512:(oc % 2) * 512 + 512],
                              x3[:, oc, 512 * j:512 * (j + 1)], sLD)
                        e.mark(sLD, ("p2xc", j, oc))

            def p2_tensor():
                e = ET
                for j in range(NJ):
                    e.wait(("p2q", j))
                    for tt in range(4):
                        for h in range(NH):
                            g = h // 4
                            gg = j * 16 + tt * 4 + g
                            if h % 4 == 0 and gg >= 2:
                                pz = gg - 2
                                e.wait(("p2zev", pz // 16, (pz % 16) // 4, (pz % 16) % 4))
                            pr, hf = h // 2, h % 2
                            qsl = qbuf[64 * hf:64 * (hf + 1),
                                       pr * 512 + tt * 128:pr * 512 + tt * 128 + 128]
                            ksl = kvext[64 * hf:64 * (hf + 1), pr * 65:(pr + 1) * 65]
                            i = e.matmul(ps_ap[:, (g % 2) * 512 + (h % 4) * 65:
                                               (g % 2) * 512 + (h % 4) * 65 + 65],
                                         qsl, ksl, start=True, stop=True)
                            if h % 4 == 3:
                                e.inc(i, sMM, 1)
                                e.mark(sMM, ("p2ap", j, tt, g))
                        for p in range(8):
                            e.wait(("p2at", j, tt, p // 2))
                            pp = (j * 4 + tt) * 8 + p
                            if pp >= 2:
                                q_, p_ = divmod(pp - 2, 8)
                                j_, tt_ = divmod(q_, 4)
                                e.wait(("p2trlo", j_, tt_, p_))
                            i = e.transpose(ps_tr[:, (p % 2) * 512:(p % 2) * 512 + 128],
                                            atile[:, p * 128:(p + 1) * 128], eye32_sb[:, :])
                            e.inc(i, sMM, 1)
                            e.mark(sMM, ("p2tr", j, tt, p))
                    e.mark(sMM, ("p2apdone", j))
                    e.wait(("p2trall", j))
                    for oc in range(8):
                        qq = j * 8 + oc
                        if qq >= 2:
                            e.wait(("p2ev", (qq - 2) // 8, (qq - 2) % 8))
                        pp_ = ps_pr[:, (oc % 2) * 512:(oc % 2) * 512 + 512]
                        n = 0
                        for (ah, bh) in ((1, 1), (1, 0), (0, 1)):
                            for k in range(8):
                                w = (wproj_h if ah else wproj_l)[:, k * C + oc * 128:
                                                                 k * C + (oc + 1) * 128]
                                xs = (athi if bh else atlo)[:, k * 512:(k + 1) * 512]
                                i = e.matmul(pp_, w, xs, start=(n == 0), stop=(n == 23))
                                n += 1
                        e.inc(i, sMM, 1)
                        e.mark(sMM, ("p2pr", j, oc))

            def p2_vector():
                e = EV
                for j in range(NJ):
                    for tt in range(4):
                        for g in range(4):
                            e.wait(("p2ap", j, tt, g))
                            if not (j == 0 and tt == 0):
                                pj, ptt = (j, tt - 1) if tt > 0 else (j - 1, 3)
                                e.wait(("p2tr", pj, ptt, 2 * g + 1))
                            pa = ps_ap[:, (g % 2) * 512:(g % 2) * 512 + 260]
                            den = pa.rearrange("p (h w) -> p h w", w=65)[:, :, 64:65]
                            e.tensor_scalar(zt[:, g * 4:(g + 1) * 4],
                                            den.rearrange("p h w -> p (h w)"),
                                            1e-6, None, ALU.add)
                            e.reciprocal(zt[:, g * 4:(g + 1) * 4], zt[:, g * 4:(g + 1) * 4])
                            for hh in range(4):
                                h = g * 4 + hh
                                i = e.tensor_scalar(atile[:, h * 64:(h + 1) * 64],
                                                    pa[:, hh * 65:hh * 65 + 64],
                                                    zt[:, h:h + 1], None, ALU.mult)
                            e.inc(i, sVE, 1)
                            e.mark(sVE, ("p2zev", j, tt, g))
                            e.mark(sVE, ("p2at", j, tt, g))
                        for p in range(8):
                            e.wait(("p2trhi", j, tt, p))
                            dst = atlo[:, p * 512 + tt * 128:p * 512 + tt * 128 + 128]
                            src_h = athi[:, p * 512 + tt * 128:p * 512 + tt * 128 + 128]
                            i = e.tensor_tensor(dst,
                                                ps_tr[:, (p % 2) * 512:(p % 2) * 512 + 128],
                                                src_h, ALU.subtract)
                            e.inc(i, sVE, 1)
                            e.mark(sVE, ("p2trlo", j, tt, p))
                    e.mark(sVE, ("p2trall", j))
                    for oc in range(8):
                        e.wait(("p2ev", j, oc))
                        e.wait(("p2xc", j, oc))
                        qq = j * 8 + oc
                        if qq >= 2:
                            e.wait(("p2x2sp", (qq - 2) // 8, (qq - 2) % 8))
                        i = e.tensor_tensor(x2b[:, (oc % 2) * 512:(oc % 2) * 512 + 512],
                                            ptmp[:, (oc % 2) * 512:(oc % 2) * 512 + 512],
                                            xck[:, (oc % 2) * 512:(oc % 2) * 512 + 512],
                                            ALU.add)
                        e.inc(i, sVE, 1)
                        e.mark(sVE, ("p2x2v", j, oc))

            def p2_scalar():
                e = EC
                for j in range(NJ):
                    for tt in range(4):
                        for p in range(8):
                            e.wait(("p2tr", j, tt, p))
                            i = e.copy(athi[:, p * 512 + tt * 128:p * 512 + tt * 128 + 128],
                                       ps_tr[:, (p % 2) * 512:(p % 2) * 512 + 128])
                            e.inc(i, sEV, 1)
                            e.mark(sEV, ("p2trhi", j, tt, p))
                    for oc in range(8):
                        e.wait(("p2pr", j, oc))
                        qq = j * 8 + oc
                        if qq >= 2:
                            e.wait(("p2x2v", (qq - 2) // 8, (qq - 2) % 8))
                        i = e.activation(ptmp[:, (oc % 2) * 512:(oc % 2) * 512 + 512],
                                         ps_pr[:, (oc % 2) * 512:(oc % 2) * 512 + 512],
                                         AF.Identity, scale=vsl(2, oc),
                                         bias=gpb_sb[:, oc:oc + 1])
                        e.inc(i, sEV, 1)
                        e.mark(sEV, ("p2ev", j, oc))

            def p2_gpsimd():
                e = EG
                for j in range(NJ):
                    for oc in range(8):
                        e.wait(("p2x2v", j, oc))
                        e.dma(d_x2[oc * 128:(oc + 1) * 128, 512 * j:512 * (j + 1)],
                              x2b[:, (oc % 2) * 512:(oc % 2) * 512 + 512], sSP)
                        e.mark(sSP, ("p2x2sp", j, oc))

            run_phase([("sync", p2_sync), ("tensor", p2_tensor), ("vector", p2_vector),
                       ("scalar", p2_scalar), ("gpsimd", p2_gpsimd)])

        sc_proj.close()

        sc_c12 = contextlib.ExitStack()
        wc1_h = sc_c12.enter_context(nc.sbuf_tensor("wc1_h", [128, 8 * HID], F16))
        wc1_l = sc_c12.enter_context(nc.sbuf_tensor("wc1_l", [128, 8 * HID], F16))
        wc2_h = sc_c12.enter_context(nc.sbuf_tensor("wc2_h", [128, 16 * C], F16))
        wc2_l = sc_c12.enter_context(nc.sbuf_tensor("wc2_l", [128, 16 * C], F16))
        with nc.sbuf_tensor("wstg2", [128, 2 * HID], F32) as wstg2, \
             nc.sbuf_tensor("wtmp2", [128, HID], F32) as wtmp2:
            cw_s, cw_c, cw_v = emit_wprep(
                "p3w", [(wc1_d, 8, HID, 1, wc1_h, wc1_l),
                        (wc2_d, 16, C, None, wc2_h, wc2_l)],
                wstg2, wtmp2, HID, False)
            run_phase([("sync", cw_s), ("scalar", cw_c), ("vector", cw_v)])

        with nc.sbuf_tensor("x2t", [128, 8 * TJ], F32) as x2t, \
             nc.sbuf_tensor("sh3", [128, 8 * TJ], F32) as sh3, \
             nc.sbuf_tensor("xn2h", [128, 8 * TJ], F16) as xn2h, \
             nc.sbuf_tensor("xn2l", [128, 8 * TJ], F16) as xn2l, \
             nc.sbuf_tensor("rbc3", [128, 2 * TJ], F32) as rbc3, \
             nc.sbuf_tensor("h1b", [128, 2 * TJ], F32) as h1b, \
             nc.sbuf_tensor("h1h", [128, 16 * TJ], F16) as h1h, \
             nc.sbuf_tensor("h1l", [128, 16 * TJ], F16) as h1l, \
             nc.sbuf_tensor("pt3", [128, 2 * TJ], F32) as pt3, \
             nc.sbuf_tensor("x3b", [128, 8 * TJ], F32) as x3b, \
             nc.sbuf_tensor("y1b", [128, 8 * TJ], F32) as y1b, \
             nc.psum_tensor("ps_c1", [128, 1024], F32) as ps_c1, \
             nc.psum_tensor("ps_c2", [128, 1024], F32) as ps_c2, \
             nc.psum_tensor("ps_ra", [128, 512], F32) as ps_ra, \
             nc.psum_tensor("ps_rb", [128, 512], F32) as ps_rb:

            def p3_sync():
                e = EY
                x23 = d_x2.rearrange("(c p) t -> p c t", p=128)
                x03 = x_in.rearrange("(c p) t -> p c t", p=128)
                for j in range(NJ2):
                    if j >= 1:
                        e.wait(("p3x3all", j - 1))
                    e.dma(x2t[:, :].rearrange("p (c t) -> p c t", t=TJ),
                          x23[:, :, TJ * j:TJ * (j + 1)], sLD)
                    if j >= 1:
                        e.wait(("p3y1sp", j - 1))
                    e.dma(y1b[:, :].rearrange("p (c t) -> p c t", t=TJ),
                          x03[:, :, TJ * j:TJ * (j + 1)], sLD)
                    e.mark(sLD, ("p3x2", j))

            def p3_scalar():
                e = EC
                for j in range(NJ2):
                    e.wait(("p3x2", j))
                    if j >= 1:
                        e.wait(("p3xn2", j - 1))
                    i = e.activation(sh3[:, :], x2t[:, :], AF.Square)
                    e.inc(i, sEV, 1)
                    e.mark(sEV, ("p3sq", j))
                    e.wait(("p3ss", j))
                    i = e.activation(row(R_SQ, TJ), ps_ra[0:1, 0:TJ], AF.Sqrt)
                    e.inc(i, sEV, 1)
                    e.mark(sEV, ("p3sqrt", j))
                    for ho in range(16):
                        e.wait(("p3mc1", j, ho))
                        hh = j * 16 + ho
                        if hh >= 2:
                            e.wait(("p3h1lo", (hh - 2) // 16, (hh - 2) % 16))
                        pc = ps_c1[:, (ho % 2) * 512:(ho % 2) * 512 + TJ]
                        e.activation(h1b[:, (ho % 2) * TJ:(ho % 2) * TJ + TJ], pc,
                                     AF.Gelu, bias=c1b_sb[:, ho:ho + 1])
                        i = e.copy(h1h[:, ho * TJ:(ho + 1) * TJ],
                                   h1b[:, (ho % 2) * TJ:(ho % 2) * TJ + TJ])
                        e.inc(i, sEV, 1)
                        e.mark(sEV, ("p3gel", j, ho))
                    for oc in range(8):
                        e.wait(("p3mc2", j, oc))
                        qq = j * 8 + oc
                        if qq >= 2:
                            e.wait(("p3x3", (qq - 2) // 8, (qq - 2) % 8))
                        i = e.activation(pt3[:, (oc % 2) * TJ:(oc % 2) * TJ + TJ],
                                         ps_c2[:, (oc % 2) * 512:(oc % 2) * 512 + TJ],
                                         AF.Identity, scale=vsl(2, oc),
                                         bias=gpb_sb[:, 8 + oc:9 + oc])
                        e.inc(i, sEV, 1)
                        e.mark(sEV, ("p3ev2", j, oc))
                    e.wait(("p3x3all", j))
                    i = e.activation(sh3[:, :], x3b[:, :], AF.Square)
                    e.inc(i, sEV, 1)
                    e.mark(sEV, ("p3sq3", j))
                    e.wait(("p3ln1mm", j))
                    e.activation(row(R_M, TJ), ps_ra[0:1, 0:TJ], AF.Identity,
                                 scale=1.0 / 1024.0)
                    i = e.activation(row(R_EX, TJ), ps_rb[0:1, 0:TJ], AF.Identity,
                                     scale=1.0 / 1024.0)
                    e.inc(i, sEV, 1)
                    e.mark(sEV, ("p3rows", j))
                    e.wait(("p3var", j))
                    i = e.activation(row(R_SD, TJ), row(R_VAR, TJ), AF.Sqrt, bias=cbias_sb[0:1, 0:1])
                    e.inc(i, sEV, 1)
                    e.mark(sEV, ("p3sd", j))

            def p3_vector():
                e = EV
                e.memset(amaxc[:, 0:1], 0.0)
                for j in range(NJ2):
                    e.wait(("p3sqrt", j))
                    e.reciprocal(row(R_RC, TJ), row(R_SQ, TJ))
                    i = e.tensor_scalar(row(R_RK, TJ), row(R_RC, TJ), RMS_K, None,
                                        ALU.mult)
                    e.inc(i, sVE, 1)
                    e.mark(sVE, ("p3r2row", j))
                    e.wait(("p3r2bc", j))
                    i = e.tensor_copy(rbc3[:, 0:TJ], ps_ra[:, 0:TJ])
                    e.inc(i, sVE, 1)
                    e.mark(sVE, ("p3r2cp", j))
                    rb3 = rbc3[:, 0:TJ].unsqueeze(1).broadcast_to([128, 8, TJ])
                    x4 = x2t[:, :].rearrange("p (c t) -> p c t", t=TJ)
                    e.tensor_tensor(xn2h[:, :].rearrange("p (c t) -> p c t", t=TJ),
                                    x4, rb3, ALU.mult)
                    e.tensor_tensor(sh3[:, :].rearrange("p (c t) -> p c t", t=TJ),
                                    x4, rb3, ALU.mult)
                    i = e.tensor_tensor(xn2l[:, :], sh3[:, :], xn2h[:, :], ALU.subtract)
                    e.inc(i, sVE, 1)
                    e.mark(sVE, ("p3xn2", j))
                    for ho in range(16):
                        e.wait(("p3gel", j, ho))
                        i = e.tensor_tensor(h1l[:, ho * TJ:(ho + 1) * TJ],
                                            h1b[:, (ho % 2) * TJ:(ho % 2) * TJ + TJ],
                                            h1h[:, ho * TJ:(ho + 1) * TJ], ALU.subtract)
                        e.inc(i, sVE, 1)
                        e.mark(sVE, ("p3h1lo", j, ho))
                    for oc in range(8):
                        e.wait(("p3ev2", j, oc))
                        e.tensor_tensor(x3b[:, oc * TJ:(oc + 1) * TJ],
                                        pt3[:, (oc % 2) * TJ:(oc % 2) * TJ + TJ],
                                        x2t[:, oc * TJ:(oc + 1) * TJ], ALU.add)
                        i = e.tensor_tensor(x3b[:, oc * TJ:(oc + 1) * TJ],
                                            x3b[:, oc * TJ:(oc + 1) * TJ],
                                            y1b[:, oc * TJ:(oc + 1) * TJ], ALU.add)
                        e.inc(i, sVE, 1)
                        e.mark(sVE, ("p3x3", j, oc))
                    e.mark(sVE, ("p3x3all", j))
                    e.wait(("p3rows", j))
                    e.tensor_tensor(row(R_VAR, TJ), row(R_M, TJ), row(R_M, TJ), ALU.mult)
                    i = e.tensor_tensor(row(R_VAR, TJ), row(R_EX, TJ), row(R_VAR, TJ),
                                        ALU.subtract)
                    e.inc(i, sVE, 1)
                    e.mark(sVE, ("p3var", j))
                    e.wait(("p3sd", j))
                    e.reciprocal(row(R_R, TJ), row(R_SD, TJ))
                    i = e.tensor_tensor(row(R_MR, TJ), row(R_M, TJ), row(R_R, TJ), ALU.mult)
                    e.inc(i, sVE, 1)
                    e.mark(sVE, ("p3lnr", j))
                    e.wait(("p3lnbc", j))
                    if j >= 1:
                        e.wait(("p3y1sp", j - 1))
                    e.tensor_copy(rbc3[:, 0:TJ], ps_ra[:, 0:TJ])
                    i = e.tensor_copy(rbc3[:, TJ:2 * TJ], ps_rb[:, 0:TJ])
                    e.inc(i, sVE, 1)
                    e.mark(sVE, ("p3lncp", j))
                    rb = rbc3[:, 0:TJ].unsqueeze(1).broadcast_to([128, 8, TJ])
                    mrb = rbc3[:, TJ:2 * TJ].unsqueeze(1).broadcast_to([128, 8, TJ])
                    x5 = x3b[:, :].rearrange("p (c t) -> p c t", t=TJ)
                    y5 = y1b[:, :].rearrange("p (c t) -> p c t", t=TJ)
                    e.tensor_tensor(y5, x5, rb, ALU.mult)
                    e.tensor_tensor(y5, y5, mrb, ALU.subtract)
                    e.tensor_reduce(amaxc[:, 4:5], y1b[:, :], AX.X, ALU.max,
                                    apply_absolute_value=True)
                    i = e.tensor_tensor(amaxc[:, 0:1], amaxc[:, 0:1], amaxc[:, 4:5],
                                        ALU.max)
                    e.inc(i, sVE, 1)
                    e.mark(sVE, ("p3y1", j))

            def p3_tensor():
                e = ET
                for j in range(NJ2):
                    e.wait(("p3sq", j))
                    if j >= 1:
                        e.wait(("p3lncp", j - 1))
                    for k in range(8):
                        i = e.matmul(ps_ra[0:1, 0:TJ], onesc_sb[:, :],
                                     sh3[:, k * TJ:(k + 1) * TJ],
                                     start=(k == 0), stop=(k == 7))
                    e.inc(i, sMM, 1)
                    e.mark(sMM, ("p3ss", j))
                    e.wait(("p3sqrt", j))
                    e.wait(("p3r2row", j))
                    i = e.matmul(ps_ra[:, 0:TJ], onesr_sb[0:1, :],
                                 row(R_RK, TJ), start=True, stop=True)
                    e.inc(i, sMM, 1)
                    e.mark(sMM, ("p3r2bc", j))
                    e.wait(("p3xn2", j))
                    for ho in range(16):
                        hh = j * 16 + ho
                        if hh >= 2:
                            e.wait(("p3gel", (hh - 2) // 16, (hh - 2) % 16))
                        pc = ps_c1[:, (ho % 2) * 512:(ho % 2) * 512 + TJ]
                        n = 0
                        for (ah, bh) in ((1, 1), (1, 0), (0, 1)):
                            for k in range(8):
                                w = (wc1_h if ah else wc1_l)[:, k * HID + ho * 128:
                                                             k * HID + (ho + 1) * 128]
                                xs = (xn2h if bh else xn2l)[:, k * TJ:(k + 1) * TJ]
                                i = e.matmul(pc, w, xs, start=(n == 0), stop=(n == 23))
                                n += 1
                        e.inc(i, sMM, 1)
                        e.mark(sMM, ("p3mc1", j, ho))
                    for k in range(16):
                        e.wait(("p3h1lo", j, k))
                    for oc in range(8):
                        qq = j * 8 + oc
                        if qq >= 2:
                            e.wait(("p3ev2", (qq - 2) // 8, (qq - 2) % 8))
                        pc2 = ps_c2[:, (oc % 2) * 512:(oc % 2) * 512 + TJ]
                        n = 0
                        for (ah, bh) in ((1, 1), (1, 0), (0, 1)):
                            for k in range(16):
                                w = (wc2_h if ah else wc2_l)[:, k * C + oc * 128:
                                                             k * C + (oc + 1) * 128]
                                xs = (h1h if bh else h1l)[:, k * TJ:(k + 1) * TJ]
                                i = e.matmul(pc2, w, xs, start=(n == 0), stop=(n == 47))
                                n += 1
                        e.inc(i, sMM, 1)
                        e.mark(sMM, ("p3mc2", j, oc))
                    e.wait(("p3sq3", j))
                    e.wait(("p3r2cp", j))
                    for k in range(8):
                        e.matmul(ps_ra[0:1, 0:TJ], onesc_sb[:, :],
                                 x3b[:, k * TJ:(k + 1) * TJ], start=(k == 0), stop=(k == 7))
                        i = e.matmul(ps_rb[0:1, 0:TJ], onesc_sb[:, :],
                                     sh3[:, k * TJ:(k + 1) * TJ], start=(k == 0),
                                     stop=(k == 7))
                    e.inc(i, sMM, 1)
                    e.mark(sMM, ("p3ln1mm", j))
                    e.wait(("p3lnr", j))
                    e.matmul(ps_ra[:, 0:TJ], onesr_sb[0:1, :], row(R_R, TJ),
                             start=True, stop=True)
                    i = e.matmul(ps_rb[:, 0:TJ], onesr_sb[0:1, :], row(R_MR, TJ),
                                 start=True, stop=True)
                    e.inc(i, sMM, 1)
                    e.mark(sMM, ("p3lnbc", j))

            def p3_gpsimd():
                e = EG
                for j in range(NJ2):
                    e.wait(("p3y1", j))
                    e.dma(d_y1.rearrange("(c p) t -> p c t", p=128)[:, :, TJ * j:TJ * (j + 1)],
                          y1b[:, :].rearrange("p (c t) -> p c t", t=TJ), sSP)
                    e.mark(sSP, ("p3y1sp", j))

            run_phase([("sync", p3_sync), ("scalar", p3_scalar), ("vector", p3_vector),
                       ("tensor", p3_tensor), ("gpsimd", p3_gpsimd)])

        sc_c12.close()

        def emit_cc_finalize(tag, pieces, ccidx, comp_fn, bcasts, ps_cc):

            def t_fn():
                e = ET
                for n_, (acol, scol) in enumerate(pieces):
                    i = e.transpose(ps_cc[0:1, n_ * 128:n_ * 128 + 128],
                                    amaxc[:, acol:acol + 1], eye32_sb[:, :])
                    e.inc(i, sMM, 1)
                e.mark(sMM, (tag, "tr"))
                e.wait((tag, "comp"))
                for n_, (scol, bcol) in enumerate(bcasts):
                    i = e.matmul(ps_cc[:, n_ * 128:n_ * 128 + 1], onesr_sb[0:1, :],
                                 scal[0:1, scol:scol + 1], start=True, stop=True)
                    e.inc(i, sMM, 1)
                e.mark(sMM, (tag, "bc"))

            def v_fn():
                e = EV
                e.wait((tag, "tr"))
                for n_, (acol, scol) in enumerate(pieces):
                    i = e.tensor_reduce(scal[0:1, scol:scol + 1],
                                        ps_cc[0:1, n_ * 128:n_ * 128 + 128],
                                        AX.X, ALU.max, apply_absolute_value=True)
                if len(pieces) == 1:
                    i = e.memset(scal[0:1, pieces[0][1] + 1:pieces[0][1] + 2], 0.0)
                e.inc(i, sVE, 1)
                e.mark(sVE, (tag, "locmax"))
                e.wait((tag, "ccld"))
                i = comp_fn(e)
                e.inc(i, sVE, 1)
                e.mark(sVE, (tag, "comp"))
                e.wait((tag, "bc"))
                for n_, (scol, bcol) in enumerate(bcasts):
                    i = e.tensor_copy(scbc[:, bcol:bcol + 1], ps_cc[:, n_ * 128:n_ * 128 + 1])
                e.inc(i, sVE, 1)
                e.mark(sVE, (tag, "done"))

            def g_fn():
                e = EG
                e.wait((tag, "locmax"))
                src_col = pieces[0][1]
                e.dma(cc_in[ccidx][0:1, 0:2], scal[0:1, src_col:src_col + 2], sSP)
                e.wait_sem_now(sSP)
                e.cc(cc_in[ccidx][:, :], cc_out[ccidx][:, :], sGP)
                e.wait_sem_now(sGP)
                e.dma(scal[0:1, src_col + 2:src_col + 4], cc_out[ccidx][0:1, 0:2], sSP)
                e.mark(sSP, (tag, "ccld"))

            return [("tensor", t_fn), ("vector", v_fn), ("gpsimd", g_fn)]

        with nc.psum_tensor("ps_cc0", [128, 256], F32) as ps_cc0:
            def comp0(e):
                e.tensor_scalar(sc1(4), sc1(2), INV_QMAX, 1e-12, ALU.mult, ALU.max)
                return e.reciprocal(sc1(5), sc1(4))

            run_phase(emit_cc_finalize("cc0", [(0, 0)], 0, comp0, [(5, 0)], ps_cc0))

        sc_mx = contextlib.ExitStack()
        mx_h = sc_mx.enter_context(nc.sbuf_tensor("mx_h", [128, 8 * T], F16))
        mx_l = sc_mx.enter_context(nc.sbuf_tensor("mx_l", [128, 8 * T], F16))

        def mx3(t_):
            return t_.rearrange("p (c t) -> p c t", t=T)

        mats5 = [(wup_d, d_U, 2), (wgate_d, d_G, 3)]

        with nc.sbuf_tensor("y1t5", [128, 2 * 512], F32) as y1t5, \
             nc.sbuf_tensor("mxi", [128, 2 * 512], I32) as mxi, \
             nc.sbuf_tensor("wff", [128, 2 * C], F32) as wff, \
             nc.sbuf_tensor("sdq", [128, 2 * 32], F32) as sdq, \
             nc.sbuf_tensor("rdq", [128, 32], F32) as rdq, \
             nc.sbuf_tensor("mwi", [128, 2 * C], I32) as mwi, \
             nc.sbuf_tensor("mwf", [128, 2 * C], F16) as mwf, \
             nc.sbuf_tensor("sTb", [32, 2 * 128], F32) as sTb, \
             nc.sbuf_tensor("srep", [128, 2 * 128], F32) as srep, \
             nc.sbuf_tensor("w16t", [128, 128], F32) as w16t, \
             nc.sbuf_tensor("wqh", [128, 2 * 8 * 128], F16) as wqh, \
             nc.sbuf_tensor("wql", [128, 2 * 8 * 128], F16) as wql, \
             nc.sbuf_tensor("ub", [128, 2 * 512], F32) as ub, \
             nc.psum_tensor("ps_ug", [128, 2 * 512], F32) as ps_ug, \
             nc.psum_tensor("ps_w16", [128, 2048], F16) as ps_w16, \
             nc.psum_tensor("ps_s5", [32, 1024], F32) as ps_s5, \
             nc.psum_tensor("ps_sr", [128, 1024], F32) as ps_sr:

            ND = 48

            def p5_sync():
                e = EY
                y13 = d_y1.rearrange("(c p) t -> p c t", p=128)
                for j in range(NJ):
                    for c in range(8):
                        fc = j * 8 + c
                        if fc >= 2:
                            e.wait(("p5mxl", (fc - 2) // 8, (fc - 2) % 8))
                        e.dma(y1t5[:, (fc % 2) * 512:(fc % 2) * 512 + 512],
                              y13[:, c, 512 * j:512 * (j + 1)], sLD)
                        e.mark(sLD, ("p5y1", j, c))
                for nb in range(ND):
                    mi, oc = divmod(nb, 24)
                    if nb >= 2:
                        e.wait(("p5mw", nb - 2))
                    e.dma(wff[:, (nb % 2) * C:(nb % 2) * C + C],
                          mats5[mi][0][oc * 128:(oc + 1) * 128, :], sLD)
                    e.mark(sLD, ("p5w", nb))

            def p5_scalar():
                e = EC
                for j in range(NJ):
                    for c in range(8):
                        fc = j * 8 + c
                        e.wait(("p5y1", j, c))
                        if fc >= 2:
                            e.wait(("p5mxl", (fc - 2) // 8, (fc - 2) % 8))
                        if DEBUG and fc >= 2:
                            e.wait(("p5mxsp", (fc - 2) // 8, (fc - 2) % 8))
                        sl = slice((fc % 2) * 512, (fc % 2) * 512 + 512)
                        e.activation(mxi[:, sl], y1t5[:, sl], AF.Identity,
                                     scale=scbc[:, 0:1])
                        i = e.copy(mx3(mx_h[:, :])[:, c, 512 * j:512 * (j + 1)],
                                   mxi[:, sl])
                        e.inc(i, sEV, 1)
                        e.mark(sEV, ("p5mxh", j, c))
                for nb in range(ND):
                    for j in range(NJ):
                        fb = nb * NJ + j
                        e.wait(("p5mm", nb, j))
                        if fb >= 2:
                            e.wait(("p5sp", fb - 2))
                            e.wait(("p5am", fb - 2))
                        i = e.copy(ub[:, (fb % 2) * 512:(fb % 2) * 512 + 512],
                                   ps_ug[:, (fb % 2) * 512:(fb % 2) * 512 + 512])
                        e.inc(i, sEV, 1)
                        e.mark(sEV, ("p5ev", fb))

            def p5_dequant_vec(e, nb):
                mi, oc = divmod(nb, 24)
                e.wait(("p5w", nb))
                if nb >= 2:
                    e.wait(("p5strd", nb - 2))
                    e.wait(("p5mwf", nb - 2))
                stg = wff[:, (nb % 2) * C:(nb % 2) * C + C]
                w3 = stg.rearrange("p (b i) -> p b i", i=32)
                ss_ = sdq[:, (nb % 2) * 32:(nb % 2) * 32 + 32]
                e.tensor_reduce(ss_, w3, AX.X, ALU.max, apply_absolute_value=True)
                e.tensor_scalar(ss_, ss_, 1.0 / 7.0, 1e-12, ALU.mult, ALU.max)
                i = e.reciprocal(rdq[:, :], ss_)
                e.inc(i, sVE, 1)
                e.mark(sVE, ("p5s", nb))
                rs3 = rdq[:, :].unsqueeze(2).broadcast_to([128, 32, 32])
                i = e.tensor_tensor(
                    mwi[:, (nb % 2) * C:(nb % 2) * C + C].rearrange(
                        "p (b i) -> p b i", i=32), w3, rs3, ALU.mult)
                e.inc(i, sVE, 1)
                e.mark(sVE, ("p5mw", nb))

            def p5_dequant_vec_tail(e, nb):
                if DEBUG and 2 <= nb < 26:
                    e.wait(("p5wqsp", nb - 2))
                for ci in range(8):
                    e.wait(("p5wt", nb, ci))
                    e.wait(("p5sr", nb, ci))
                    i = e.tensor_copy(srep[:, (ci % 2) * 128:(ci % 2) * 128 + 128],
                                      ps_sr[:, (ci % 2) * 512:(ci % 2) * 512 + 128])
                    e.inc(i, sVE, 1)
                    e.mark(sVE, ("p5srcp", nb, ci))
                    pw = ps_w16[:, (ci % 2) * 1024:(ci % 2) * 1024 + 128]
                    sr = srep[:, (ci % 2) * 128:(ci % 2) * 128 + 128]
                    sl = slice((nb % 2) * 1024 + ci * 128, (nb % 2) * 1024 + (ci + 1) * 128)
                    e.tensor_tensor(wqh[:, sl], pw, sr, ALU.mult)
                    e.tensor_tensor(w16t[:, :], pw, sr, ALU.mult)
                    i = e.tensor_tensor(wql[:, sl], w16t[:, :], wqh[:, sl], ALU.subtract)
                    e.inc(i, sVE, 1)
                    e.mark(sVE, ("p5dq", nb, ci))

            def p5_vector():
                e = EV
                e.memset(amaxc[:, 2:4], 0.0)
                i = e.memset(amaxc[:, 5:7], 0.0)
                e.inc(i, sVE, 1)
                e.mark(sVE, ("p5amz",))
                for j in range(NJ):
                    for c in range(8):
                        fc = j * 8 + c
                        e.wait(("p5mxh", j, c))
                        sl = slice((fc % 2) * 512, (fc % 2) * 512 + 512)
                        i = e.tensor_tensor(mx3(mx_l[:, :])[:, c, 512 * j:512 * (j + 1)],
                                            mxi[:, sl],
                                            mx3(mx_h[:, :])[:, c, 512 * j:512 * (j + 1)],
                                            ALU.subtract)
                        e.inc(i, sVE, 1)
                        e.mark(sVE, ("p5mxl", j, c))
                p5_dequant_vec(e, 0)
                p5_dequant_vec_tail(e, 0)
                p5_dequant_vec(e, 1)
                p5_dequant_vec_tail(e, 1)
                for nb in range(ND):
                    mi, oc = divmod(nb, 24)
                    for j in range(NJ):
                        fb = nb * NJ + j
                        e.wait(("p5ev", fb))
                        e.tensor_reduce(amaxc[:, 4:5],
                                        ub[:, (fb % 2) * 512:(fb % 2) * 512 + 512],
                                        AX.X, ALU.max, apply_absolute_value=True)
                        i = e.tensor_tensor(amaxc[:, 2 + mi:3 + mi], amaxc[:, 2 + mi:3 + mi],
                                            amaxc[:, 4:5], ALU.max)
                        e.inc(i, sVE, 1)
                        e.mark(sVE, ("p5am", fb))
                    if nb + 2 < ND:
                        p5_dequant_vec(e, nb + 2)
                        p5_dequant_vec_tail(e, nb + 2)

            def p5_gpsimd():
                e = EG
                if DEBUG:
                    for j in range(NJ):
                        for c in range(8):
                            e.wait(("p5mxh", j, c))
                            fc = j * 8 + c
                            e.dma(d_mx[c * 128:(c + 1) * 128, 512 * j:512 * (j + 1)],
                                  mxi[:, (fc % 2) * 512:(fc % 2) * 512 + 512], sSP)
                            e.mark(sSP, ("p5mxsp", j, c))
                for nb in range(2):
                    e.wait(("p5mw", nb))
                    i = e.tensor_copy(mwf[:, (nb % 2) * C:(nb % 2) * C + C],
                                      mwi[:, (nb % 2) * C:(nb % 2) * C + C])
                    e.inc(i, sGP, 1)
                    e.mark(sGP, ("p5mwf", nb))
                for nb in range(ND):
                    mi, oc = divmod(nb, 24)
                    ddst = mats5[mi][1]
                    if DEBUG and mi == 0:
                        for ci in range(8):
                            e.wait(("p5dq", nb, ci))
                            sl = slice((nb % 2) * 1024 + ci * 128,
                                       (nb % 2) * 1024 + (ci + 1) * 128)
                            e.dma(d_wqh[ci * 128:(ci + 1) * 128, oc * 128:(oc + 1) * 128],
                                  wqh[:, sl], sSP)
                            e.dma(d_wql[ci * 128:(ci + 1) * 128, oc * 128:(oc + 1) * 128],
                                  wql[:, sl], sSP)
                        e.mark(sSP, ("p5wqsp", nb))
                    for j in range(NJ):
                        fb = nb * NJ + j
                        e.wait(("p5ev", fb))
                        e.dma(ddst[oc * 128:(oc + 1) * 128, 512 * j:512 * (j + 1)],
                              ub[:, (fb % 2) * 512:(fb % 2) * 512 + 512], sSP)
                        e.mark(sSP, ("p5sp", fb))
                    if nb + 2 < ND:
                        e.wait(("p5mw", nb + 2))
                        e.wait(("p5wt", nb, 7))
                        i = e.tensor_copy(mwf[:, (nb % 2) * C:(nb % 2) * C + C],
                                          mwi[:, (nb % 2) * C:(nb % 2) * C + C])
                        e.inc(i, sGP, 1)
                        e.mark(sGP, ("p5mwf", nb + 2))

            def p5_tensor_dq(e, nb):
                e.wait(("p5s", nb))
                i = e.transpose(ps_s5[0:32, (nb % 2) * 512:(nb % 2) * 512 + 128],
                                sdq[:, (nb % 2) * 32:(nb % 2) * 32 + 32], eye32_sb[:, :])
                e.inc(i, sMM, 1)
                e.mark(sMM, ("p5strd", nb))
                e.wait(("p5stcp", nb))
                e.wait(("p5mwf", nb))
                for ci in range(8):
                    cc_ = nb * 8 + ci
                    if cc_ >= 2:
                        e.wait(("p5dq", (cc_ - 2) // 8, (cc_ - 2) % 8))
                    i = e.transpose(ps_w16[:, (ci % 2) * 1024:(ci % 2) * 1024 + 128],
                                    mwf[:, (nb % 2) * C + ci * 128:(nb % 2) * C + (ci + 1) * 128],
                                    eye16_sb[:, :])
                    e.inc(i, sMM, 1)
                    e.mark(sMM, ("p5wt", nb, ci))
                    if cc_ >= 2:
                        e.wait(("p5srcp", (cc_ - 2) // 8, (cc_ - 2) % 8))
                    i = e.matmul(ps_sr[:, (ci % 2) * 512:(ci % 2) * 512 + 128],
                                 eexpA_sb[:, ci * 128:(ci + 1) * 128],
                                 sTb[:, (nb % 2) * 128:(nb % 2) * 128 + 128],
                                 start=True, stop=True)
                    e.inc(i, sMM, 1)
                    e.mark(sMM, ("p5sr", nb, ci))

            def p5_tensor():
                e = ET
                p5_tensor_dq(e, 0)
                p5_tensor_dq(e, 1)
                for nb in range(ND):
                    for ci in range(8):
                        e.wait(("p5dq", nb, ci))
                    for j in range(NJ):
                        fb = nb * NJ + j
                        for c in range(8):
                            e.wait(("p5mxl", j, c))
                        if fb >= 2:
                            e.wait(("p5ev", fb - 2))
                        pu = ps_ug[:, (fb % 2) * 512:(fb % 2) * 512 + 512]
                        n = 0
                        for (ah, bh) in ((1, 1), (1, 0), (0, 1)):
                            for k in range(8):
                                sl = slice((nb % 2) * 1024 + k * 128,
                                           (nb % 2) * 1024 + (k + 1) * 128)
                                w = (wqh if ah else wql)[:, sl]
                                xs = mx3((mx_h if bh else mx_l)[:, :])[:, k,
                                                                       512 * j:512 * (j + 1)]
                                i = e.matmul(pu, w, xs, start=(n == 0), stop=(n == 23))
                                n += 1
                        e.inc(i, sMM, 1)
                        e.mark(sMM, ("p5mm", nb, j))
                    if nb + 2 < ND:
                        p5_tensor_dq(e, nb + 2)

            def p5_scalar_st():
                pass


            def p5_scalar_pre():
                e = EC
                for nb in range(ND):
                    e.wait(("p5strd", nb))
                    i = e.copy(sTb[0:32, (nb % 2) * 128:(nb % 2) * 128 + 128],
                               ps_s5[0:32, (nb % 2) * 512:(nb % 2) * 512 + 128])
                    e.inc(i, sEV, 1)
                    e.mark(sEV, ("p5stcp", nb))

            def p5_scalar_all():
                e = EC
                for j in range(NJ):
                    for c in range(8):
                        fc = j * 8 + c
                        e.wait(("p5y1", j, c))
                        if fc >= 2:
                            e.wait(("p5mxl", (fc - 2) // 8, (fc - 2) % 8))
                        if DEBUG and fc >= 2:
                            e.wait(("p5mxsp", (fc - 2) // 8, (fc - 2) % 8))
                        sl = slice((fc % 2) * 512, (fc % 2) * 512 + 512)
                        e.activation(mxi[:, sl], y1t5[:, sl], AF.Identity,
                                     scale=scbc[:, 0:1])
                        i = e.copy(mx3(mx_h[:, :])[:, c, 512 * j:512 * (j + 1)],
                                   mxi[:, sl])
                        e.inc(i, sEV, 1)
                        e.mark(sEV, ("p5mxh", j, c))
                e.wait(("p5strd", 0))
                i = e.copy(sTb[0:32, 0:128], ps_s5[0:32, 0:128])
                e.inc(i, sEV, 1)
                e.mark(sEV, ("p5stcp", 0))
                e.wait(("p5strd", 1))
                i = e.copy(sTb[0:32, 128:256], ps_s5[0:32, 512:640])
                e.inc(i, sEV, 1)
                e.mark(sEV, ("p5stcp", 1))
                for nb in range(ND):
                    for j in range(NJ):
                        fb = nb * NJ + j
                        e.wait(("p5mm", nb, j))
                        if fb >= 2:
                            e.wait(("p5sp", fb - 2))
                            e.wait(("p5am", fb - 2))
                        i = e.copy(ub[:, (fb % 2) * 512:(fb % 2) * 512 + 512],
                                   ps_ug[:, (fb % 2) * 512:(fb % 2) * 512 + 512])
                        e.inc(i, sEV, 1)
                        e.mark(sEV, ("p5ev", fb))
                    if nb + 2 < ND:
                        e.wait(("p5strd", nb + 2))
                        i = e.copy(sTb[0:32, ((nb + 2) % 2) * 128:((nb + 2) % 2) * 128 + 128],
                                   ps_s5[0:32, ((nb + 2) % 2) * 512:((nb + 2) % 2) * 512 + 128])
                        e.inc(i, sEV, 1)
                        e.mark(sEV, ("p5stcp", nb + 2))

            run_phase([("sync", p5_sync), ("scalar", p5_scalar_all),
                       ("vector", p5_vector), ("gpsimd", p5_gpsimd),
                       ("tensor", p5_tensor)])

        with nc.psum_tensor("ps_cc1", [128, 384], F32) as ps_cc1:
            def comp1(e):
                e.tensor_tensor(sc1(11), sc1(8), sc1(4), ALU.mult)
                e.tensor_scalar(sc1(10), sc1(11), INV_QMAX, 1e-12, ALU.mult, ALU.max)
                e.reciprocal(sc1(11), sc1(10))
                e.tensor_tensor(sc1(12), sc1(4), sc1(11), ALU.mult)
                e.tensor_tensor(sc1(11), sc1(9), sc1(4), ALU.mult)
                e.tensor_scalar(sc1(13), sc1(11), INV_QMAX, 1e-12, ALU.mult, ALU.max)
                e.reciprocal(sc1(11), sc1(13))
                e.tensor_tensor(sc1(14), sc1(4), sc1(11), ALU.mult)
                return e.tensor_scalar(sc1(15), sc1(13), 1.0 / 65536.0, None, ALU.mult)

            run_phase(emit_cc_finalize("cc1", [(2, 6), (3, 7)], 1, comp1,
                                       [(12, 1), (14, 2), (13, 3)], ps_cc1))
        sc_mx.close()

        with nc.sbuf_tensor("u6", [128, 2 * 512], F32) as u6, \
             nc.sbuf_tensor("g6", [128, 2 * 512], F32) as g6, \
             nc.sbuf_tensor("mu6", [128, 2 * 512], I32) as mu6, \
             nc.sbuf_tensor("mg6", [128, 2 * 512], I32) as mg6, \
             nc.sbuf_tensor("sg6", [128, 2 * 512], F32) as sg6, \
             nc.sbuf_tensor("ms6", [128, 2 * 512], I32) as ms6, \
             nc.sbuf_tensor("p6", [128, 2 * 512], F32) as p6:

            NB6 = 24 * NJ

            def p6a_sync():
                e = EY
                u3 = d_U.rearrange("(c p) t -> p c t", p=128)
                g3 = d_G.rearrange("(c p) t -> p c t", p=128)
                for fb in range(NB6):
                    oc, j = divmod(fb, NJ)
                    if fb >= 2:
                        e.wait(("p6musp", fb - 2))
                        e.wait(("p6psp", fb - 2))
                    e.dma(u6[:, (fb % 2) * 512:(fb % 2) * 512 + 512],
                          u3[:, oc, 512 * j:512 * (j + 1)], sLD)
                    e.dma(g6[:, (fb % 2) * 512:(fb % 2) * 512 + 512],
                          g3[:, oc, 512 * j:512 * (j + 1)], sLD)
                    e.mark(sLD, ("p6ld", fb))

            def p6a_scalar():
                e = EC
                for fb in range(NB6):
                    e.wait(("p6ld", fb))
                    if fb >= 2:
                        e.wait(("p6p", fb - 2))
                        e.wait(("p6musp", fb - 2))
                    sl = slice((fb % 2) * 512, (fb % 2) * 512 + 512)
                    e.activation(mu6[:, sl], u6[:, sl], AF.Identity, scale=scbc[:, 1:2])
                    e.activation(mg6[:, sl], g6[:, sl], AF.Identity, scale=scbc[:, 2:3])
                    i = e.activation(sg6[:, sl], mg6[:, sl], AF.Sigmoid, scale=scbc[:, 3:4])
                    e.inc(i, sEV, 1)
                    e.mark(sEV, ("p6s", fb))

            def p6a_vector():
                e = EV
                for fb in range(NB6):
                    e.wait(("p6s", fb))
                    if fb >= 2:
                        e.wait(("p6psp", fb - 2))
                    sl = slice((fb % 2) * 512, (fb % 2) * 512 + 512)
                    e.tensor_scalar(ms6[:, sl], sg6[:, sl], 65536.0, 65535.0,
                                    ALU.mult, ALU.min)
                    e.tensor_tensor(p6[:, sl], mg6[:, sl], ms6[:, sl], ALU.mult)
                    e.tensor_reduce(amaxc[:, 4:5], p6[:, sl], AX.X, ALU.max,
                                    apply_absolute_value=True)
                    i = e.tensor_tensor(amaxc[:, 5:6], amaxc[:, 5:6], amaxc[:, 4:5],
                                        ALU.max)
                    e.inc(i, sVE, 1)
                    e.mark(sVE, ("p6p", fb))

            def p6a_gpsimd():
                e = EG
                mu3 = d_mu.rearrange("(c p) t -> p c t", p=128)
                p3_ = d_P.rearrange("(c p) t -> p c t", p=128)
                for fb in range(NB6):
                    oc, j = divmod(fb, NJ)
                    sl = slice((fb % 2) * 512, (fb % 2) * 512 + 512)
                    e.wait(("p6s", fb))
                    e.dma(mu3[:, oc, 512 * j:512 * (j + 1)], mu6[:, sl], sSP)
                    e.mark(sSP, ("p6musp", fb))
                    e.wait(("p6p", fb))
                    e.dma(p3_[:, oc, 512 * j:512 * (j + 1)], p6[:, sl], sSP)
                    e.mark(sSP, ("p6psp", fb))

            run_phase([("sync", p6a_sync), ("scalar", p6a_scalar),
                       ("vector", p6a_vector), ("gpsimd", p6a_gpsimd)])

        with nc.psum_tensor("ps_cc2", [128, 256], F32) as ps_cc2:
            def comp2(e):
                e.tensor_tensor(sc1(22), sc1(19), sc1(15), ALU.mult)
                e.tensor_scalar(sc1(21), sc1(22), INV_QMAX, 1e-12, ALU.mult, ALU.max)
                e.reciprocal(sc1(22), sc1(21))
                return e.tensor_tensor(sc1(23), sc1(15), sc1(22), ALU.mult)

            run_phase(emit_cc_finalize("cc2", [(5, 17)], 2, comp2, [(23, 4)], ps_cc2))

        with nc.sbuf_tensor("p6b", [128, 2 * 512], F32) as p6b, \
             nc.sbuf_tensor("mu6b", [128, 2 * 512], I32) as mu6b, \
             nc.sbuf_tensor("mg2b", [128, 2 * 512], I32) as mg2b, \
             nc.sbuf_tensor("q6", [128, 2 * 512], F32) as q6:

            NB6 = 24 * NJ

            def p6b_sync():
                e = EY
                p3_ = d_P.rearrange("(c p) t -> p c t", p=128)
                mu3 = d_mu.rearrange("(c p) t -> p c t", p=128)
                for fb in range(NB6):
                    oc, j = divmod(fb, NJ)
                    if fb >= 2:
                        e.wait(("p6qsp", fb - 2))
                    e.dma(p6b[:, (fb % 2) * 512:(fb % 2) * 512 + 512],
                          p3_[:, oc, 512 * j:512 * (j + 1)], sLD)
                    e.dma(mu6b[:, (fb % 2) * 512:(fb % 2) * 512 + 512],
                          mu3[:, oc, 512 * j:512 * (j + 1)], sLD)
                    e.mark(sLD, ("p6bld", fb))

            def p6b_scalar():
                e = EC
                for fb in range(NB6):
                    e.wait(("p6bld", fb))
                    if fb >= 2:
                        e.wait(("p6q", fb - 2))
                    sl = slice((fb % 2) * 512, (fb % 2) * 512 + 512)
                    i = e.activation(mg2b[:, sl], p6b[:, sl], AF.Identity,
                                     scale=scbc[:, 4:5])
                    e.inc(i, sEV, 1)
                    e.mark(sEV, ("p6mg2", fb))

            def p6b_vector():
                e = EV
                for fb in range(NB6):
                    e.wait(("p6mg2", fb))
                    if fb >= 2:
                        e.wait(("p6qsp", fb - 2))
                    sl = slice((fb % 2) * 512, (fb % 2) * 512 + 512)
                    e.tensor_tensor(q6[:, sl], mg2b[:, sl], mu6b[:, sl], ALU.mult)
                    e.tensor_reduce(amaxc[:, 4:5], q6[:, sl], AX.X, ALU.max,
                                    apply_absolute_value=True)
                    i = e.tensor_tensor(amaxc[:, 6:7], amaxc[:, 6:7], amaxc[:, 4:5],
                                        ALU.max)
                    e.inc(i, sVE, 1)
                    e.mark(sVE, ("p6q", fb))

            def p6b_gpsimd():
                e = EG
                q3 = d_Q.rearrange("(c p) t -> p c t", p=128)
                for fb in range(NB6):
                    oc, j = divmod(fb, NJ)
                    e.wait(("p6q", fb))
                    e.dma(q3[:, oc, 512 * j:512 * (j + 1)],
                          q6[:, (fb % 2) * 512:(fb % 2) * 512 + 512], sSP)
                    e.mark(sSP, ("p6qsp", fb))

            run_phase([("sync", p6b_sync), ("scalar", p6b_scalar),
                       ("vector", p6b_vector), ("gpsimd", p6b_gpsimd)])

        with nc.psum_tensor("ps_cc3", [128, 256], F32) as ps_cc3:
            def comp3(e):
                e.tensor_tensor(sc1(28), sc1(21), sc1(10), ALU.mult)
                e.tensor_tensor(sc1(29), sc1(26), sc1(28), ALU.mult)
                e.tensor_scalar(sc1(30), sc1(29), INV_QMAX, 1e-12, ALU.mult, ALU.max)
                e.reciprocal(sc1(31), sc1(30))
                return e.tensor_tensor(sc1(29), sc1(28), sc1(31), ALU.mult)

            run_phase(emit_cc_finalize("cc3", [(6, 24)], 0, comp3,
                                       [(29, 5), (30, 6)], ps_cc3))

        sc_wd = contextlib.ExitStack()
        wdq_h = sc_wd.enter_context(nc.sbuf_tensor("wdq_h", [128, 8 * FF], F16))
        wdq_l = sc_wd.enter_context(nc.sbuf_tensor("wdq_l", [128, 8 * FF], F16))

        with nc.sbuf_tensor("eexpB_sb", [96, 24 * 128], F32) as eexpB_sb, \
             nc.sbuf_tensor("wff7", [128, 2 * FF], F32) as wff7, \
             nc.sbuf_tensor("sdq7", [128, 2 * 96], F32) as sdq7, \
             nc.sbuf_tensor("rdq7", [128, 96], F32) as rdq7, \
             nc.sbuf_tensor("mwi7", [128, 2 * FF], I32) as mwi7, \
             nc.sbuf_tensor("mwf7", [128, 2 * FF], F16) as mwf7, \
             nc.sbuf_tensor("sTb7", [96, 2 * 128], F32) as sTb7, \
             nc.sbuf_tensor("srep7", [128, 2 * 128], F32) as srep7, \
             nc.sbuf_tensor("w16t7", [128, 128], F32) as w16t7, \
             nc.psum_tensor("ps_w167", [128, 2048], F16) as ps_w167, \
             nc.psum_tensor("ps_s57", [96, 1024], F32) as ps_s57, \
             nc.psum_tensor("ps_sr7", [128, 1024], F32) as ps_sr7:

            def p7a_sync():
                e = EY
                e.dma(eexpB_sb[:, :], eexpB_c[:, :], sLD)
                for nb in range(8):
                    if nb >= 2:
                        e.wait(("p7mw", nb - 2))
                    e.dma(wff7[:, (nb % 2) * FF:(nb % 2) * FF + FF],
                          wdown_d[nb * 128:(nb + 1) * 128, :], sLD)
                    e.mark(sLD, ("p7w", nb))

            def p7a_vector():
                e = EV
                for nb in range(8):
                    e.wait(("p7w", nb))
                    if nb >= 2:
                        e.wait(("p7strd", nb - 2))
                        e.wait(("p7mwf", nb - 2))
                    stg = wff7[:, (nb % 2) * FF:(nb % 2) * FF + FF]
                    w3 = stg.rearrange("p (b i) -> p b i", i=32)
                    ss_ = sdq7[:, (nb % 2) * 96:(nb % 2) * 96 + 96]
                    e.tensor_reduce(ss_, w3, AX.X, ALU.max, apply_absolute_value=True)
                    e.tensor_scalar(ss_, ss_, 1.0 / 7.0, 1e-12, ALU.mult, ALU.max)
                    i = e.reciprocal(rdq7[:, :], ss_)
                    e.inc(i, sVE, 1)
                    e.mark(sVE, ("p7s", nb))
                    rs3 = rdq7[:, :].unsqueeze(2).broadcast_to([128, 96, 32])
                    i = e.tensor_tensor(
                        mwi7[:, (nb % 2) * FF:(nb % 2) * FF + FF].rearrange(
                            "p (b i) -> p b i", i=32), w3, rs3, ALU.mult)
                    e.inc(i, sVE, 1)
                    e.mark(sVE, ("p7mw", nb))
                    for ci in range(24):
                        e.wait(("p7wt", nb, ci))
                        e.wait(("p7sr", nb, ci))
                        i = e.tensor_copy(srep7[:, (ci % 2) * 128:(ci % 2) * 128 + 128],
                                          ps_sr7[:, (ci % 2) * 512:(ci % 2) * 512 + 128])
                        e.inc(i, sVE, 1)
                        e.mark(sVE, ("p7srcp", nb, ci))
                        pw = ps_w167[:, (ci % 2) * 1024:(ci % 2) * 1024 + 128]
                        sr = srep7[:, (ci % 2) * 128:(ci % 2) * 128 + 128]
                        sl = slice(nb * FF + ci * 128, nb * FF + (ci + 1) * 128)
                        e.tensor_tensor(wdq_h[:, sl], pw, sr, ALU.mult)
                        e.tensor_tensor(w16t7[:, :], pw, sr, ALU.mult)
                        i = e.tensor_tensor(wdq_l[:, sl], w16t7[:, :], wdq_h[:, sl],
                                            ALU.subtract)
                        e.inc(i, sVE, 1)
                        e.mark(sVE, ("p7dq", nb, ci))

            def p7a_scalar():
                e = EC
                for nb in range(8):
                    e.wait(("p7strd", nb))
                    if nb >= 2:
                        e.wait(("p7sr", nb - 2, 23))
                    i = e.copy(sTb7[0:96, (nb % 2) * 128:(nb % 2) * 128 + 128],
                               ps_s57[0:96, (nb % 2) * 512:(nb % 2) * 512 + 128])
                    e.inc(i, sEV, 1)
                    e.mark(sEV, ("p7stcp", nb))

            def p7a_gpsimd():
                e = EG
                for nb in range(8):
                    e.wait(("p7mw", nb))
                    if nb >= 2:
                        e.wait(("p7wt", nb - 2, 23))
                    i = e.tensor_copy(mwf7[:, (nb % 2) * FF:(nb % 2) * FF + FF],
                                      mwi7[:, (nb % 2) * FF:(nb % 2) * FF + FF])
                    e.inc(i, sGP, 1)
                    e.mark(sGP, ("p7mwf", nb))

            def p7a_tensor():
                e = ET
                for nb in range(8):
                    e.wait(("p7s", nb))
                    i = e.transpose(ps_s57[0:96, (nb % 2) * 512:(nb % 2) * 512 + 128],
                                    sdq7[:, (nb % 2) * 96:(nb % 2) * 96 + 96],
                                    eye32_sb[:, :])
                    e.inc(i, sMM, 1)
                    e.mark(sMM, ("p7strd", nb))
                    e.wait(("p7stcp", nb))
                    e.wait(("p7mwf", nb))
                    for ci in range(24):
                        cc_ = nb * 24 + ci
                        if cc_ >= 2:
                            e.wait(("p7dq", (cc_ - 2) // 24, (cc_ - 2) % 24))
                        i = e.transpose(ps_w167[:, (ci % 2) * 1024:(ci % 2) * 1024 + 128],
                                        mwf7[:, (nb % 2) * FF + ci * 128:
                                             (nb % 2) * FF + (ci + 1) * 128],
                                        eye16_sb[:, :])
                        e.inc(i, sMM, 1)
                        e.mark(sMM, ("p7wt", nb, ci))
                        if cc_ >= 2:
                            e.wait(("p7srcp", (cc_ - 2) // 24, (cc_ - 2) % 24))
                        i = e.matmul(ps_sr7[:, (ci % 2) * 512:(ci % 2) * 512 + 128],
                                     eexpB_sb[:, ci * 128:(ci + 1) * 128],
                                     sTb7[:, (nb % 2) * 128:(nb % 2) * 128 + 128],
                                     start=True, stop=True)
                        e.inc(i, sMM, 1)
                        e.mark(sMM, ("p7sr", nb, ci))

            run_phase([("sync", p7a_sync), ("vector", p7a_vector), ("scalar", p7a_scalar),
                       ("gpsimd", p7a_gpsimd), ("tensor", p7a_tensor)])

        with nc.sbuf_tensor("q7", [128, 2 * TJ], F32) as q7, \
             nc.sbuf_tensor("moi", [128, 2 * TJ], I32) as moi, \
             nc.sbuf_tensor("mo_h", [128, 24 * TJ], F16) as mo_h, \
             nc.sbuf_tensor("mo_l", [128, 24 * TJ], F16) as mo_l, \
             nc.sbuf_tensor("y1t7", [128, 8 * TJ], F32) as y1t7, \
             nc.sbuf_tensor("pt7", [128, 2 * TJ], F32) as pt7, \
             nc.sbuf_tensor("x4b", [128, 8 * TJ], F32) as x4b, \
             nc.sbuf_tensor("sh7", [128, 8 * TJ], F32) as sh7, \
             nc.sbuf_tensor("rbc7", [128, 2 * TJ], F32) as rbc7, \
             nc.sbuf_tensor("yb", [128, 8 * TJ], F32) as yb, \
             nc.psum_tensor("ps_dn", [128, 1024], F32) as ps_dn, \
             nc.psum_tensor("ps_r7a", [128, 512], F32) as ps_r7a, \
             nc.psum_tensor("ps_r7b", [128, 512], F32) as ps_r7b:

            def p7b_sync():
                e = EY
                q3 = d_Q.rearrange("(c p) t -> p c t", p=128)
                y13 = d_y1.rearrange("(c p) t -> p c t", p=128)
                for j in range(NJ2):
                    if j >= 1:
                        e.wait(("p7x4all", j - 1))
                    e.dma(y1t7[:, :].rearrange("p (c t) -> p c t", t=TJ),
                          y13[:, :, TJ * j:TJ * (j + 1)], sLD)
                    e.mark(sLD, ("p7y1", j))
                    for ci in range(24):
                        fb = j * 24 + ci
                        if fb >= 2:
                            pf = fb - 2
                            e.wait(("p7mol", pf // 24, pf % 24))
                        e.dma(q7[:, (ci % 2) * TJ:(ci % 2) * TJ + TJ],
                              q3[:, ci, TJ * j:TJ * (j + 1)], sLD)
                        e.mark(sLD, ("p7q", j, ci))

            def p7b_scalar():
                e = EC
                for j in range(NJ2):
                    for ci in range(24):
                        e.wait(("p7q", j, ci))
                        if j * 24 + ci >= 2:
                            pf = j * 24 + ci - 2
                            e.wait(("p7mol", pf // 24, pf % 24))
                        sl = slice((ci % 2) * TJ, (ci % 2) * TJ + TJ)
                        e.activation(moi[:, sl], q7[:, sl], AF.Identity,
                                     scale=scbc[:, 5:6])
                        i = e.copy(mo_h[:, ci * TJ:(ci + 1) * TJ], moi[:, sl])
                        e.inc(i, sEV, 1)
                        e.mark(sEV, ("p7moh", j, ci))
                    for oc in range(8):
                        e.wait(("p7dn", j, oc))
                        qq = j * 8 + oc
                        if qq >= 2:
                            e.wait(("p7x4", (qq - 2) // 8, (qq - 2) % 8))
                        i = e.activation(pt7[:, (oc % 2) * TJ:(oc % 2) * TJ + TJ],
                                         ps_dn[:, (oc % 2) * 512:(oc % 2) * 512 + TJ],
                                         AF.Identity, scale=scbc[:, 6:7])
                        e.inc(i, sEV, 1)
                        e.mark(sEV, ("p7ev", j, oc))
                    e.wait(("p7x4all", j))
                    i = e.activation(sh7[:, :], x4b[:, :], AF.Square)
                    e.inc(i, sEV, 1)
                    e.mark(sEV, ("p7sq", j))
                    e.wait(("p7lnmm", j))
                    e.activation(row(R_M, TJ), ps_r7a[0:1, 0:TJ], AF.Identity,
                                 scale=1.0 / 1024.0)
                    i = e.activation(row(R_EX, TJ), ps_r7b[0:1, 0:TJ], AF.Identity,
                                     scale=1.0 / 1024.0)
                    e.inc(i, sEV, 1)
                    e.mark(sEV, ("p7rows", j))
                    e.wait(("p7var", j))
                    i = e.activation(row(R_SD, TJ), row(R_VAR, TJ), AF.Sqrt, bias=cbias_sb[0:1, 0:1])
                    e.inc(i, sEV, 1)
                    e.mark(sEV, ("p7sd", j))

            def p7b_vector():
                e = EV
                for j in range(NJ2):
                    for ci in range(24):
                        e.wait(("p7moh", j, ci))
                        sl = slice((ci % 2) * TJ, (ci % 2) * TJ + TJ)
                        i = e.tensor_tensor(mo_l[:, ci * TJ:(ci + 1) * TJ],
                                            moi[:, sl], mo_h[:, ci * TJ:(ci + 1) * TJ],
                                            ALU.subtract)
                        e.inc(i, sVE, 1)
                        e.mark(sVE, ("p7mol", j, ci))
                    for oc in range(8):
                        e.wait(("p7ev", j, oc))
                        e.wait(("p7y1", j))
                        i = e.tensor_tensor(x4b[:, oc * TJ:(oc + 1) * TJ],
                                            pt7[:, (oc % 2) * TJ:(oc % 2) * TJ + TJ],
                                            y1t7[:, oc * TJ:(oc + 1) * TJ], ALU.add)
                        e.inc(i, sVE, 1)
                        e.mark(sVE, ("p7x4", j, oc))
                    e.mark(sVE, ("p7x4all", j))
                    e.wait(("p7rows", j))
                    e.tensor_tensor(row(R_VAR, TJ), row(R_M, TJ), row(R_M, TJ), ALU.mult)
                    i = e.tensor_tensor(row(R_VAR, TJ), row(R_EX, TJ), row(R_VAR, TJ),
                                        ALU.subtract)
                    e.inc(i, sVE, 1)
                    e.mark(sVE, ("p7var", j))
                    e.wait(("p7sd", j))
                    e.reciprocal(row(R_R, TJ), row(R_SD, TJ))
                    i = e.tensor_tensor(row(R_MR, TJ), row(R_M, TJ), row(R_R, TJ),
                                        ALU.mult)
                    e.inc(i, sVE, 1)
                    e.mark(sVE, ("p7lnr", j))
                    e.wait(("p7lnbc", j))
                    e.tensor_copy(rbc7[:, 0:TJ], ps_r7a[:, 0:TJ])
                    i = e.tensor_copy(rbc7[:, TJ:2 * TJ], ps_r7b[:, 0:TJ])
                    e.inc(i, sVE, 1)
                    e.mark(sVE, ("p7lncp", j))
                    if j >= 1:
                        e.wait(("p7ysp", j - 1))
                    rb = rbc7[:, 0:TJ].unsqueeze(1).broadcast_to([128, 8, TJ])
                    mrb = rbc7[:, TJ:2 * TJ].unsqueeze(1).broadcast_to([128, 8, TJ])
                    x5 = x4b[:, :].rearrange("p (c t) -> p c t", t=TJ)
                    y5 = yb[:, :].rearrange("p (c t) -> p c t", t=TJ)
                    e.tensor_tensor(y5, x5, rb, ALU.mult)
                    i = e.tensor_tensor(y5, y5, mrb, ALU.subtract)
                    e.inc(i, sVE, 1)
                    e.mark(sVE, ("p7y", j))

            def p7b_tensor():
                e = ET
                for j in range(NJ2):
                    for ci in range(24):
                        e.wait(("p7mol", j, ci))
                    for oc in range(8):
                        qq = j * 8 + oc
                        if qq >= 2:
                            e.wait(("p7ev", (qq - 2) // 8, (qq - 2) % 8))
                        pd = ps_dn[:, (oc % 2) * 512:(oc % 2) * 512 + TJ]
                        n = 0
                        for (ah, bh) in ((1, 1), (1, 0), (0, 1)):
                            for k in range(24):
                                w = (wdq_h if ah else wdq_l)[:, oc * FF + k * 128:
                                                             oc * FF + (k + 1) * 128]
                                xs = (mo_h if bh else mo_l)[:, k * TJ:(k + 1) * TJ]
                                i = e.matmul(pd, w, xs, start=(n == 0), stop=(n == 71))
                                n += 1
                        e.inc(i, sMM, 1)
                        e.mark(sMM, ("p7dn", j, oc))
                    e.wait(("p7sq", j))
                    if j >= 1:
                        e.wait(("p7lncp", j - 1))
                    e.wait(("p7x4all", j))
                    for k in range(8):
                        e.matmul(ps_r7a[0:1, 0:TJ], onesc_sb[:, :],
                                 x4b[:, k * TJ:(k + 1) * TJ], start=(k == 0), stop=(k == 7))
                        i = e.matmul(ps_r7b[0:1, 0:TJ], onesc_sb[:, :],
                                     sh7[:, k * TJ:(k + 1) * TJ], start=(k == 0),
                                     stop=(k == 7))
                    e.inc(i, sMM, 1)
                    e.mark(sMM, ("p7lnmm", j))
                    e.wait(("p7lnr", j))
                    e.matmul(ps_r7a[:, 0:TJ], onesr_sb[0:1, :], row(R_R, TJ),
                             start=True, stop=True)
                    i = e.matmul(ps_r7b[:, 0:TJ], onesr_sb[0:1, :], row(R_MR, TJ),
                                 start=True, stop=True)
                    e.inc(i, sMM, 1)
                    e.mark(sMM, ("p7lnbc", j))

            def p7b_gpsimd():
                e = EG
                for j in range(NJ2):
                    e.wait(("p7y", j))
                    e.dma(y_out.rearrange("(c p) t -> p c t", p=128)
                          [:, :, TJ * j:TJ * (j + 1)],
                          yb[:, :].rearrange("p (c t) -> p c t", t=TJ), sSP)
                    e.mark(sSP, ("p7ysp", j))
                e.wait_sem_now(sSP)

            run_phase([("sync", p7b_sync), ("scalar", p7b_scalar), ("vector", p7b_vector),
                       ("tensor", p7b_tensor), ("gpsimd", p7b_gpsimd)])

        sc_wd.close()

    return nc


_PROG_CACHE = {}


def _get_prog(T):
    if T not in _PROG_CACHE:
        _PROG_CACHE[T] = build_program(T)
    return _PROG_CACHE[T]


def make_inmaps(inputs, T):
    x = np.ascontiguousarray(np.asarray(inputs["L0_attn_x"]))
    B = x.shape[0]
    wqkv = np.ascontiguousarray(np.asarray(inputs["L0_attn_qkv_w"]).T)
    wproj = np.ascontiguousarray(np.asarray(inputs["L0_attn_proj_w"]).T)
    wc1 = np.ascontiguousarray(np.asarray(inputs["L0_attn_conv1_w"]).T)
    wc2 = np.ascontiguousarray(np.asarray(inputs["L0_attn_conv2_w"]).T)
    wup = np.ascontiguousarray(np.asarray(inputs["L0_ffn_w_gate" if False else "L0_ffn_w_up"]))
    wgate = np.ascontiguousarray(np.asarray(inputs["L0_ffn_w_gate"]))
    wdown = np.ascontiguousarray(np.asarray(inputs["L0_ffn_w_down"]))
    vecs = np.zeros((6, C), np.float32)
    vecs[0] = np.asarray(inputs["L0_attn_scale1"])
    vecs[1] = np.asarray(inputs["L0_attn_scale2"])
    vecs[2] = np.asarray(inputs["L0_attn_gamma"])
    vecs[3] = np.asarray(inputs["L0_attn_proj_b"])
    vecs[4] = np.asarray(inputs["L0_attn_conv2_b"])
    vecs = np.ascontiguousarray(vecs.reshape(6, 8, 128).transpose(2, 0, 1).reshape(128, 48))
    c1b = np.asarray(inputs["L0_attn_conv1_b"], dtype=np.float32)
    c1b = np.ascontiguousarray(c1b.reshape(16, 128).T)
    shared = dict(wqkv=wqkv, wproj=wproj, wc1=wc1, wc2=wc2, wup=wup,
                  wgate=wgate, wdown=wdown, vecs=vecs, c1b=c1b)
    in_maps = []
    for b in range(B):
        m = dict(shared)
        m["x"] = np.ascontiguousarray(x[b].T)
        in_maps.append(m)
    return in_maps


def kernel(**inputs):
    T = int(np.asarray(inputs["L0_attn_x"]).shape[1])
    nc = _get_prog(T)
    in_maps = make_inmaps(inputs, T)
    res = run_bass_kernel_spmd(nc, in_maps, core_ids=list(range(8)))
    out = np.stack([np.asarray(r["y"]).T for r in res.results], axis=0)
    return out.astype(np.float32)


# version 03
# speedup vs baseline: 1.0668x; 1.0668x over previous
import contextlib
import numpy as np
import concourse.bass as bass
import concourse.mybir as mybir
from concourse.bass_utils import run_bass_kernel_spmd
from concourse.bass_types import AP

F32 = mybir.dt.float32
F16 = mybir.dt.float16
I32 = mybir.dt.int32
AF = mybir.ActivationFunctionType
ALU = mybir.AluOpType
AX = mybir.AxisListType

C = 1024
NH = 16
HID = 2048
FF = 3072
INV_QMAX = 1.0 / 32767.0
RMS_K = float(1024 ** 0.5 + 1e-06)
DEBUG = False

_ENG_OPS = {
    "tensor_copy", "tensor_tensor", "tensor_add", "tensor_sub", "tensor_mul",
    "tensor_max", "tensor_scalar", "tensor_single_scalar", "tensor_scalar_add",
    "tensor_scalar_sub", "tensor_scalar_mul", "tensor_scalar_max",
    "tensor_scalar_min", "tensor_reduce", "reduce_max", "reduce_sum",
    "scalar_tensor_tensor", "select", "transpose", "bn_stats", "bn_aggr",
    "reciprocal", "memset", "memzero", "activation", "copy", "add", "mul",
    "sqrt", "square", "sign", "iota", "matmul", "collective_compute",
}


def _ap_names(args, kwargs):
    ins, outs = [], []

    def walk(v, into):
        if isinstance(v, AP):
            into.append(v.tensor.name)
        elif isinstance(v, (list, tuple)):
            for x in v:
                walk(x, into)

    for i, v in enumerate(args):
        walk(v, ins if i > 0 else outs)
    for k, v in kwargs.items():
        walk(v, outs if k in ("out", "accum_out", "outs") else ins)
    return ins, outs


class Sem:
    def __init__(self, h, name):
        self.h = h
        self.name = name
        self.val = 0


class Eng:

    def __init__(self, eng, ctx):
        self._eng = eng
        self._ctx = ctx
        self._touched = set()
        self._written = set()

    @property
    def counting(self):
        return self._ctx["counting"]

    def drain(self):
        if not self.counting:
            self._eng.drain()
        self._touched = set()
        self._written = set()

    def fence(self):
        self._touched = set()
        self._written = set()

    def mark(self, sem, key):
        if self.counting:
            assert key not in self._ctx["marks"], f"duplicate mark {key}"
            self._ctx["marks"][key] = (sem, sem.val)
        else:
            rec = self._ctx["marks"].get(key)
            assert rec is not None and rec[0] is sem and rec[1] == sem.val, \
                f"mark drift at {key}: recorded {rec and rec[1]} vs emit {sem.val}"

    def inc(self, inst, sem, n=1):
        if not self.counting:
            inst.then_inc(sem.h, n)
        sem.val += n

    def wait(self, key):
        if self.counting:
            return
        sem, v = self._ctx["marks"][key]
        self._eng.wait_ge(sem.h, v)

    def wait_sem_now(self, sem):
        if self.counting:
            return
        self._eng.wait_ge(sem.h, sem.val)

    def dma(self, out, in_, sem):
        if self.counting:
            sem.val += 16
            return None
        self._eng.wait_ge(sem.h, sem.val)
        inst = self._eng.dma_start(out=out, in_=in_)
        inst.then_inc(sem.h, 16)
        sem.val += 16
        return inst

    def cc(self, in_ap, out_ap, sem):
        if self.counting:
            sem.val += 1
            return None
        inst = self._eng.collective_compute(
            "AllReduce", ALU.max, replica_groups=[list(range(8))],
            ins=[in_ap], outs=[out_ap])
        inst.then_inc(sem.h, 1)
        sem.val += 1
        return inst

    def __getattr__(self, name):
        if name.startswith("_"):
            raise AttributeError(name)
        if name not in _ENG_OPS:
            fn = getattr(self._eng, name)
            if self.counting:
                return lambda *a, **k: None
            return fn

        def wrapped(*args, **kwargs):
            if self.counting:
                return None
            ins, outs = _ap_names(args, kwargs)
            if any(n in self._written for n in ins) or any(n in self._touched for n in outs):
                self.drain()
            r = getattr(self._eng, name)(*args, **kwargs)
            self._touched.update(ins + outs)
            self._written.update(outs)
            return r

        return wrapped


R_SQ, R_RC, R_RK, R_M, R_EX = 0, 512, 1024, 1536, 2048
R_VAR, R_SD, R_R, R_MR = R_SQ, R_RC, R_RK, R_EX


def build_program(T):
    NJ = T // 512
    NJ2 = T // 256
    TJ = 256
    nc = bass.Bass("TRN2", target_bir_lowering=False, debug=False)
    scratch_kind = "ExternalOutput" if DEBUG else "Internal"

    x_in = nc.dram_tensor("x", [C, T], F32, kind="ExternalInput").ap()
    wqkv_d = nc.dram_tensor("wqkv", [C, 3 * C], F32, kind="ExternalInput").ap()
    wproj_d = nc.dram_tensor("wproj", [C, C], F32, kind="ExternalInput").ap()
    wc1_d = nc.dram_tensor("wc1", [C, HID], F32, kind="ExternalInput").ap()
    wc2_d = nc.dram_tensor("wc2", [HID, C], F32, kind="ExternalInput").ap()
    wup_d = nc.dram_tensor("wup", [FF, C], F32, kind="ExternalInput").ap()
    wgate_d = nc.dram_tensor("wgate", [FF, C], F32, kind="ExternalInput").ap()
    wdown_d = nc.dram_tensor("wdown", [C, FF], F32, kind="ExternalInput").ap()
    wup_t = nc.dram_tensor("wup_t", [C, FF], F32, kind="ExternalInput").ap()
    wgate_t = nc.dram_tensor("wgate_t", [C, FF], F32, kind="ExternalInput").ap()
    wdown_t = nc.dram_tensor("wdown_t", [FF, C], F32, kind="ExternalInput").ap()
    vecs_d = nc.dram_tensor("vecs", [128, 48], F32, kind="ExternalInput").ap()
    c1b_d = nc.dram_tensor("c1b", [128, 16], F32, kind="ExternalInput").ap()
    y_out = nc.dram_tensor("y", [C, T], F32, kind="ExternalOutput").ap()

    def scratch(name, shape, dt):
        return nc.dram_tensor(name, shape, dt, kind=scratch_kind).ap()

    d_phiq = scratch("d_phiq", [C, T], F32)
    d_x2 = scratch("d_x2", [C, T], F32)
    d_y1 = scratch("d_y1", [C, T], F32)
    d_U = scratch("d_U", [FF, T], F32)
    d_G = scratch("d_G", [FF, T], F32)
    d_mu = scratch("d_mu", [FF, T], I32)
    d_P = scratch("d_P", [FF, T], F32)
    d_Q = scratch("d_Q", [FF, T], F32)
    cc_in = [nc.dram_tensor(f"cc_in{i}", [1, 2], F32).ap() for i in range(3)]
    cc_out = [nc.dram_tensor(f"cc_out{i}", [1, 2], F32).ap() for i in range(3)]

    eye32_c = nc.inline_tensor(np.eye(128, dtype=np.float32), name="eye32c").ap()
    eye16_c = nc.inline_tensor(np.eye(128, dtype=np.float16), name="eye16c").ap()
    onesr_c = nc.inline_tensor(np.ones((1, 128), np.float32), name="onesrc").ap()
    onesc_c = nc.inline_tensor(np.ones((128, 1), np.float32), name="onescc").ap()
    cbias_c = nc.inline_tensor(np.full((128, 1), 1e-5, np.float32), name="cbiasc").ap()
    eA = np.zeros((32, 8 * 128), np.float32)
    for ci in range(8):
        for p in range(128):
            eA[4 * ci + p // 32, ci * 128 + p] = 1.0
    eexpA_c = nc.inline_tensor(eA, name="eexpAc").ap()
    eB = np.zeros((96, 24 * 128), np.float32)
    for ci in range(24):
        for p in range(128):
            eB[4 * ci + p // 32, ci * 128 + p] = 1.0
    eexpB_c = nc.inline_tensor(eB, name="eexpBc").ap()

    es = contextlib.ExitStack()

    def sbuf(name, shape, dt):
        return es.enter_context(nc.sbuf_tensor(name, shape, dt))

    def mksem(name):
        return Sem(es.enter_context(nc.semaphore(name)), name)

    with es, nc.Block() as block:
        vec_sb = sbuf("vec_sb", [128, 48], F32)
        gpb_sb = sbuf("gpb_sb", [128, 16], F32)
        c1b_sb = sbuf("c1b_sb", [128, 16], F32)
        eye32_sb = sbuf("eye32_sb", [128, 128], F32)
        eye16_sb = sbuf("eye16_sb", [128, 128], F16)
        onesr_sb = sbuf("onesr_sb", [1, 128], F32)
        onesc_sb = sbuf("onesc_sb", [128, 1], F32)
        eexpA_sb = sbuf("eexpA_sb", [32, 8 * 128], F32)
        cbias_sb = sbuf("cbias_sb", [128, 1], F32)
        kvext = sbuf("kvext", [128, 8 * 65], F32)
        scal = sbuf("scal", [1, 32], F32)
        scbc = sbuf("scbc", [128, 8], F32)
        rowb = sbuf("rowb", [1, 5 * 512], F32)
        amaxc = sbuf("amaxc", [128, 8], F32)

        sLD = mksem("sLD")
        sL2 = mksem("sL2")
        sSP = mksem("sSP")
        sS2 = mksem("sS2")
        sMM = mksem("sMM")
        sEV = mksem("sEV")
        sVE = mksem("sVE")
        sGP = mksem("sGP")
        sPB = mksem("sPB")
        ALL_SEMS = [sLD, sL2, sSP, sS2, sMM, sEV, sVE, sGP, sPB]

        ctx = {"counting": False, "marks": {}}
        EV = Eng(nc.vector, ctx)
        EC = Eng(nc.scalar, ctx)
        EG = Eng(nc.gpsimd, ctx)
        ET = Eng(nc.tensor, ctx)
        EY = Eng(nc.sync, ctx)

        def vsl(row, ci):
            return vec_sb[:, row * 8 + ci: row * 8 + ci + 1]

        def row(slot, n=512):
            return rowb[0:1, slot:slot + n]

        def sc1(i):
            return scal[0:1, i:i + 1]

        _phase_no = [0]

        def run_phase(fns, label=None):
            label = label or f"ph{_phase_no[0]}"
            _phase_no[0] += 1
            snap = {s.name: s.val for s in ALL_SEMS}
            ctx["counting"] = True
            for _, fn in fns:
                fn()
                for e in (EV, EC, EG, ET, EY):
                    e.fence()
            for s in ALL_SEMS:
                s.val = snap[s.name]
            ctx["counting"] = False
            deco = {"sync": block.sync, "vector": block.vector, "scalar": block.scalar,
                    "gpsimd": block.gpsimd, "tensor": block.tensor}
            with nc.named_scope(label):
                for name, fn in fns:
                    deco[name](lambda e, fn=fn: fn())
            tgt = sPB.val + 5
            for dec, eng, qsems in ((block.sync, nc.sync, (sLD, sL2)),
                                    (block.vector, nc.vector, ()),
                                    (block.scalar, nc.scalar, ()),
                                    (block.gpsimd, nc.gpsimd, (sSP, sS2)),
                                    (block.tensor, nc.tensor, ())):
                def bfn(e, eng=eng, tgt=tgt, qsems=qsems):
                    for q in qsems:
                        eng.wait_ge(q.h, q.val)
                    if eng is not nc.sync:
                        eng.drain()
                    eng.nop().then_inc(sPB.h, 1)
                    eng.wait_ge(sPB.h, tgt)
                dec(bfn)
            sPB.val = tgt
            for e in (EV, EC, EG, ET, EY):
                e.fence()
            ctx["marks"] = {}

        sc_qkv = contextlib.ExitStack()
        wqkv_h = sc_qkv.enter_context(nc.sbuf_tensor("wqkv_h", [128, 8 * 3 * C], F16))
        wqkv_l = sc_qkv.enter_context(nc.sbuf_tensor("wqkv_l", [128, 8 * 3 * C], F16))
        def emit_wprep(tag, plan, wstg, wtmp, stg_cols, after_consts):
            items = []
            for (src, nch, ncols, fold, dh, dl) in plan:
                for ci in range(nch):
                    items.append((src, ci, ncols, fold, dh, dl))

            def w_sync():
                e = EY
                for nb, (src, ci, ncols, fold, dh, dl) in enumerate(items):
                    if nb >= 2:
                        e.wait((tag, "lo", nb - 2))
                    e.dma(wstg[:, (nb % 2) * stg_cols:(nb % 2) * stg_cols + ncols],
                          src[ci * 128:(ci + 1) * 128, :], sLD)
                    e.mark(sLD, (tag, "w", nb))

            def w_scalar():
                e = EC
                if after_consts:
                    e.wait("p0consts")
                for nb, (src, ci, ncols, fold, dh, dl) in enumerate(items):
                    e.wait((tag, "w", nb))
                    if nb >= 1:
                        e.wait((tag, "lo", nb - 1))
                    stg = wstg[:, (nb % 2) * stg_cols:(nb % 2) * stg_cols + ncols]
                    sc = vsl(fold, ci) if fold is not None else 1.0
                    e.activation(dh[:, ci * ncols:(ci + 1) * ncols], stg,
                                 AF.Identity, scale=sc)
                    i = e.activation(wtmp[:, 0:ncols], stg, AF.Identity, scale=sc)
                    e.inc(i, sEV, 1)
                    e.mark(sEV, (tag, "hi", nb))

            def w_vector():
                e = EV
                for nb, (src, ci, ncols, fold, dh, dl) in enumerate(items):
                    e.wait((tag, "hi", nb))
                    i = e.tensor_tensor(dl[:, ci * ncols:(ci + 1) * ncols],
                                        wtmp[:, 0:ncols],
                                        dh[:, ci * ncols:(ci + 1) * ncols],
                                        ALU.subtract)
                    e.inc(i, sVE, 1)
                    e.mark(sVE, (tag, "lo", nb))

            return w_sync, w_scalar, w_vector

        with nc.sbuf_tensor("wstg", [128, 2 * 3 * C], F32) as wstg, \
             nc.sbuf_tensor("wtmp", [128, 3 * C], F32) as wtmp:

            ws0, wc0, wv0 = emit_wprep(
                "p0", [(wqkv_d, 8, 3 * C, 0, wqkv_h, wqkv_l)],
                wstg, wtmp, 3 * C, True)

            def p0_sync():
                e = EY
                e.dma(eye32_sb[:, :], eye32_c[:, :], sLD)
                e.dma(eye16_sb[:, :], eye16_c[:, :], sLD)
                e.dma(onesr_sb[:, :], onesr_c[:, :], sLD)
                e.dma(onesc_sb[:, :], onesc_c[:, :], sLD)
                e.dma(eexpA_sb[:, :], eexpA_c[:, :], sLD)
                e.dma(cbias_sb[:, :], cbias_c[:, :], sLD)
                e.dma(vec_sb[:, :], vecs_d[:, :], sLD)
                e.dma(c1b_sb[:, :], c1b_d[:, :], sLD)
                e.mark(sLD, "p0consts")
                ws0()

            def p0_vector():
                e = EV
                e.wait("p0consts")
                e.tensor_tensor(gpb_sb[:, 0:8], vec_sb[:, 16:24], vec_sb[:, 24:32], ALU.mult)
                e.tensor_tensor(gpb_sb[:, 8:16], vec_sb[:, 16:24], vec_sb[:, 32:40], ALU.mult)
                wv0()

            run_phase([("sync", p0_sync), ("scalar", wc0), ("vector", p0_vector)], "p0")

        with nc.sbuf_tensor("xbuf", [128, 8 * 512], F32) as xbuf, \
             nc.sbuf_tensor("sh1", [128, 8 * 512], F32) as sh1, \
             nc.sbuf_tensor("r1bc", [128, 512], F32) as r1bc, \
             nc.sbuf_tensor("xnh", [128, 8 * 512], F16) as xnh, \
             nc.sbuf_tensor("xnl", [128, 8 * 512], F16) as xnl, \
             nc.sbuf_tensor("phib", [128, 2 * 512], F32) as phib, \
             nc.sbuf_tensor("k_sb", [128, 1024], F32) as k_sb, \
             nc.sbuf_tensor("v_sb", [128, 1024], F32) as v_sb, \
             nc.sbuf_tensor("qtmp", [128, 512], F32) as qtmp, \
             nc.psum_tensor("ps_misc", [128, 512], F32) as ps_misc, \
             nc.psum_tensor("ps_q", [128, 1024], F32) as ps_q, \
             nc.psum_tensor("ps_kv", [128, 1024], F32) as ps_kv, \
             nc.psum_tensor("ps_acc", [128, 1024], F32) as ps_acc, \
             nc.psum_tensor("ps_ks", [128, 16], F32) as ps_ks:

            def wq_sl(hi, k, a, b):
                w = wqkv_h if hi else wqkv_l
                return w[:, k * 3 * C + a:k * 3 * C + b]

            def p1_sync():
                e = EY
                x3 = x_in.rearrange("(c p) t -> p c t", p=128)
                for j in range(NJ):
                    if j >= 1:
                        e.wait(("p1xn", j - 1))
                    e.dma(xbuf[:, :].rearrange("p (c t) -> p c t", t=512),
                          x3[:, :, 512 * j:512 * (j + 1)], sLD)
                    e.mark(sLD, ("p1x", j))

            def p1_scalar():
                e = EC
                for j in range(NJ):
                    e.wait(("p1x", j))
                    if j >= 1:
                        e.wait(("p1xn", j - 1))
                    i = e.activation(sh1[:, :], xbuf[:, :], AF.Square)
                    e.inc(i, sEV, 1)
                    e.mark(sEV, ("p1sq", j))
                    e.wait(("p1ss", j))
                    i = e.activation(row(R_SQ), ps_misc[0:1, 0:512], AF.Sqrt)
                    e.inc(i, sEV, 1)
                    e.mark(sEV, ("p1sqrt", j))
                    for co in range(8):
                        e.wait(("p1mmq", j, co))
                        gg = j * 8 + co
                        if gg >= 2:
                            e.wait(("p1phisp", (gg - 2) // 8, (gg - 2) % 8))
                        pq = ps_q[:, (co % 2) * 512:(co % 2) * 512 + 512]
                        e.activation(qtmp[:, :], pq, AF.Square)
                        i = e.activation(phib[:, (co % 2) * 512:(co % 2) * 512 + 512],
                                         qtmp[:, :], AF.Exp, scale=-0.5)
                        e.inc(i, sEV, 1)
                        e.mark(sEV, ("p1phi", j, co))
                    for tt in range(4):
                        for fo in range(4):
                            e.wait(("p1mmkv", j, tt, fo))
                            if fo == 0 and (tt > 0 or j > 0):
                                e.wait(("p1kvb", j if tt > 0 else j - 1,
                                        tt - 1 if tt > 0 else 3))
                            pkv = ps_kv[:, (fo % 2) * 512:(fo % 2) * 512 + 512]
                            if fo < 2:
                                e.activation(qtmp[:, :], pkv, AF.Square)
                                i = e.activation(k_sb[:, fo * 512:(fo + 1) * 512],
                                                 qtmp[:, :], AF.Exp, scale=-0.5)
                            else:
                                i = e.copy(v_sb[:, (fo - 2) * 512:(fo - 1) * 512], pkv)
                            e.inc(i, sEV, 1)
                            e.mark(sEV, ("p1kvev", j, tt, fo))

            def p1_vector():
                e = EV
                for j in range(NJ):
                    e.wait(("p1sqrt", j))
                    e.reciprocal(row(R_RC), row(R_SQ))
                    i = e.tensor_scalar(row(R_RK), row(R_RC), RMS_K, None, ALU.mult)
                    e.inc(i, sVE, 1)
                    e.mark(sVE, ("p1r1row", j))
                    e.wait(("p1r1bcmm", j))
                    i = e.tensor_copy(r1bc[:, :], ps_misc[:, :])
                    e.inc(i, sVE, 1)
                    e.mark(sVE, ("p1r1bc", j))
                    r1b3 = r1bc[:, :].unsqueeze(1).broadcast_to([128, 8, 512])
                    x4 = xbuf[:, :].rearrange("p (c t) -> p c t", t=512)
                    e.tensor_tensor(xnh[:, :].rearrange("p (c t) -> p c t", t=512),
                                    x4, r1b3, ALU.mult)
                    e.tensor_tensor(sh1[:, :].rearrange("p (c t) -> p c t", t=512),
                                    x4, r1b3, ALU.mult)
                    i = e.tensor_tensor(xnl[:, :], sh1[:, :], xnh[:, :], ALU.subtract)
                    e.inc(i, sVE, 1)
                    e.mark(sVE, ("p1xn", j))

            def p1_tensor():
                e = ET
                for j in range(NJ):
                    e.wait(("p1sq", j))
                    if j >= 1:
                        e.wait(("p1r1bc", j - 1))
                    for k in range(8):
                        i = e.matmul(ps_misc[0:1, 0:512], onesc_sb[:, :],
                                     sh1[:, k * 512:(k + 1) * 512],
                                     start=(k == 0), stop=(k == 7))
                    e.inc(i, sMM, 1)
                    e.mark(sMM, ("p1ss", j))
                    e.wait(("p1sqrt", j))
                    e.wait(("p1r1row", j))
                    i = e.matmul(ps_misc[:, :], onesr_sb[0:1, :], row(R_RK),
                                 start=True, stop=True)
                    e.inc(i, sMM, 1)
                    e.mark(sMM, ("p1r1bcmm", j))
                    e.wait(("p1xn", j))
                    for co in range(8):
                        gg = j * 8 + co
                        if gg >= 2:
                            e.wait(("p1phi", (gg - 2) // 8, (gg - 2) % 8))
                        pq = ps_q[:, (co % 2) * 512:(co % 2) * 512 + 512]
                        n = 0
                        for (ah, bh) in ((1, 1), (1, 0), (0, 1)):
                            for k in range(8):
                                xs = (xnh if bh else xnl)[:, k * 512:(k + 1) * 512]
                                i = e.matmul(pq, wq_sl(ah, k, co * 128, (co + 1) * 128),
                                             xs, start=(n == 0), stop=(n == 23))
                                n += 1
                        e.inc(i, sMM, 1)
                        e.mark(sMM, ("p1mmq", j, co))
                    for tt in range(4):
                        for fo in range(4):
                            ff = j * 16 + tt * 4 + fo
                            if ff >= 2:
                                pf = ff - 2
                                e.wait(("p1kvev", pf // 16, (pf % 16) // 4, (pf % 16) % 4))
                            pkv = ps_kv[:, (fo % 2) * 512:(fo % 2) * 512 + 512]
                            n = 0
                            for (ah, bh) in ((1, 1), (1, 0), (0, 1)):
                                for k in range(8):
                                    xs = (xnh if ah else xnl)[:, k * 512 + tt * 128:
                                                              k * 512 + tt * 128 + 128]
                                    ws_ = wq_sl(bh, k, C + fo * 512, C + (fo + 1) * 512)
                                    i = e.matmul(pkv, xs, ws_, start=(n == 0), stop=(n == 23))
                                    n += 1
                            e.inc(i, sMM, 1)
                            e.mark(sMM, ("p1mmkv", j, tt, fo))
                        for fo in range(4):
                            e.wait(("p1kvev", j, tt, fo))
                        first = (j == 0 and tt == 0)
                        last = (j == NJ - 1 and tt == 3)
                        for p in range(8):
                            e.matmul(ps_acc[:, p * 128:(p + 1) * 128],
                                     k_sb[:, p * 128:(p + 1) * 128],
                                     v_sb[:, p * 128:(p + 1) * 128],
                                     start=(first and p % 4 == 0),
                                     stop=(last and p % 4 == 3))
                            i = e.matmul(ps_ks[:, p:p + 1],
                                         k_sb[:, p * 128:(p + 1) * 128],
                                         onesc_sb[:, :], start=(first and p == 0),
                                         stop=(last and p == 7))
                        e.inc(i, sMM, 1)
                        e.mark(sMM, ("p1kvb", j, tt))

            def p1_gpsimd():
                e = EG
                for j in range(NJ):
                    for co in range(8):
                        e.wait(("p1phi", j, co))
                        e.dma(d_phiq[co * 128:(co + 1) * 128, 512 * j:512 * (j + 1)],
                              phib[:, (co % 2) * 512:(co % 2) * 512 + 512], sSP)
                        e.mark(sSP, ("p1phisp", j, co))

            run_phase([("sync", p1_sync), ("scalar", p1_scalar), ("vector", p1_vector),
                       ("tensor", p1_tensor), ("gpsimd", p1_gpsimd)], "p1")

            def p15_vector():
                e = EV
                for p in range(8):
                    e.tensor_copy(kvext[0:64, p * 65:p * 65 + 64],
                                  ps_acc[0:64, p * 128:p * 128 + 64])
                    e.tensor_copy(kvext[64:128, p * 65:p * 65 + 64],
                                  ps_acc[64:128, p * 128 + 64:p * 128 + 128])
                    e.tensor_copy(kvext[:, p * 65 + 64:p * 65 + 65], ps_ks[:, p:p + 1])

            run_phase([("vector", p15_vector)], "p15")

        sc_qkv.close()

        sc_proj = contextlib.ExitStack()
        wproj_h = sc_proj.enter_context(nc.sbuf_tensor("wproj_h", [128, 8 * C], F16))
        wproj_l = sc_proj.enter_context(nc.sbuf_tensor("wproj_l", [128, 8 * C], F16))
        with nc.sbuf_tensor("wstgp", [128, 2 * C], F32) as wstgp, \
             nc.sbuf_tensor("wtmpp", [128, C], F32) as wtmpp:
            pws, pwc, pwv = emit_wprep(
                "p2p", [(wproj_d, 8, C, None, wproj_h, wproj_l)],
                wstgp, wtmpp, C, False)
            run_phase([("sync", pws), ("scalar", pwc), ("vector", pwv)], "p2pre")

        with nc.sbuf_tensor("qbuf", [128, 8 * 512], F32) as qbuf, \
             nc.sbuf_tensor("atile", [128, 1024], F32) as atile, \
             nc.sbuf_tensor("zt", [128, 16], F32) as zt, \
             nc.sbuf_tensor("athi", [128, 8 * 512], F16) as athi, \
             nc.sbuf_tensor("atlo", [128, 8 * 512], F16) as atlo, \
             nc.sbuf_tensor("xck", [128, 2 * 512], F32) as xck, \
             nc.sbuf_tensor("ptmp", [128, 2 * 512], F32) as ptmp, \
             nc.sbuf_tensor("x2b", [128, 2 * 512], F32) as x2b, \
             nc.psum_tensor("ps_ap", [128, 1024], F32) as ps_ap, \
             nc.psum_tensor("ps_tr", [128, 1024], F32) as ps_tr, \
             nc.psum_tensor("ps_pr", [128, 2 * 512], F32) as ps_pr:

            def p2_sync():
                e = EY
                ph3 = d_phiq.rearrange("(c p) t -> p c t", p=128)
                x3 = x_in.rearrange("(c p) t -> p c t", p=128)
                for j in range(NJ):
                    if j >= 1:
                        e.wait(("p2apdone", j - 1))
                    e.dma(qbuf[:, :].rearrange("p (c t) -> p c t", t=512),
                          ph3[:, :, 512 * j:512 * (j + 1)], sLD)
                    e.mark(sLD, ("p2q", j))
                    for oc in range(8):
                        qq = j * 8 + oc
                        if qq >= 2:
                            e.wait(("p2x2v", (qq - 2) // 8, (qq - 2) % 8))
                        e.dma(xck[:, (oc % 2) * 512:(oc % 2) * 512 + 512],
                              x3[:, oc, 512 * j:512 * (j + 1)], sLD)
                        e.mark(sLD, ("p2xc", j, oc))

            def p2_tensor():
                e = ET
                for j in range(NJ):
                    e.wait(("p2q", j))
                    for tt in range(4):
                        for h in range(NH):
                            g = h // 4
                            gg = j * 16 + tt * 4 + g
                            if h % 4 == 0 and gg >= 2:
                                pz = gg - 2
                                e.wait(("p2zev", pz // 16, (pz % 16) // 4, (pz % 16) % 4))
                            pr, hf = h // 2, h % 2
                            qsl = qbuf[64 * hf:64 * (hf + 1),
                                       pr * 512 + tt * 128:pr * 512 + tt * 128 + 128]
                            ksl = kvext[64 * hf:64 * (hf + 1), pr * 65:(pr + 1) * 65]
                            i = e.matmul(ps_ap[:, (g % 2) * 512 + (h % 4) * 65:
                                               (g % 2) * 512 + (h % 4) * 65 + 65],
                                         qsl, ksl, start=True, stop=True)
                            if h % 4 == 3:
                                e.inc(i, sMM, 1)
                                e.mark(sMM, ("p2ap", j, tt, g))
                        for p in range(8):
                            e.wait(("p2at", j, tt, p // 2))
                            pp = (j * 4 + tt) * 8 + p
                            if pp >= 2:
                                q_, p_ = divmod(pp - 2, 8)
                                j_, tt_ = divmod(q_, 4)
                                e.wait(("p2trlo", j_, tt_, p_))
                            i = e.transpose(ps_tr[:, (p % 2) * 512:(p % 2) * 512 + 128],
                                            atile[:, p * 128:(p + 1) * 128], eye32_sb[:, :])
                            e.inc(i, sMM, 1)
                            e.mark(sMM, ("p2tr", j, tt, p))
                    e.mark(sMM, ("p2apdone", j))
                    e.wait(("p2trall", j))
                    for oc in range(8):
                        qq = j * 8 + oc
                        if qq >= 2:
                            e.wait(("p2ev", (qq - 2) // 8, (qq - 2) % 8))
                        pp_ = ps_pr[:, (oc % 2) * 512:(oc % 2) * 512 + 512]
                        n = 0
                        for (ah, bh) in ((1, 1), (1, 0), (0, 1)):
                            for k in range(8):
                                w = (wproj_h if ah else wproj_l)[:, k * C + oc * 128:
                                                                 k * C + (oc + 1) * 128]
                                xs = (athi if bh else atlo)[:, k * 512:(k + 1) * 512]
                                i = e.matmul(pp_, w, xs, start=(n == 0), stop=(n == 23))
                                n += 1
                        e.inc(i, sMM, 1)
                        e.mark(sMM, ("p2pr", j, oc))

            def p2_vector():
                e = EV
                for j in range(NJ):
                    for tt in range(4):
                        for g in range(4):
                            e.wait(("p2ap", j, tt, g))
                            if not (j == 0 and tt == 0):
                                pj, ptt = (j, tt - 1) if tt > 0 else (j - 1, 3)
                                e.wait(("p2tr", pj, ptt, 2 * g + 1))
                            pa = ps_ap[:, (g % 2) * 512:(g % 2) * 512 + 260]
                            den = pa.rearrange("p (h w) -> p h w", w=65)[:, :, 64:65]
                            e.tensor_scalar(zt[:, g * 4:(g + 1) * 4],
                                            den.rearrange("p h w -> p (h w)"),
                                            1e-6, None, ALU.add)
                            e.reciprocal(zt[:, g * 4:(g + 1) * 4], zt[:, g * 4:(g + 1) * 4])
                            for hh in range(4):
                                h = g * 4 + hh
                                i = e.tensor_scalar(atile[:, h * 64:(h + 1) * 64],
                                                    pa[:, hh * 65:hh * 65 + 64],
                                                    zt[:, h:h + 1], None, ALU.mult)
                            e.inc(i, sVE, 1)
                            e.mark(sVE, ("p2zev", j, tt, g))
                            e.mark(sVE, ("p2at", j, tt, g))
                        for p in range(8):
                            e.wait(("p2trhi", j, tt, p))
                            dst = atlo[:, p * 512 + tt * 128:p * 512 + tt * 128 + 128]
                            src_h = athi[:, p * 512 + tt * 128:p * 512 + tt * 128 + 128]
                            i = e.tensor_tensor(dst,
                                                ps_tr[:, (p % 2) * 512:(p % 2) * 512 + 128],
                                                src_h, ALU.subtract)
                            e.inc(i, sVE, 1)
                            e.mark(sVE, ("p2trlo", j, tt, p))
                    e.mark(sVE, ("p2trall", j))
                    for oc in range(8):
                        e.wait(("p2ev", j, oc))
                        e.wait(("p2xc", j, oc))
                        qq = j * 8 + oc
                        if qq >= 2:
                            e.wait(("p2x2sp", (qq - 2) // 8, (qq - 2) % 8))
                        i = e.tensor_tensor(x2b[:, (oc % 2) * 512:(oc % 2) * 512 + 512],
                                            ptmp[:, (oc % 2) * 512:(oc % 2) * 512 + 512],
                                            xck[:, (oc % 2) * 512:(oc % 2) * 512 + 512],
                                            ALU.add)
                        e.inc(i, sVE, 1)
                        e.mark(sVE, ("p2x2v", j, oc))

            def p2_scalar():
                e = EC
                for j in range(NJ):
                    for tt in range(4):
                        for p in range(8):
                            e.wait(("p2tr", j, tt, p))
                            i = e.copy(athi[:, p * 512 + tt * 128:p * 512 + tt * 128 + 128],
                                       ps_tr[:, (p % 2) * 512:(p % 2) * 512 + 128])
                            e.inc(i, sEV, 1)
                            e.mark(sEV, ("p2trhi", j, tt, p))
                    for oc in range(8):
                        e.wait(("p2pr", j, oc))
                        qq = j * 8 + oc
                        if qq >= 2:
                            e.wait(("p2x2v", (qq - 2) // 8, (qq - 2) % 8))
                        i = e.activation(ptmp[:, (oc % 2) * 512:(oc % 2) * 512 + 512],
                                         ps_pr[:, (oc % 2) * 512:(oc % 2) * 512 + 512],
                                         AF.Identity, scale=vsl(2, oc),
                                         bias=gpb_sb[:, oc:oc + 1])
                        e.inc(i, sEV, 1)
                        e.mark(sEV, ("p2ev", j, oc))

            def p2_gpsimd():
                e = EG
                for j in range(NJ):
                    for oc in range(8):
                        e.wait(("p2x2v", j, oc))
                        e.dma(d_x2[oc * 128:(oc + 1) * 128, 512 * j:512 * (j + 1)],
                              x2b[:, (oc % 2) * 512:(oc % 2) * 512 + 512], sSP)
                        e.mark(sSP, ("p2x2sp", j, oc))

            run_phase([("sync", p2_sync), ("tensor", p2_tensor), ("vector", p2_vector),
                       ("scalar", p2_scalar), ("gpsimd", p2_gpsimd)], "p2")

        sc_proj.close()

        sc_c12 = contextlib.ExitStack()
        wc1_h = sc_c12.enter_context(nc.sbuf_tensor("wc1_h", [128, 8 * HID], F16))
        wc1_l = sc_c12.enter_context(nc.sbuf_tensor("wc1_l", [128, 8 * HID], F16))
        wc2_h = sc_c12.enter_context(nc.sbuf_tensor("wc2_h", [128, 16 * C], F16))
        wc2_l = sc_c12.enter_context(nc.sbuf_tensor("wc2_l", [128, 16 * C], F16))
        with nc.sbuf_tensor("wstg2", [128, 2 * HID], F32) as wstg2, \
             nc.sbuf_tensor("wtmp2", [128, HID], F32) as wtmp2:
            cw_s, cw_c, cw_v = emit_wprep(
                "p3w", [(wc1_d, 8, HID, 1, wc1_h, wc1_l),
                        (wc2_d, 16, C, None, wc2_h, wc2_l)],
                wstg2, wtmp2, HID, False)
            run_phase([("sync", cw_s), ("scalar", cw_c), ("vector", cw_v)], "p3w")

        with nc.sbuf_tensor("x2t", [128, 8 * TJ], F32) as x2t, \
             nc.sbuf_tensor("sh3", [128, 8 * TJ], F32) as sh3, \
             nc.sbuf_tensor("xn2h", [128, 8 * TJ], F16) as xn2h, \
             nc.sbuf_tensor("xn2l", [128, 8 * TJ], F16) as xn2l, \
             nc.sbuf_tensor("rbc3", [128, 2 * TJ], F32) as rbc3, \
             nc.sbuf_tensor("h1b", [128, 2 * TJ], F32) as h1b, \
             nc.sbuf_tensor("h1h", [128, 16 * TJ], F16) as h1h, \
             nc.sbuf_tensor("h1l", [128, 16 * TJ], F16) as h1l, \
             nc.sbuf_tensor("pt3", [128, 2 * TJ], F32) as pt3, \
             nc.sbuf_tensor("x3b", [128, 8 * TJ], F32) as x3b, \
             nc.sbuf_tensor("y1b", [128, 8 * TJ], F32) as y1b, \
             nc.psum_tensor("ps_c1", [128, 1024], F32) as ps_c1, \
             nc.psum_tensor("ps_c2", [128, 1024], F32) as ps_c2, \
             nc.psum_tensor("ps_ra", [128, 512], F32) as ps_ra, \
             nc.psum_tensor("ps_rb", [128, 512], F32) as ps_rb:

            def p3_sync():
                e = EY
                x23 = d_x2.rearrange("(c p) t -> p c t", p=128)
                x03 = x_in.rearrange("(c p) t -> p c t", p=128)
                for j in range(NJ2):
                    if j >= 1:
                        e.wait(("p3x3all", j - 1))
                    e.dma(x2t[:, :].rearrange("p (c t) -> p c t", t=TJ),
                          x23[:, :, TJ * j:TJ * (j + 1)], sLD)
                    e.mark(sLD, ("p3x2ld", j))
                    if j >= 1:
                        e.wait(("p3y1sp", j - 1))
                    e.dma(y1b[:, :].rearrange("p (c t) -> p c t", t=TJ),
                          x03[:, :, TJ * j:TJ * (j + 1)], sL2)
                    e.mark(sL2, ("p3x0", j))

            def p3_scalar():
                e = EC
                for j in range(NJ2):
                    e.wait(("p3x2ld", j))
                    if j >= 1:
                        e.wait(("p3xn2", j - 1))
                    i = e.activation(sh3[:, :], x2t[:, :], AF.Square)
                    e.inc(i, sEV, 1)
                    e.mark(sEV, ("p3sq", j))
                    e.wait(("p3ss", j))
                    i = e.activation(row(R_SQ, TJ), ps_ra[0:1, 0:TJ], AF.Sqrt)
                    e.inc(i, sEV, 1)
                    e.mark(sEV, ("p3sqrt", j))
                    for ho in range(16):
                        e.wait(("p3mc1", j, ho))
                        hh = j * 16 + ho
                        if hh >= 2:
                            e.wait(("p3h1lo", (hh - 2) // 16, (hh - 2) % 16))
                        pc = ps_c1[:, (ho % 2) * 512:(ho % 2) * 512 + TJ]
                        e.activation(h1b[:, (ho % 2) * TJ:(ho % 2) * TJ + TJ], pc,
                                     AF.Gelu, bias=c1b_sb[:, ho:ho + 1])
                        i = e.copy(h1h[:, ho * TJ:(ho + 1) * TJ],
                                   h1b[:, (ho % 2) * TJ:(ho % 2) * TJ + TJ])
                        e.inc(i, sEV, 1)
                        e.mark(sEV, ("p3gel", j, ho))
                    for oc in range(8):
                        e.wait(("p3mc2", j, oc))
                        qq = j * 8 + oc
                        if qq >= 2:
                            e.wait(("p3x3", (qq - 2) // 8, (qq - 2) % 8))
                        i = e.activation(pt3[:, (oc % 2) * TJ:(oc % 2) * TJ + TJ],
                                         ps_c2[:, (oc % 2) * 512:(oc % 2) * 512 + TJ],
                                         AF.Identity, scale=vsl(2, oc),
                                         bias=gpb_sb[:, 8 + oc:9 + oc])
                        e.inc(i, sEV, 1)
                        e.mark(sEV, ("p3ev2", j, oc))
                    e.wait(("p3x3all", j))
                    i = e.activation(sh3[:, :], x3b[:, :], AF.Square)
                    e.inc(i, sEV, 1)
                    e.mark(sEV, ("p3sq3", j))
                    e.wait(("p3ln1mm", j))
                    e.activation(row(R_M, TJ), ps_ra[0:1, 0:TJ], AF.Identity,
                                 scale=1.0 / 1024.0)
                    i = e.activation(row(R_EX, TJ), ps_rb[0:1, 0:TJ], AF.Identity,
                                     scale=1.0 / 1024.0)
                    e.inc(i, sEV, 1)
                    e.mark(sEV, ("p3rows", j))
                    e.wait(("p3var", j))
                    i = e.activation(row(R_SD, TJ), row(R_VAR, TJ), AF.Sqrt, bias=cbias_sb[0:1, 0:1])
                    e.inc(i, sEV, 1)
                    e.mark(sEV, ("p3sd", j))

            def p3_vector():
                e = EV
                e.memset(amaxc[:, 0:1], 0.0)
                for j in range(NJ2):
                    e.wait(("p3sqrt", j))
                    e.reciprocal(row(R_RC, TJ), row(R_SQ, TJ))
                    i = e.tensor_scalar(row(R_RK, TJ), row(R_RC, TJ), RMS_K, None,
                                        ALU.mult)
                    e.inc(i, sVE, 1)
                    e.mark(sVE, ("p3r2row", j))
                    e.wait(("p3r2bc", j))
                    i = e.tensor_copy(rbc3[:, 0:TJ], ps_ra[:, 0:TJ])
                    e.inc(i, sVE, 1)
                    e.mark(sVE, ("p3r2cp", j))
                    rb3 = rbc3[:, 0:TJ].unsqueeze(1).broadcast_to([128, 8, TJ])
                    x4 = x2t[:, :].rearrange("p (c t) -> p c t", t=TJ)
                    e.tensor_tensor(xn2h[:, :].rearrange("p (c t) -> p c t", t=TJ),
                                    x4, rb3, ALU.mult)
                    e.tensor_tensor(sh3[:, :].rearrange("p (c t) -> p c t", t=TJ),
                                    x4, rb3, ALU.mult)
                    i = e.tensor_tensor(xn2l[:, :], sh3[:, :], xn2h[:, :], ALU.subtract)
                    e.inc(i, sVE, 1)
                    e.mark(sVE, ("p3xn2", j))
                    for ho in range(16):
                        e.wait(("p3gel", j, ho))
                        i = e.tensor_tensor(h1l[:, ho * TJ:(ho + 1) * TJ],
                                            h1b[:, (ho % 2) * TJ:(ho % 2) * TJ + TJ],
                                            h1h[:, ho * TJ:(ho + 1) * TJ], ALU.subtract)
                        e.inc(i, sVE, 1)
                        e.mark(sVE, ("p3h1lo", j, ho))
                    for oc in range(8):
                        e.wait(("p3ev2", j, oc))
                        e.wait(("p3x0", j))
                        e.tensor_tensor(x3b[:, oc * TJ:(oc + 1) * TJ],
                                        pt3[:, (oc % 2) * TJ:(oc % 2) * TJ + TJ],
                                        x2t[:, oc * TJ:(oc + 1) * TJ], ALU.add)
                        i = e.tensor_tensor(x3b[:, oc * TJ:(oc + 1) * TJ],
                                            x3b[:, oc * TJ:(oc + 1) * TJ],
                                            y1b[:, oc * TJ:(oc + 1) * TJ], ALU.add)
                        e.inc(i, sVE, 1)
                        e.mark(sVE, ("p3x3", j, oc))
                    e.mark(sVE, ("p3x3all", j))
                    e.wait(("p3rows", j))
                    e.tensor_tensor(row(R_VAR, TJ), row(R_M, TJ), row(R_M, TJ), ALU.mult)
                    i = e.tensor_tensor(row(R_VAR, TJ), row(R_EX, TJ), row(R_VAR, TJ),
                                        ALU.subtract)
                    e.inc(i, sVE, 1)
                    e.mark(sVE, ("p3var", j))
                    e.wait(("p3sd", j))
                    e.reciprocal(row(R_R, TJ), row(R_SD, TJ))
                    i = e.tensor_tensor(row(R_MR, TJ), row(R_M, TJ), row(R_R, TJ), ALU.mult)
                    e.inc(i, sVE, 1)
                    e.mark(sVE, ("p3lnr", j))
                    e.wait(("p3lnbc", j))
                    if j >= 1:
                        e.wait(("p3y1sp", j - 1))
                    e.tensor_copy(rbc3[:, 0:TJ], ps_ra[:, 0:TJ])
                    i = e.tensor_copy(rbc3[:, TJ:2 * TJ], ps_rb[:, 0:TJ])
                    e.inc(i, sVE, 1)
                    e.mark(sVE, ("p3lncp", j))
                    rb = rbc3[:, 0:TJ].unsqueeze(1).broadcast_to([128, 8, TJ])
                    mrb = rbc3[:, TJ:2 * TJ].unsqueeze(1).broadcast_to([128, 8, TJ])
                    x5 = x3b[:, :].rearrange("p (c t) -> p c t", t=TJ)
                    y5 = y1b[:, :].rearrange("p (c t) -> p c t", t=TJ)
                    e.tensor_tensor(y5, x5, rb, ALU.mult)
                    e.tensor_tensor(y5, y5, mrb, ALU.subtract)
                    e.tensor_reduce(amaxc[:, 4:5], y1b[:, :], AX.X, ALU.max,
                                    apply_absolute_value=True)
                    i = e.tensor_tensor(amaxc[:, 0:1], amaxc[:, 0:1], amaxc[:, 4:5],
                                        ALU.max)
                    e.inc(i, sVE, 1)
                    e.mark(sVE, ("p3y1", j))

            def p3_tensor():
                e = ET
                for j in range(NJ2):
                    e.wait(("p3sq", j))
                    if j >= 1:
                        e.wait(("p3lncp", j - 1))
                    for k in range(8):
                        i = e.matmul(ps_ra[0:1, 0:TJ], onesc_sb[:, :],
                                     sh3[:, k * TJ:(k + 1) * TJ],
                                     start=(k == 0), stop=(k == 7))
                    e.inc(i, sMM, 1)
                    e.mark(sMM, ("p3ss", j))
                    e.wait(("p3sqrt", j))
                    e.wait(("p3r2row", j))
                    i = e.matmul(ps_ra[:, 0:TJ], onesr_sb[0:1, :],
                                 row(R_RK, TJ), start=True, stop=True)
                    e.inc(i, sMM, 1)
                    e.mark(sMM, ("p3r2bc", j))
                    e.wait(("p3xn2", j))
                    for ho in range(16):
                        hh = j * 16 + ho
                        if hh >= 2:
                            e.wait(("p3gel", (hh - 2) // 16, (hh - 2) % 16))
                        pc = ps_c1[:, (ho % 2) * 512:(ho % 2) * 512 + TJ]
                        n = 0
                        for (ah, bh) in ((1, 1), (1, 0), (0, 1)):
                            for k in range(8):
                                w = (wc1_h if ah else wc1_l)[:, k * HID + ho * 128:
                                                             k * HID + (ho + 1) * 128]
                                xs = (xn2h if bh else xn2l)[:, k * TJ:(k + 1) * TJ]
                                i = e.matmul(pc, w, xs, start=(n == 0), stop=(n == 23))
                                n += 1
                        e.inc(i, sMM, 1)
                        e.mark(sMM, ("p3mc1", j, ho))
                    for k in range(16):
                        e.wait(("p3h1lo", j, k))
                    for oc in range(8):
                        qq = j * 8 + oc
                        if qq >= 2:
                            e.wait(("p3ev2", (qq - 2) // 8, (qq - 2) % 8))
                        pc2 = ps_c2[:, (oc % 2) * 512:(oc % 2) * 512 + TJ]
                        n = 0
                        for (ah, bh) in ((1, 1), (1, 0), (0, 1)):
                            for k in range(16):
                                w = (wc2_h if ah else wc2_l)[:, k * C + oc * 128:
                                                             k * C + (oc + 1) * 128]
                                xs = (h1h if bh else h1l)[:, k * TJ:(k + 1) * TJ]
                                i = e.matmul(pc2, w, xs, start=(n == 0), stop=(n == 47))
                                n += 1
                        e.inc(i, sMM, 1)
                        e.mark(sMM, ("p3mc2", j, oc))
                    e.wait(("p3sq3", j))
                    e.wait(("p3r2cp", j))
                    for k in range(8):
                        e.matmul(ps_ra[0:1, 0:TJ], onesc_sb[:, :],
                                 x3b[:, k * TJ:(k + 1) * TJ], start=(k == 0), stop=(k == 7))
                        i = e.matmul(ps_rb[0:1, 0:TJ], onesc_sb[:, :],
                                     sh3[:, k * TJ:(k + 1) * TJ], start=(k == 0),
                                     stop=(k == 7))
                    e.inc(i, sMM, 1)
                    e.mark(sMM, ("p3ln1mm", j))
                    e.wait(("p3lnr", j))
                    e.matmul(ps_ra[:, 0:TJ], onesr_sb[0:1, :], row(R_R, TJ),
                             start=True, stop=True)
                    i = e.matmul(ps_rb[:, 0:TJ], onesr_sb[0:1, :], row(R_MR, TJ),
                                 start=True, stop=True)
                    e.inc(i, sMM, 1)
                    e.mark(sMM, ("p3lnbc", j))

            def p3_gpsimd():
                e = EG
                for j in range(NJ2):
                    e.wait(("p3y1", j))
                    e.dma(d_y1.rearrange("(c p) t -> p c t", p=128)[:, :, TJ * j:TJ * (j + 1)],
                          y1b[:, :].rearrange("p (c t) -> p c t", t=TJ), sSP)
                    e.mark(sSP, ("p3y1sp", j))

            run_phase([("sync", p3_sync), ("scalar", p3_scalar), ("vector", p3_vector),
                       ("tensor", p3_tensor), ("gpsimd", p3_gpsimd)], "p3")

        sc_c12.close()

        def emit_cc_finalize(tag, pieces, ccidx, comp_fn, bcasts, ps_cc):

            def t_fn():
                e = ET
                for n_, (acol, scol) in enumerate(pieces):
                    i = e.transpose(ps_cc[0:1, n_ * 128:n_ * 128 + 128],
                                    amaxc[:, acol:acol + 1], eye32_sb[:, :])
                    e.inc(i, sMM, 1)
                e.mark(sMM, (tag, "tr"))
                e.wait((tag, "comp"))
                for n_, (scol, bcol) in enumerate(bcasts):
                    i = e.matmul(ps_cc[:, n_ * 128:n_ * 128 + 1], onesr_sb[0:1, :],
                                 scal[0:1, scol:scol + 1], start=True, stop=True)
                    e.inc(i, sMM, 1)
                e.mark(sMM, (tag, "bc"))

            def v_fn():
                e = EV
                e.wait((tag, "tr"))
                for n_, (acol, scol) in enumerate(pieces):
                    i = e.tensor_reduce(scal[0:1, scol:scol + 1],
                                        ps_cc[0:1, n_ * 128:n_ * 128 + 128],
                                        AX.X, ALU.max, apply_absolute_value=True)
                if len(pieces) == 1:
                    i = e.memset(scal[0:1, pieces[0][1] + 1:pieces[0][1] + 2], 0.0)
                e.inc(i, sVE, 1)
                e.mark(sVE, (tag, "locmax"))
                e.wait((tag, "ccld"))
                i = comp_fn(e)
                e.inc(i, sVE, 1)
                e.mark(sVE, (tag, "comp"))
                e.wait((tag, "bc"))
                for n_, (scol, bcol) in enumerate(bcasts):
                    i = e.tensor_copy(scbc[:, bcol:bcol + 1], ps_cc[:, n_ * 128:n_ * 128 + 1])
                e.inc(i, sVE, 1)
                e.mark(sVE, (tag, "done"))

            def g_fn():
                e = EG
                e.wait((tag, "locmax"))
                src_col = pieces[0][1]
                e.dma(cc_in[ccidx][0:1, 0:2], scal[0:1, src_col:src_col + 2], sSP)
                e.wait_sem_now(sSP)
                e.cc(cc_in[ccidx][:, :], cc_out[ccidx][:, :], sGP)
                e.wait_sem_now(sGP)
                e.dma(scal[0:1, src_col + 2:src_col + 4], cc_out[ccidx][0:1, 0:2], sSP)
                e.mark(sSP, (tag, "ccld"))

            return [("tensor", t_fn), ("vector", v_fn), ("gpsimd", g_fn)]

        with nc.psum_tensor("ps_cc0", [128, 256], F32) as ps_cc0:
            def comp0(e):
                e.tensor_scalar(sc1(4), sc1(2), INV_QMAX, 1e-12, ALU.mult, ALU.max)
                return e.reciprocal(sc1(5), sc1(4))

            run_phase(emit_cc_finalize("cc0", [(0, 0)], 0, comp0, [(5, 0)], ps_cc0), "cc0")

        sc_mx = contextlib.ExitStack()
        mx_h = sc_mx.enter_context(nc.sbuf_tensor("mx_h", [128, 8 * T], F16))
        mx_l = sc_mx.enter_context(nc.sbuf_tensor("mx_l", [128, 8 * T], F16))

        def mx3(t_):
            return t_.rearrange("p (c t) -> p c t", t=T)

        mats5 = [(wup_d, wup_t, d_U, 2), (wgate_d, wgate_t, d_G, 3)]

        with nc.sbuf_tensor("y1t5", [128, 2 * 512], F32) as y1t5, \
             nc.sbuf_tensor("mxi", [128, 2 * 512], I32) as mxi, \
             nc.sbuf_tensor("wff", [128, 2 * C], F32) as wff, \
             nc.sbuf_tensor("wts", [128, 2 * 8 * 128], F32) as wts, \
             nc.sbuf_tensor("sdq", [128, 2 * 32], F32) as sdq, \
             nc.sbuf_tensor("sTb", [32, 2 * 128], F32) as sTb, \
             nc.sbuf_tensor("srep", [128, 2 * 128], F32) as srep, \
             nc.sbuf_tensor("rsrep", [128, 2 * 128], F32) as rsrep, \
             nc.sbuf_tensor("mwt", [128, 128], I32) as mwt, \
             nc.sbuf_tensor("w16t", [128, 128], F32) as w16t, \
             nc.sbuf_tensor("wqh", [128, 2 * 8 * 128], F16) as wqh, \
             nc.sbuf_tensor("wql", [128, 2 * 8 * 128], F16) as wql, \
             nc.sbuf_tensor("ub", [128, 4 * 512], F32) as ub, \
             nc.psum_tensor("ps_ug", [128, 2 * 512], F32) as ps_ug, \
             nc.psum_tensor("ps_s5", [32, 1024], F32) as ps_s5, \
             nc.psum_tensor("ps_sr", [128, 1024], F32) as ps_sr:

            ND = 48

            def p5_sync():
                e = EY
                y13 = d_y1.rearrange("(c p) t -> p c t", p=128)
                for j in range(NJ):
                    for c in range(8):
                        fc = j * 8 + c
                        if fc >= 2:
                            e.wait(("p5mxl", (fc - 2) // 8, (fc - 2) % 8))
                        e.dma(y1t5[:, (fc % 2) * 512:(fc % 2) * 512 + 512],
                              y13[:, c, 512 * j:512 * (j + 1)], sLD)
                        e.mark(sLD, ("p5y1", j, c))
                for nb in range(ND):
                    mi, oc = divmod(nb, 24)
                    if nb >= 2:
                        e.wait(("p5s", nb - 2))
                    e.dma(wff[:, (nb % 2) * C:(nb % 2) * C + C],
                          mats5[mi][0][oc * 128:(oc + 1) * 128, :], sLD)
                    e.mark(sLD, ("p5w", nb))
                    if nb >= 2:
                        e.wait(("p5dq", nb - 2, 7))
                    e.dma(wts[:, (nb % 2) * 1024:(nb % 2) * 1024 + 1024]
                          .rearrange("p (ci o) -> p ci o", o=128),
                          mats5[mi][1].rearrange("(ci p) o -> p ci o", p=128)
                          [:, :, oc * 128:(oc + 1) * 128], sL2)
                    e.mark(sL2, ("p5wt", nb))

            def p5_scalar_all():
                e = EC
                for j in range(NJ):
                    for c in range(8):
                        fc = j * 8 + c
                        e.wait(("p5y1", j, c))
                        if fc >= 2:
                            e.wait(("p5mxl", (fc - 2) // 8, (fc - 2) % 8))
                        sl = slice((fc % 2) * 512, (fc % 2) * 512 + 512)
                        e.activation(mxi[:, sl], y1t5[:, sl], AF.Identity,
                                     scale=scbc[:, 0:1])
                        i = e.copy(mx3(mx_h[:, :])[:, c, 512 * j:512 * (j + 1)],
                                   mxi[:, sl])
                        e.inc(i, sEV, 1)
                        e.mark(sEV, ("p5mxh", j, c))
                for nb0 in range(2):
                    e.wait(("p5strd", nb0))
                    i = e.copy(sTb[0:32, (nb0 % 2) * 128:(nb0 % 2) * 128 + 128],
                               ps_s5[0:32, (nb0 % 2) * 512:(nb0 % 2) * 512 + 128])
                    e.inc(i, sEV, 1)
                    e.mark(sEV, ("p5stcp", nb0))
                for nb in range(ND):
                    for j in range(NJ):
                        fb = nb * NJ + j
                        e.wait(("p5mm", nb, j))
                        if fb >= 4:
                            e.wait(("p5sp", fb - 4))
                            e.wait(("p5am", fb - 4))
                        i = e.copy(ub[:, (fb % 4) * 512:(fb % 4) * 512 + 512],
                                   ps_ug[:, (fb % 2) * 512:(fb % 2) * 512 + 512])
                        e.inc(i, sEV, 1)
                        e.mark(sEV, ("p5ev", fb))
                    if nb + 2 < ND:
                        e.wait(("p5strd", nb + 2))
                        e.wait(("p5sr", nb, 7))
                        i = e.copy(sTb[0:32, ((nb + 2) % 2) * 128:((nb + 2) % 2) * 128 + 128],
                                   ps_s5[0:32, ((nb + 2) % 2) * 512:((nb + 2) % 2) * 512 + 128])
                        e.inc(i, sEV, 1)
                        e.mark(sEV, ("p5stcp", nb + 2))

            def p5_dequant_head(e, nb):
                e.wait(("p5w", nb))
                if nb >= 2:
                    e.wait(("p5strd", nb - 2))
                stg = wff[:, (nb % 2) * C:(nb % 2) * C + C]
                w3 = stg.rearrange("p (b i) -> p b i", i=32)
                ss_ = sdq[:, (nb % 2) * 32:(nb % 2) * 32 + 32]
                e.tensor_reduce(ss_, w3, AX.X, ALU.max, apply_absolute_value=True)
                i = e.tensor_scalar(ss_, ss_, 1.0 / 7.0, 1e-12, ALU.mult, ALU.max)
                e.inc(i, sVE, 1)
                e.mark(sVE, ("p5s", nb))

            def p5_dequant_tail(e, nb):
                e.wait(("p5wt", nb))
                for ci in range(8):
                    e.wait(("p5sr", nb, ci))
                    sl2 = slice((ci % 2) * 128, (ci % 2) * 128 + 128)
                    e.tensor_copy(srep[:, sl2], ps_sr[:, (ci % 2) * 512:(ci % 2) * 512 + 128])
                    i = e.reciprocal(rsrep[:, sl2], ps_sr[:, (ci % 2) * 512:(ci % 2) * 512 + 128])
                    e.inc(i, sVE, 1)
                    e.mark(sVE, ("p5srcp", nb, ci))
                    wt_sl = wts[:, (nb % 2) * 1024 + ci * 128:(nb % 2) * 1024 + (ci + 1) * 128]
                    e.tensor_tensor(mwt[:, :], wt_sl, rsrep[:, sl2], ALU.mult)
                    sl = slice((nb % 2) * 1024 + ci * 128, (nb % 2) * 1024 + (ci + 1) * 128)
                    e.tensor_tensor(wqh[:, sl], mwt[:, :], srep[:, sl2], ALU.mult)
                    e.tensor_tensor(w16t[:, :], mwt[:, :], srep[:, sl2], ALU.mult)
                    i = e.tensor_tensor(wql[:, sl], w16t[:, :], wqh[:, sl], ALU.subtract)
                    e.inc(i, sVE, 1)
                    e.mark(sVE, ("p5dq", nb, ci))

            def p5_vector():
                e = EV
                e.memset(amaxc[:, 2:4], 0.0)
                i = e.memset(amaxc[:, 5:7], 0.0)
                e.inc(i, sVE, 1)
                e.mark(sVE, ("p5amz",))
                for j in range(NJ):
                    for c in range(8):
                        fc = j * 8 + c
                        e.wait(("p5mxh", j, c))
                        sl = slice((fc % 2) * 512, (fc % 2) * 512 + 512)
                        i = e.tensor_tensor(mx3(mx_l[:, :])[:, c, 512 * j:512 * (j + 1)],
                                            mxi[:, sl],
                                            mx3(mx_h[:, :])[:, c, 512 * j:512 * (j + 1)],
                                            ALU.subtract)
                        e.inc(i, sVE, 1)
                        e.mark(sVE, ("p5mxl", j, c))
                p5_dequant_head(e, 0)
                p5_dequant_tail(e, 0)
                p5_dequant_head(e, 1)
                p5_dequant_tail(e, 1)
                for nb in range(ND):
                    mi, oc = divmod(nb, 24)
                    for j in range(NJ):
                        fb = nb * NJ + j
                        e.wait(("p5ev", fb))
                        e.tensor_reduce(amaxc[:, 4:5],
                                        ub[:, (fb % 4) * 512:(fb % 4) * 512 + 512],
                                        AX.X, ALU.max, apply_absolute_value=True)
                        i = e.tensor_tensor(amaxc[:, 2 + mi:3 + mi], amaxc[:, 2 + mi:3 + mi],
                                            amaxc[:, 4:5], ALU.max)
                        e.inc(i, sVE, 1)
                        e.mark(sVE, ("p5am", fb))
                    if nb + 2 < ND:
                        p5_dequant_head(e, nb + 2)
                        p5_dequant_tail(e, nb + 2)

            def p5_gpsimd():
                e = EG
                for nb in range(ND):
                    mi, oc = divmod(nb, 24)
                    ddst = mats5[mi][2]
                    for j in range(NJ):
                        fb = nb * NJ + j
                        e.wait(("p5ev", fb))
                        e.dma(ddst[oc * 128:(oc + 1) * 128, 512 * j:512 * (j + 1)],
                              ub[:, (fb % 4) * 512:(fb % 4) * 512 + 512], sSP)
                        e.mark(sSP, ("p5sp", fb))

            def p5_tensor_dq(e, nb):
                e.wait(("p5s", nb))
                i = e.transpose(ps_s5[0:32, (nb % 2) * 512:(nb % 2) * 512 + 128],
                                sdq[:, (nb % 2) * 32:(nb % 2) * 32 + 32], eye32_sb[:, :])
                e.inc(i, sMM, 1)
                e.mark(sMM, ("p5strd", nb))
                e.wait(("p5stcp", nb))
                for ci in range(8):
                    cc_ = nb * 8 + ci
                    if cc_ >= 2:
                        e.wait(("p5srcp", (cc_ - 2) // 8, (cc_ - 2) % 8))
                    i = e.matmul(ps_sr[:, (ci % 2) * 512:(ci % 2) * 512 + 128],
                                 eexpA_sb[:, ci * 128:(ci + 1) * 128],
                                 sTb[:, (nb % 2) * 128:(nb % 2) * 128 + 128],
                                 start=True, stop=True)
                    e.inc(i, sMM, 1)
                    e.mark(sMM, ("p5sr", nb, ci))

            def p5_tensor():
                e = ET
                p5_tensor_dq(e, 0)
                p5_tensor_dq(e, 1)
                for nb in range(ND):
                    for ci in range(8):
                        e.wait(("p5dq", nb, ci))
                    for j in range(NJ):
                        fb = nb * NJ + j
                        for c in range(8):
                            e.wait(("p5mxl", j, c))
                        if fb >= 2:
                            e.wait(("p5ev", fb - 2))
                        pu = ps_ug[:, (fb % 2) * 512:(fb % 2) * 512 + 512]
                        n = 0
                        for (ah, bh) in ((1, 1), (1, 0), (0, 1)):
                            for k in range(8):
                                sl = slice((nb % 2) * 1024 + k * 128,
                                           (nb % 2) * 1024 + (k + 1) * 128)
                                w = (wqh if ah else wql)[:, sl]
                                xs = mx3((mx_h if bh else mx_l)[:, :])[:, k,
                                                                       512 * j:512 * (j + 1)]
                                i = e.matmul(pu, w, xs, start=(n == 0), stop=(n == 23))
                                n += 1
                        e.inc(i, sMM, 1)
                        e.mark(sMM, ("p5mm", nb, j))
                    if nb + 2 < ND:
                        p5_tensor_dq(e, nb + 2)

            run_phase([("sync", p5_sync), ("scalar", p5_scalar_all),
                       ("vector", p5_vector), ("gpsimd", p5_gpsimd),
                       ("tensor", p5_tensor)], "p5")

        with nc.psum_tensor("ps_cc1", [128, 384], F32) as ps_cc1:
            def comp1(e):
                e.tensor_tensor(sc1(11), sc1(8), sc1(4), ALU.mult)
                e.tensor_scalar(sc1(10), sc1(11), INV_QMAX, 1e-12, ALU.mult, ALU.max)
                e.reciprocal(sc1(11), sc1(10))
                e.tensor_tensor(sc1(12), sc1(4), sc1(11), ALU.mult)
                e.tensor_tensor(sc1(11), sc1(9), sc1(4), ALU.mult)
                e.tensor_scalar(sc1(13), sc1(11), INV_QMAX, 1e-12, ALU.mult, ALU.max)
                e.reciprocal(sc1(11), sc1(13))
                e.tensor_tensor(sc1(14), sc1(4), sc1(11), ALU.mult)
                return e.tensor_scalar(sc1(15), sc1(13), 1.0 / 65536.0, None, ALU.mult)

            run_phase(emit_cc_finalize("cc1", [(2, 6), (3, 7)], 1, comp1,
                                       [(12, 1), (14, 2), (13, 3)], ps_cc1), "cc1")
        sc_mx.close()

        with nc.sbuf_tensor("u6", [128, 2 * 2048], F32) as u6, \
             nc.sbuf_tensor("g6", [128, 2 * 2048], F32) as g6, \
             nc.sbuf_tensor("mu6", [128, 2 * 2048], I32) as mu6, \
             nc.sbuf_tensor("mg6", [128, 2 * 2048], I32) as mg6, \
             nc.sbuf_tensor("sg6", [128, 2 * 2048], F32) as sg6, \
             nc.sbuf_tensor("ms6", [128, 2 * 2048], I32) as ms6, \
             nc.sbuf_tensor("p6", [128, 2 * 2048], F32) as p6:

            NJ4 = T // 2048 if T >= 2048 else 1
            W6 = min(T, 2048)
            NB6 = 24 * NJ4

            def p6a_sync():
                e = EY
                for fb in range(NB6):
                    oc, j = divmod(fb, NJ4)
                    if fb >= 2:
                        e.wait(("p6musp", fb - 2))
                        e.wait(("p6psp", fb - 2))
                    e.dma(u6[:, (fb % 2) * W6:(fb % 2) * W6 + W6],
                          d_U[oc * 128:(oc + 1) * 128, W6 * j:W6 * (j + 1)], sLD)
                    e.dma(g6[:, (fb % 2) * W6:(fb % 2) * W6 + W6],
                          d_G[oc * 128:(oc + 1) * 128, W6 * j:W6 * (j + 1)], sL2)
                    e.mark(sLD, ("p6ldu", fb))
                    e.mark(sL2, ("p6ldg", fb))

            def p6a_scalar():
                e = EC
                for fb in range(NB6):
                    e.wait(("p6ldu", fb))
                    e.wait(("p6ldg", fb))
                    if fb >= 2:
                        e.wait(("p6p", fb - 2))
                        e.wait(("p6musp", fb - 2))
                    sl = slice((fb % 2) * W6, (fb % 2) * W6 + W6)
                    e.activation(mu6[:, sl], u6[:, sl], AF.Identity, scale=scbc[:, 1:2])
                    e.activation(mg6[:, sl], g6[:, sl], AF.Identity, scale=scbc[:, 2:3])
                    i = e.activation(sg6[:, sl], mg6[:, sl], AF.Sigmoid, scale=scbc[:, 3:4])
                    e.inc(i, sEV, 1)
                    e.mark(sEV, ("p6s", fb))

            def p6a_vector():
                e = EV
                for fb in range(NB6):
                    e.wait(("p6s", fb))
                    if fb >= 2:
                        e.wait(("p6psp", fb - 2))
                    sl = slice((fb % 2) * W6, (fb % 2) * W6 + W6)
                    e.tensor_scalar(ms6[:, sl], sg6[:, sl], 65536.0, 65535.0,
                                    ALU.mult, ALU.min)
                    e.tensor_tensor(p6[:, sl], mg6[:, sl], ms6[:, sl], ALU.mult)
                    e.tensor_reduce(amaxc[:, 4:5], p6[:, sl], AX.X, ALU.max,
                                    apply_absolute_value=True)
                    i = e.tensor_tensor(amaxc[:, 5:6], amaxc[:, 5:6], amaxc[:, 4:5],
                                        ALU.max)
                    e.inc(i, sVE, 1)
                    e.mark(sVE, ("p6p", fb))

            def p6a_gpsimd():
                e = EG
                for fb in range(NB6):
                    oc, j = divmod(fb, NJ4)
                    sl = slice((fb % 2) * W6, (fb % 2) * W6 + W6)
                    e.wait(("p6s", fb))
                    e.dma(d_mu[oc * 128:(oc + 1) * 128, W6 * j:W6 * (j + 1)],
                          mu6[:, sl], sSP)
                    e.mark(sSP, ("p6musp", fb))
                    e.wait(("p6p", fb))
                    e.dma(d_P[oc * 128:(oc + 1) * 128, W6 * j:W6 * (j + 1)],
                          p6[:, sl], sS2)
                    e.mark(sS2, ("p6psp", fb))

            run_phase([("sync", p6a_sync), ("scalar", p6a_scalar),
                       ("vector", p6a_vector), ("gpsimd", p6a_gpsimd)], "p6a")

        with nc.psum_tensor("ps_cc2", [128, 256], F32) as ps_cc2:
            def comp2(e):
                e.tensor_tensor(sc1(22), sc1(19), sc1(15), ALU.mult)
                e.tensor_scalar(sc1(21), sc1(22), INV_QMAX, 1e-12, ALU.mult, ALU.max)
                e.reciprocal(sc1(22), sc1(21))
                return e.tensor_tensor(sc1(23), sc1(15), sc1(22), ALU.mult)

            run_phase(emit_cc_finalize("cc2", [(5, 17)], 2, comp2, [(23, 4)], ps_cc2), "cc2")

        with nc.sbuf_tensor("p6b", [128, 2 * 2048], F32) as p6b, \
             nc.sbuf_tensor("mu6b", [128, 2 * 2048], I32) as mu6b, \
             nc.sbuf_tensor("mg2b", [128, 2 * 2048], I32) as mg2b, \
             nc.sbuf_tensor("q6", [128, 2 * 2048], F32) as q6:

            NJ4 = T // 2048 if T >= 2048 else 1
            W6 = min(T, 2048)
            NB6 = 24 * NJ4

            def p6b_sync():
                e = EY
                for fb in range(NB6):
                    oc, j = divmod(fb, NJ4)
                    if fb >= 2:
                        e.wait(("p6qsp", fb - 2))
                    e.dma(p6b[:, (fb % 2) * W6:(fb % 2) * W6 + W6],
                          d_P[oc * 128:(oc + 1) * 128, W6 * j:W6 * (j + 1)], sLD)
                    e.dma(mu6b[:, (fb % 2) * W6:(fb % 2) * W6 + W6],
                          d_mu[oc * 128:(oc + 1) * 128, W6 * j:W6 * (j + 1)], sL2)
                    e.mark(sLD, ("p6bldp", fb))
                    e.mark(sL2, ("p6bldm", fb))

            def p6b_scalar():
                e = EC
                for fb in range(NB6):
                    e.wait(("p6bldp", fb))
                    if fb >= 2:
                        e.wait(("p6q", fb - 2))
                    sl = slice((fb % 2) * W6, (fb % 2) * W6 + W6)
                    i = e.activation(mg2b[:, sl], p6b[:, sl], AF.Identity,
                                     scale=scbc[:, 4:5])
                    e.inc(i, sEV, 1)
                    e.mark(sEV, ("p6mg2", fb))

            def p6b_vector():
                e = EV
                for fb in range(NB6):
                    e.wait(("p6mg2", fb))
                    e.wait(("p6bldm", fb))
                    if fb >= 2:
                        e.wait(("p6qsp", fb - 2))
                    sl = slice((fb % 2) * W6, (fb % 2) * W6 + W6)
                    e.tensor_tensor(q6[:, sl], mg2b[:, sl], mu6b[:, sl], ALU.mult)
                    e.tensor_reduce(amaxc[:, 4:5], q6[:, sl], AX.X, ALU.max,
                                    apply_absolute_value=True)
                    i = e.tensor_tensor(amaxc[:, 6:7], amaxc[:, 6:7], amaxc[:, 4:5],
                                        ALU.max)
                    e.inc(i, sVE, 1)
                    e.mark(sVE, ("p6q", fb))

            def p6b_gpsimd():
                e = EG
                for fb in range(NB6):
                    oc, j = divmod(fb, NJ4)
                    e.wait(("p6q", fb))
                    e.dma(d_Q[oc * 128:(oc + 1) * 128, W6 * j:W6 * (j + 1)],
                          q6[:, (fb % 2) * W6:(fb % 2) * W6 + W6], sSP)
                    e.mark(sSP, ("p6qsp", fb))

            run_phase([("sync", p6b_sync), ("scalar", p6b_scalar),
                       ("vector", p6b_vector), ("gpsimd", p6b_gpsimd)], "p6b")

        with nc.psum_tensor("ps_cc3", [128, 256], F32) as ps_cc3:
            def comp3(e):
                e.tensor_tensor(sc1(28), sc1(21), sc1(10), ALU.mult)
                e.tensor_tensor(sc1(29), sc1(26), sc1(28), ALU.mult)
                e.tensor_scalar(sc1(30), sc1(29), INV_QMAX, 1e-12, ALU.mult, ALU.max)
                e.reciprocal(sc1(31), sc1(30))
                return e.tensor_tensor(sc1(29), sc1(28), sc1(31), ALU.mult)

            run_phase(emit_cc_finalize("cc3", [(6, 24)], 0, comp3,
                                       [(29, 5), (30, 6)], ps_cc3), "cc3")

        sc_wd = contextlib.ExitStack()
        wdq_h = sc_wd.enter_context(nc.sbuf_tensor("wdq_h", [128, 8 * FF], F16))
        wdq_l = sc_wd.enter_context(nc.sbuf_tensor("wdq_l", [128, 8 * FF], F16))

        with nc.sbuf_tensor("eexpB_sb", [96, 24 * 128], F32) as eexpB_sb, \
             nc.sbuf_tensor("wff7", [128, 2 * FF], F32) as wff7, \
             nc.sbuf_tensor("wts7", [128, 2 * 24 * 128], F32) as wts7, \
             nc.sbuf_tensor("sdq7", [128, 2 * 96], F32) as sdq7, \
             nc.sbuf_tensor("sTb7", [96, 2 * 128], F32) as sTb7, \
             nc.sbuf_tensor("srep7", [128, 2 * 128], F32) as srep7, \
             nc.sbuf_tensor("rsrep7", [128, 2 * 128], F32) as rsrep7, \
             nc.sbuf_tensor("mwt7", [128, 128], I32) as mwt7, \
             nc.sbuf_tensor("w16t7", [128, 128], F32) as w16t7, \
             nc.psum_tensor("ps_s57", [96, 1024], F32) as ps_s57, \
             nc.psum_tensor("ps_sr7", [128, 1024], F32) as ps_sr7:

            def p7a_sync():
                e = EY
                e.dma(eexpB_sb[:, :], eexpB_c[:, :], sLD)
                for nb in range(8):
                    if nb >= 2:
                        e.wait(("p7s", nb - 2))
                    e.dma(wff7[:, (nb % 2) * FF:(nb % 2) * FF + FF],
                          wdown_d[nb * 128:(nb + 1) * 128, :], sLD)
                    e.mark(sLD, ("p7w", nb))
                    if nb >= 2:
                        e.wait(("p7dq", nb - 2, 23))
                    e.dma(wts7[:, (nb % 2) * 3072:(nb % 2) * 3072 + 3072]
                          .rearrange("p (ci o) -> p ci o", o=128),
                          wdown_t.rearrange("(ci p) o -> p ci o", p=128)
                          [:, :, nb * 128:(nb + 1) * 128], sL2)
                    e.mark(sL2, ("p7wt_ld", nb))

            def p7a_vector():
                e = EV
                for nb in range(8):
                    e.wait(("p7w", nb))
                    if nb >= 2:
                        e.wait(("p7strd", nb - 2))
                    stg = wff7[:, (nb % 2) * FF:(nb % 2) * FF + FF]
                    w3 = stg.rearrange("p (b i) -> p b i", i=32)
                    ss_ = sdq7[:, (nb % 2) * 96:(nb % 2) * 96 + 96]
                    e.tensor_reduce(ss_, w3, AX.X, ALU.max, apply_absolute_value=True)
                    i = e.tensor_scalar(ss_, ss_, 1.0 / 7.0, 1e-12, ALU.mult, ALU.max)
                    e.inc(i, sVE, 1)
                    e.mark(sVE, ("p7s", nb))
                    e.wait(("p7wt_ld", nb))
                    for ci in range(24):
                        e.wait(("p7sr", nb, ci))
                        sl2 = slice((ci % 2) * 128, (ci % 2) * 128 + 128)
                        e.tensor_copy(srep7[:, sl2],
                                      ps_sr7[:, (ci % 2) * 512:(ci % 2) * 512 + 128])
                        i = e.reciprocal(rsrep7[:, sl2],
                                         ps_sr7[:, (ci % 2) * 512:(ci % 2) * 512 + 128])
                        e.inc(i, sVE, 1)
                        e.mark(sVE, ("p7srcp", nb, ci))
                        wt_sl = wts7[:, (nb % 2) * 3072 + ci * 128:
                                     (nb % 2) * 3072 + (ci + 1) * 128]
                        e.tensor_tensor(mwt7[:, :], wt_sl, rsrep7[:, sl2], ALU.mult)
                        sl = slice(nb * FF + ci * 128, nb * FF + (ci + 1) * 128)
                        e.tensor_tensor(wdq_h[:, sl], mwt7[:, :], srep7[:, sl2], ALU.mult)
                        e.tensor_tensor(w16t7[:, :], mwt7[:, :], srep7[:, sl2], ALU.mult)
                        i = e.tensor_tensor(wdq_l[:, sl], w16t7[:, :], wdq_h[:, sl],
                                            ALU.subtract)
                        e.inc(i, sVE, 1)
                        e.mark(sVE, ("p7dq", nb, ci))

            def p7a_scalar():
                e = EC
                for nb in range(8):
                    e.wait(("p7strd", nb))
                    if nb >= 2:
                        e.wait(("p7sr", nb - 2, 23))
                    i = e.copy(sTb7[0:96, (nb % 2) * 128:(nb % 2) * 128 + 128],
                               ps_s57[0:96, (nb % 2) * 512:(nb % 2) * 512 + 128])
                    e.inc(i, sEV, 1)
                    e.mark(sEV, ("p7stcp", nb))

            def p7a_tensor():
                e = ET
                for nb in range(8):
                    e.wait(("p7s", nb))
                    i = e.transpose(ps_s57[0:96, (nb % 2) * 512:(nb % 2) * 512 + 128],
                                    sdq7[:, (nb % 2) * 96:(nb % 2) * 96 + 96],
                                    eye32_sb[:, :])
                    e.inc(i, sMM, 1)
                    e.mark(sMM, ("p7strd", nb))
                    e.wait(("p7stcp", nb))
                    for ci in range(24):
                        cc_ = nb * 24 + ci
                        if cc_ >= 2:
                            e.wait(("p7srcp", (cc_ - 2) // 24, (cc_ - 2) % 24))
                        i = e.matmul(ps_sr7[:, (ci % 2) * 512:(ci % 2) * 512 + 128],
                                     eexpB_sb[:, ci * 128:(ci + 1) * 128],
                                     sTb7[:, (nb % 2) * 128:(nb % 2) * 128 + 128],
                                     start=True, stop=True)
                        e.inc(i, sMM, 1)
                        e.mark(sMM, ("p7sr", nb, ci))

            run_phase([("sync", p7a_sync), ("vector", p7a_vector), ("scalar", p7a_scalar),
                       ("tensor", p7a_tensor)], "p7a")

        with nc.sbuf_tensor("q7", [128, 2 * TJ], F32) as q7, \
             nc.sbuf_tensor("moi", [128, 2 * TJ], I32) as moi, \
             nc.sbuf_tensor("mo_h", [128, 24 * TJ], F16) as mo_h, \
             nc.sbuf_tensor("mo_l", [128, 24 * TJ], F16) as mo_l, \
             nc.sbuf_tensor("y1t7", [128, 8 * TJ], F32) as y1t7, \
             nc.sbuf_tensor("pt7", [128, 2 * TJ], F32) as pt7, \
             nc.sbuf_tensor("x4b", [128, 8 * TJ], F32) as x4b, \
             nc.sbuf_tensor("sh7", [128, 8 * TJ], F32) as sh7, \
             nc.sbuf_tensor("rbc7", [128, 2 * TJ], F32) as rbc7, \
             nc.sbuf_tensor("yb", [128, 8 * TJ], F32) as yb, \
             nc.psum_tensor("ps_dn", [128, 1024], F32) as ps_dn, \
             nc.psum_tensor("ps_r7a", [128, 512], F32) as ps_r7a, \
             nc.psum_tensor("ps_r7b", [128, 512], F32) as ps_r7b:

            def p7b_sync():
                e = EY
                q3 = d_Q.rearrange("(c p) t -> p c t", p=128)
                y13 = d_y1.rearrange("(c p) t -> p c t", p=128)
                for j in range(NJ2):
                    if j >= 1:
                        e.wait(("p7x4all", j - 1))
                    e.dma(y1t7[:, :].rearrange("p (c t) -> p c t", t=TJ),
                          y13[:, :, TJ * j:TJ * (j + 1)], sLD)
                    e.mark(sLD, ("p7y1", j))
                    for ci in range(24):
                        fb = j * 24 + ci
                        if fb >= 2:
                            pf = fb - 2
                            e.wait(("p7mol", pf // 24, pf % 24))
                        e.dma(q7[:, (ci % 2) * TJ:(ci % 2) * TJ + TJ],
                              q3[:, ci, TJ * j:TJ * (j + 1)], sLD)
                        e.mark(sLD, ("p7q", j, ci))

            def p7b_scalar():
                e = EC
                for j in range(NJ2):
                    for ci in range(24):
                        e.wait(("p7q", j, ci))
                        if j * 24 + ci >= 2:
                            pf = j * 24 + ci - 2
                            e.wait(("p7mol", pf // 24, pf % 24))
                        sl = slice((ci % 2) * TJ, (ci % 2) * TJ + TJ)
                        e.activation(moi[:, sl], q7[:, sl], AF.Identity,
                                     scale=scbc[:, 5:6])
                        i = e.copy(mo_h[:, ci * TJ:(ci + 1) * TJ], moi[:, sl])
                        e.inc(i, sEV, 1)
                        e.mark(sEV, ("p7moh", j, ci))
                    for oc in range(8):
                        e.wait(("p7dn", j, oc))
                        qq = j * 8 + oc
                        if qq >= 2:
                            e.wait(("p7x4", (qq - 2) // 8, (qq - 2) % 8))
                        i = e.activation(pt7[:, (oc % 2) * TJ:(oc % 2) * TJ + TJ],
                                         ps_dn[:, (oc % 2) * 512:(oc % 2) * 512 + TJ],
                                         AF.Identity, scale=scbc[:, 6:7])
                        e.inc(i, sEV, 1)
                        e.mark(sEV, ("p7ev", j, oc))
                    e.wait(("p7x4all", j))
                    i = e.activation(sh7[:, :], x4b[:, :], AF.Square)
                    e.inc(i, sEV, 1)
                    e.mark(sEV, ("p7sq", j))
                    e.wait(("p7lnmm", j))
                    e.activation(row(R_M, TJ), ps_r7a[0:1, 0:TJ], AF.Identity,
                                 scale=1.0 / 1024.0)
                    i = e.activation(row(R_EX, TJ), ps_r7b[0:1, 0:TJ], AF.Identity,
                                     scale=1.0 / 1024.0)
                    e.inc(i, sEV, 1)
                    e.mark(sEV, ("p7rows", j))
                    e.wait(("p7var", j))
                    i = e.activation(row(R_SD, TJ), row(R_VAR, TJ), AF.Sqrt, bias=cbias_sb[0:1, 0:1])
                    e.inc(i, sEV, 1)
                    e.mark(sEV, ("p7sd", j))

            def p7b_vector():
                e = EV
                for j in range(NJ2):
                    for ci in range(24):
                        e.wait(("p7moh", j, ci))
                        sl = slice((ci % 2) * TJ, (ci % 2) * TJ + TJ)
                        i = e.tensor_tensor(mo_l[:, ci * TJ:(ci + 1) * TJ],
                                            moi[:, sl], mo_h[:, ci * TJ:(ci + 1) * TJ],
                                            ALU.subtract)
                        e.inc(i, sVE, 1)
                        e.mark(sVE, ("p7mol", j, ci))
                    for oc in range(8):
                        e.wait(("p7ev", j, oc))
                        e.wait(("p7y1", j))
                        i = e.tensor_tensor(x4b[:, oc * TJ:(oc + 1) * TJ],
                                            pt7[:, (oc % 2) * TJ:(oc % 2) * TJ + TJ],
                                            y1t7[:, oc * TJ:(oc + 1) * TJ], ALU.add)
                        e.inc(i, sVE, 1)
                        e.mark(sVE, ("p7x4", j, oc))
                    e.mark(sVE, ("p7x4all", j))
                    e.wait(("p7rows", j))
                    e.tensor_tensor(row(R_VAR, TJ), row(R_M, TJ), row(R_M, TJ), ALU.mult)
                    i = e.tensor_tensor(row(R_VAR, TJ), row(R_EX, TJ), row(R_VAR, TJ),
                                        ALU.subtract)
                    e.inc(i, sVE, 1)
                    e.mark(sVE, ("p7var", j))
                    e.wait(("p7sd", j))
                    e.reciprocal(row(R_R, TJ), row(R_SD, TJ))
                    i = e.tensor_tensor(row(R_MR, TJ), row(R_M, TJ), row(R_R, TJ),
                                        ALU.mult)
                    e.inc(i, sVE, 1)
                    e.mark(sVE, ("p7lnr", j))
                    e.wait(("p7lnbc", j))
                    e.tensor_copy(rbc7[:, 0:TJ], ps_r7a[:, 0:TJ])
                    i = e.tensor_copy(rbc7[:, TJ:2 * TJ], ps_r7b[:, 0:TJ])
                    e.inc(i, sVE, 1)
                    e.mark(sVE, ("p7lncp", j))
                    if j >= 1:
                        e.wait(("p7ysp", j - 1))
                    rb = rbc7[:, 0:TJ].unsqueeze(1).broadcast_to([128, 8, TJ])
                    mrb = rbc7[:, TJ:2 * TJ].unsqueeze(1).broadcast_to([128, 8, TJ])
                    x5 = x4b[:, :].rearrange("p (c t) -> p c t", t=TJ)
                    y5 = yb[:, :].rearrange("p (c t) -> p c t", t=TJ)
                    e.tensor_tensor(y5, x5, rb, ALU.mult)
                    i = e.tensor_tensor(y5, y5, mrb, ALU.subtract)
                    e.inc(i, sVE, 1)
                    e.mark(sVE, ("p7y", j))

            def p7b_tensor():
                e = ET
                for j in range(NJ2):
                    for ci in range(24):
                        e.wait(("p7mol", j, ci))
                    for oc in range(8):
                        qq = j * 8 + oc
                        if qq >= 2:
                            e.wait(("p7ev", (qq - 2) // 8, (qq - 2) % 8))
                        pd = ps_dn[:, (oc % 2) * 512:(oc % 2) * 512 + TJ]
                        n = 0
                        for (ah, bh) in ((1, 1), (1, 0), (0, 1)):
                            for k in range(24):
                                w = (wdq_h if ah else wdq_l)[:, oc * FF + k * 128:
                                                             oc * FF + (k + 1) * 128]
                                xs = (mo_h if bh else mo_l)[:, k * TJ:(k + 1) * TJ]
                                i = e.matmul(pd, w, xs, start=(n == 0), stop=(n == 71))
                                n += 1
                        e.inc(i, sMM, 1)
                        e.mark(sMM, ("p7dn", j, oc))
                    e.wait(("p7sq", j))
                    if j >= 1:
                        e.wait(("p7lncp", j - 1))
                    e.wait(("p7x4all", j))
                    for k in range(8):
                        e.matmul(ps_r7a[0:1, 0:TJ], onesc_sb[:, :],
                                 x4b[:, k * TJ:(k + 1) * TJ], start=(k == 0), stop=(k == 7))
                        i = e.matmul(ps_r7b[0:1, 0:TJ], onesc_sb[:, :],
                                     sh7[:, k * TJ:(k + 1) * TJ], start=(k == 0),
                                     stop=(k == 7))
                    e.inc(i, sMM, 1)
                    e.mark(sMM, ("p7lnmm", j))
                    e.wait(("p7lnr", j))
                    e.matmul(ps_r7a[:, 0:TJ], onesr_sb[0:1, :], row(R_R, TJ),
                             start=True, stop=True)
                    i = e.matmul(ps_r7b[:, 0:TJ], onesr_sb[0:1, :], row(R_MR, TJ),
                                 start=True, stop=True)
                    e.inc(i, sMM, 1)
                    e.mark(sMM, ("p7lnbc", j))

            def p7b_gpsimd():
                e = EG
                for j in range(NJ2):
                    e.wait(("p7y", j))
                    e.dma(y_out.rearrange("(c p) t -> p c t", p=128)
                          [:, :, TJ * j:TJ * (j + 1)],
                          yb[:, :].rearrange("p (c t) -> p c t", t=TJ), sSP)
                    e.mark(sSP, ("p7ysp", j))
                e.wait_sem_now(sSP)

            run_phase([("sync", p7b_sync), ("scalar", p7b_scalar), ("vector", p7b_vector),
                       ("tensor", p7b_tensor), ("gpsimd", p7b_gpsimd)], "p7b")

        sc_wd.close()

    return nc


_PROG_CACHE = {}


def _get_prog(T):
    if T not in _PROG_CACHE:
        _PROG_CACHE[T] = build_program(T)
    return _PROG_CACHE[T]


def make_inmaps(inputs, T):
    x = np.ascontiguousarray(np.asarray(inputs["L0_attn_x"]))
    B = x.shape[0]
    wqkv = np.ascontiguousarray(np.asarray(inputs["L0_attn_qkv_w"]).T)
    wproj = np.ascontiguousarray(np.asarray(inputs["L0_attn_proj_w"]).T)
    wc1 = np.ascontiguousarray(np.asarray(inputs["L0_attn_conv1_w"]).T)
    wc2 = np.ascontiguousarray(np.asarray(inputs["L0_attn_conv2_w"]).T)
    wup = np.ascontiguousarray(np.asarray(inputs["L0_ffn_w_gate" if False else "L0_ffn_w_up"]))
    wgate = np.ascontiguousarray(np.asarray(inputs["L0_ffn_w_gate"]))
    wdown = np.ascontiguousarray(np.asarray(inputs["L0_ffn_w_down"]))
    vecs = np.zeros((6, C), np.float32)
    vecs[0] = np.asarray(inputs["L0_attn_scale1"])
    vecs[1] = np.asarray(inputs["L0_attn_scale2"])
    vecs[2] = np.asarray(inputs["L0_attn_gamma"])
    vecs[3] = np.asarray(inputs["L0_attn_proj_b"])
    vecs[4] = np.asarray(inputs["L0_attn_conv2_b"])
    vecs = np.ascontiguousarray(vecs.reshape(6, 8, 128).transpose(2, 0, 1).reshape(128, 48))
    c1b = np.asarray(inputs["L0_attn_conv1_b"], dtype=np.float32)
    c1b = np.ascontiguousarray(c1b.reshape(16, 128).T)
    shared = dict(wqkv=wqkv, wproj=wproj, wc1=wc1, wc2=wc2, wup=wup,
                  wgate=wgate, wdown=wdown, vecs=vecs, c1b=c1b,
                  wup_t=np.ascontiguousarray(wup.T),
                  wgate_t=np.ascontiguousarray(wgate.T),
                  wdown_t=np.ascontiguousarray(wdown.T))
    in_maps = []
    for b in range(B):
        m = dict(shared)
        m["x"] = np.ascontiguousarray(x[b].T)
        in_maps.append(m)
    return in_maps


def kernel(**inputs):
    T = int(np.asarray(inputs["L0_attn_x"]).shape[1])
    nc = _get_prog(T)
    in_maps = make_inmaps(inputs, T)
    res = run_bass_kernel_spmd(nc, in_maps, core_ids=list(range(8)))
    out = np.stack([np.asarray(r["y"]).T for r in res.results], axis=0)
    return out.astype(np.float32)


# version 04
# speedup vs baseline: 2.0471x; 1.9189x over previous
import contextlib
import os
import numpy as np
import concourse.bass as bass
import concourse.mybir as mybir
from concourse.bass_utils import run_bass_kernel_spmd
from concourse.bass_types import AP

F32 = mybir.dt.float32
F16 = mybir.dt.float16
I32 = mybir.dt.int32
AF = mybir.ActivationFunctionType
ALU = mybir.AluOpType
AX = mybir.AxisListType

C = 1024
NH = 16
HID = 2048
FF = 3072
INV_QMAX = 1.0 / 32767.0
RMS_K = float(1024 ** 0.5 + 1e-06)
DEBUG = False

_ENG_OPS = {
    "tensor_copy", "tensor_tensor", "tensor_add", "tensor_sub", "tensor_mul",
    "tensor_max", "tensor_scalar", "tensor_single_scalar", "tensor_scalar_add",
    "tensor_scalar_sub", "tensor_scalar_mul", "tensor_scalar_max",
    "tensor_scalar_min", "tensor_reduce", "reduce_max", "reduce_sum",
    "scalar_tensor_tensor", "select", "transpose", "bn_stats", "bn_aggr",
    "reciprocal", "memset", "memzero", "activation", "copy", "add", "mul",
    "sqrt", "square", "sign", "iota", "matmul", "collective_compute",
}


def _ap_names(args, kwargs):
    ins, outs = [], []

    def walk(v, into):
        if isinstance(v, AP):
            into.append(v.tensor.name)
        elif isinstance(v, (list, tuple)):
            for x in v:
                walk(x, into)

    for i, v in enumerate(args):
        walk(v, ins if i > 0 else outs)
    for k, v in kwargs.items():
        walk(v, outs if k in ("out", "accum_out", "outs") else ins)
    return ins, outs


class Sem:
    def __init__(self, h, name):
        self.h = h
        self.name = name
        self.val = 0


class Eng:

    def __init__(self, eng, ctx, no_hazard=False):
        self._eng = eng
        self._ctx = ctx
        self._no_hazard = no_hazard
        self._touched = set()
        self._written = set()

    @property
    def counting(self):
        return self._ctx["counting"]

    def drain(self):
        if not self.counting:
            self._eng.drain()
        self._touched = set()
        self._written = set()

    def fence(self):
        self._touched = set()
        self._written = set()

    def mark(self, sem, key):
        if self.counting:
            assert key not in self._ctx["marks"], f"duplicate mark {key}"
            self._ctx["marks"][key] = (sem, sem.val)
        else:
            rec = self._ctx["marks"].get(key)
            assert rec is not None and rec[0] is sem and rec[1] == sem.val, \
                f"mark drift at {key}: recorded {rec and rec[1]} vs emit {sem.val}"

    def inc(self, inst, sem, n=1):
        if not self.counting:
            if self._no_hazard:
                inst = self._eng.drain()
            inst.then_inc(sem.h, n)
        sem.val += n

    def wait(self, key):
        if self.counting:
            return
        sem, v = self._ctx["marks"][key]
        self._eng.wait_ge(sem.h, v)

    def wait_sem_now(self, sem):
        if self.counting:
            return
        self._eng.wait_ge(sem.h, sem.val)

    def dma(self, out, in_, sem):
        if self.counting:
            sem.val += 16
            return None
        self._eng.wait_ge(sem.h, sem.val)
        inst = self._eng.dma_start(out=out, in_=in_)
        inst.then_inc(sem.h, 16)
        sem.val += 16
        return inst

    def cc(self, in_ap, out_ap, sem):
        if self.counting:
            sem.val += 1
            return None
        inst = self._eng.collective_compute(
            "AllReduce", ALU.max, replica_groups=[list(range(8))],
            ins=[in_ap], outs=[out_ap])
        inst.then_inc(sem.h, 1)
        sem.val += 1
        return inst

    def __getattr__(self, name):
        if name.startswith("_"):
            raise AttributeError(name)
        if name not in _ENG_OPS:
            fn = getattr(self._eng, name)
            if self.counting:
                return lambda *a, **k: None
            return fn

        def wrapped(*args, **kwargs):
            if self.counting:
                return None
            if not self._no_hazard:
                ins, outs = _ap_names(args, kwargs)
                if any(n in self._written for n in ins) or any(n in self._touched for n in outs):
                    self.drain()
                r = getattr(self._eng, name)(*args, **kwargs)
                self._touched.update(ins + outs)
                self._written.update(outs)
                return r
            return getattr(self._eng, name)(*args, **kwargs)

        return wrapped


R_SQ, R_RC, R_RK, R_M, R_EX = 0, 512, 1024, 1536, 2048
R_VAR, R_SD, R_R, R_MR = R_SQ, R_RC, R_RK, R_EX


def build_program(T):
    NJ = T // 512
    NJ2 = T // 256
    TJ = 256
    nc = bass.Bass("TRN2", target_bir_lowering=False, debug=False)
    scratch_kind = "ExternalOutput" if DEBUG else "Internal"

    x_in = nc.dram_tensor("x", [C, T], F32, kind="ExternalInput").ap()
    wqkv_d = nc.dram_tensor("wqkv", [C, 3 * C], F32, kind="ExternalInput").ap()
    wproj_d = nc.dram_tensor("wproj", [C, C], F32, kind="ExternalInput").ap()
    wc1_d = nc.dram_tensor("wc1", [C, HID], F32, kind="ExternalInput").ap()
    wc2_d = nc.dram_tensor("wc2", [HID, C], F32, kind="ExternalInput").ap()
    wup_d = nc.dram_tensor("wup", [FF, C], F32, kind="ExternalInput").ap()
    wgate_d = nc.dram_tensor("wgate", [FF, C], F32, kind="ExternalInput").ap()
    wdown_d = nc.dram_tensor("wdown", [C, FF], F32, kind="ExternalInput").ap()
    wup_t = nc.dram_tensor("wup_t", [C, FF], F32, kind="ExternalInput").ap()
    wgate_t = nc.dram_tensor("wgate_t", [C, FF], F32, kind="ExternalInput").ap()
    wdown_t = nc.dram_tensor("wdown_t", [FF, C], F32, kind="ExternalInput").ap()
    vecs_d = nc.dram_tensor("vecs", [128, 48], F32, kind="ExternalInput").ap()
    c1b_d = nc.dram_tensor("c1b", [128, 16], F32, kind="ExternalInput").ap()
    y_out = nc.dram_tensor("y", [C, T], F32, kind="ExternalOutput").ap()

    def scratch(name, shape, dt):
        return nc.dram_tensor(name, shape, dt, kind=scratch_kind).ap()

    d_phiq = scratch("d_phiq", [C, T], F32)
    d_x2 = scratch("d_x2", [C, T], F32)
    d_y1 = scratch("d_y1", [C, T], F32)
    d_U = scratch("d_U", [FF, T], F32)
    d_G = scratch("d_G", [FF, T], F32)
    d_mu = scratch("d_mu", [FF, T], I32)
    d_P = scratch("d_P", [FF, T], F32)
    d_Q = scratch("d_Q", [FF, T], F32)
    cc_in = [nc.dram_tensor(f"cc_in{i}", [1, 2], F32).ap() for i in range(3)]
    cc_out = [nc.dram_tensor(f"cc_out{i}", [1, 2], F32).ap() for i in range(3)]

    eye32_c = nc.inline_tensor(np.eye(128, dtype=np.float32), name="eye32c").ap()
    eye16_c = nc.inline_tensor(np.eye(128, dtype=np.float16), name="eye16c").ap()
    onesr_c = nc.inline_tensor(np.ones((1, 128), np.float32), name="onesrc").ap()
    onesc_c = nc.inline_tensor(np.ones((128, 1), np.float32), name="onescc").ap()
    cbias_c = nc.inline_tensor(np.full((128, 1), 1e-5, np.float32), name="cbiasc").ap()
    eA = np.zeros((32, 8 * 128), np.float32)
    for ci in range(8):
        for p in range(128):
            eA[4 * ci + p // 32, ci * 128 + p] = 1.0
    eexpA_c = nc.inline_tensor(eA, name="eexpAc").ap()
    eB = np.zeros((96, 24 * 128), np.float32)
    for ci in range(24):
        for p in range(128):
            eB[4 * ci + p // 32, ci * 128 + p] = 1.0
    eexpB_c = nc.inline_tensor(eB, name="eexpBc").ap()

    es = contextlib.ExitStack()

    def sbuf(name, shape, dt):
        return es.enter_context(nc.sbuf_tensor(name, shape, dt))

    def mksem(name):
        return Sem(es.enter_context(nc.semaphore(name)), name)

    with es, nc.Block() as block:
        vec_sb = sbuf("vec_sb", [128, 48], F32)
        gpb_sb = sbuf("gpb_sb", [128, 16], F32)
        c1b_sb = sbuf("c1b_sb", [128, 16], F32)
        eye32_sb = sbuf("eye32_sb", [128, 128], F32)
        eye16_sb = sbuf("eye16_sb", [128, 128], F16)
        onesr_sb = sbuf("onesr_sb", [1, 128], F32)
        onesc_sb = sbuf("onesc_sb", [128, 1], F32)
        eexpA_sb = sbuf("eexpA_sb", [32, 8 * 128], F32)
        cbias_sb = sbuf("cbias_sb", [128, 1], F32)
        kvext = sbuf("kvext", [128, 8 * 65], F32)
        scal = sbuf("scal", [1, 32], F32)
        scbc = sbuf("scbc", [128, 8], F32)
        rowb = sbuf("rowb", [1, 5 * 512], F32)
        amaxc = sbuf("amaxc", [128, 8], F32)

        sLD = mksem("sLD")
        sL2 = mksem("sL2")
        sSP = mksem("sSP")
        sS2 = mksem("sS2")
        sMM = mksem("sMM")
        sEV = mksem("sEV")
        sVE = mksem("sVE")
        sGP = mksem("sGP")
        sPB = mksem("sPB")
        ALL_SEMS = [sLD, sL2, sSP, sS2, sMM, sEV, sVE, sGP, sPB]

        ctx = {"counting": False, "marks": {}}
        EV = Eng(nc.vector, ctx)
        EC = Eng(nc.scalar, ctx)
        EG = Eng(nc.gpsimd, ctx)
        ET = Eng(nc.tensor, ctx, no_hazard=True)
        EY = Eng(nc.sync, ctx)

        def vsl(row, ci):
            return vec_sb[:, row * 8 + ci: row * 8 + ci + 1]

        def row(slot, n=512):
            return rowb[0:1, slot:slot + n]

        def sc1(i):
            return scal[0:1, i:i + 1]

        _phase_no = [0]

        def run_phase(fns, label=None):
            label = label or f"ph{_phase_no[0]}"
            _phase_no[0] += 1
            nh = os.environ.get("NH_PHASES")
            if nh is not None:
                ET._no_hazard = label in nh.split(",")
            snap = {s.name: s.val for s in ALL_SEMS}
            ctx["counting"] = True
            for _, fn in fns:
                fn()
                for e in (EV, EC, EG, ET, EY):
                    e.fence()
            for s in ALL_SEMS:
                s.val = snap[s.name]
            ctx["counting"] = False
            deco = {"sync": block.sync, "vector": block.vector, "scalar": block.scalar,
                    "gpsimd": block.gpsimd, "tensor": block.tensor}
            with nc.named_scope(label):
                for name, fn in fns:
                    deco[name](lambda e, fn=fn: fn())
            tgt = sPB.val + 5
            for dec, eng, qsems in ((block.sync, nc.sync, (sLD, sL2)),
                                    (block.vector, nc.vector, ()),
                                    (block.scalar, nc.scalar, ()),
                                    (block.gpsimd, nc.gpsimd, (sSP, sS2)),
                                    (block.tensor, nc.tensor, ())):
                def bfn(e, eng=eng, tgt=tgt, qsems=qsems):
                    for q in qsems:
                        eng.wait_ge(q.h, q.val)
                    if eng is not nc.sync:
                        eng.drain()
                    eng.nop().then_inc(sPB.h, 1)
                    eng.wait_ge(sPB.h, tgt)
                dec(bfn)
            sPB.val = tgt
            for e in (EV, EC, EG, ET, EY):
                e.fence()
            ctx["marks"] = {}

        sc_qkv = contextlib.ExitStack()
        wqkv_h = sc_qkv.enter_context(nc.sbuf_tensor("wqkv_h", [128, 8 * 3 * C], F16))
        wqkv_l = sc_qkv.enter_context(nc.sbuf_tensor("wqkv_l", [128, 8 * 3 * C], F16))
        def emit_wprep(tag, plan, wstg, wtmp, stg_cols, after_consts):
            items = []
            for (src, nch, ncols, fold, dh, dl) in plan:
                for ci in range(nch):
                    items.append((src, ci, ncols, fold, dh, dl))

            def w_sync():
                e = EY
                for nb, (src, ci, ncols, fold, dh, dl) in enumerate(items):
                    if nb >= 2:
                        e.wait((tag, "lo", nb - 2))
                    e.dma(wstg[:, (nb % 2) * stg_cols:(nb % 2) * stg_cols + ncols],
                          src[ci * 128:(ci + 1) * 128, :], sLD)
                    e.mark(sLD, (tag, "w", nb))

            def w_scalar():
                e = EC
                if after_consts:
                    e.wait("p0consts")
                for nb, (src, ci, ncols, fold, dh, dl) in enumerate(items):
                    e.wait((tag, "w", nb))
                    if nb >= 1:
                        e.wait((tag, "lo", nb - 1))
                    stg = wstg[:, (nb % 2) * stg_cols:(nb % 2) * stg_cols + ncols]
                    sc = vsl(fold, ci) if fold is not None else 1.0
                    e.activation(dh[:, ci * ncols:(ci + 1) * ncols], stg,
                                 AF.Identity, scale=sc)
                    i = e.activation(wtmp[:, 0:ncols], stg, AF.Identity, scale=sc)
                    e.inc(i, sEV, 1)
                    e.mark(sEV, (tag, "hi", nb))

            def w_vector():
                e = EV
                for nb, (src, ci, ncols, fold, dh, dl) in enumerate(items):
                    e.wait((tag, "hi", nb))
                    i = e.tensor_tensor(dl[:, ci * ncols:(ci + 1) * ncols],
                                        wtmp[:, 0:ncols],
                                        dh[:, ci * ncols:(ci + 1) * ncols],
                                        ALU.subtract)
                    e.inc(i, sVE, 1)
                    e.mark(sVE, (tag, "lo", nb))

            return w_sync, w_scalar, w_vector

        with nc.sbuf_tensor("wstg", [128, 2 * 3 * C], F32) as wstg, \
             nc.sbuf_tensor("wtmp", [128, 3 * C], F32) as wtmp:

            ws0, wc0, wv0 = emit_wprep(
                "p0", [(wqkv_d, 8, 3 * C, 0, wqkv_h, wqkv_l)],
                wstg, wtmp, 3 * C, True)

            def p0_sync():
                e = EY
                e.dma(eye32_sb[:, :], eye32_c[:, :], sLD)
                e.dma(eye16_sb[:, :], eye16_c[:, :], sLD)
                e.dma(onesr_sb[:, :], onesr_c[:, :], sLD)
                e.dma(onesc_sb[:, :], onesc_c[:, :], sLD)
                e.dma(eexpA_sb[:, :], eexpA_c[:, :], sLD)
                e.dma(cbias_sb[:, :], cbias_c[:, :], sLD)
                e.dma(vec_sb[:, :], vecs_d[:, :], sLD)
                e.dma(c1b_sb[:, :], c1b_d[:, :], sLD)
                e.mark(sLD, "p0consts")
                ws0()

            def p0_vector():
                e = EV
                e.wait("p0consts")
                e.tensor_tensor(gpb_sb[:, 0:8], vec_sb[:, 16:24], vec_sb[:, 24:32], ALU.mult)
                e.tensor_tensor(gpb_sb[:, 8:16], vec_sb[:, 16:24], vec_sb[:, 32:40], ALU.mult)
                wv0()

            run_phase([("sync", p0_sync), ("scalar", wc0), ("vector", p0_vector)], "p0")

        with nc.sbuf_tensor("xbuf", [128, 8 * 512], F32) as xbuf, \
             nc.sbuf_tensor("sh1", [128, 8 * 512], F32) as sh1, \
             nc.sbuf_tensor("r1bc", [128, 512], F32) as r1bc, \
             nc.sbuf_tensor("xnh", [128, 8 * 512], F16) as xnh, \
             nc.sbuf_tensor("xnl", [128, 8 * 512], F16) as xnl, \
             nc.sbuf_tensor("phib", [128, 2 * 512], F32) as phib, \
             nc.sbuf_tensor("k_sb", [128, 1024], F32) as k_sb, \
             nc.sbuf_tensor("v_sb", [128, 1024], F32) as v_sb, \
             nc.sbuf_tensor("qtmp", [128, 512], F32) as qtmp, \
             nc.psum_tensor("ps_misc", [128, 512], F32) as ps_misc, \
             nc.psum_tensor("ps_q", [128, 1024], F32) as ps_q, \
             nc.psum_tensor("ps_kv", [128, 1024], F32) as ps_kv, \
             nc.psum_tensor("ps_acc", [128, 1024], F32) as ps_acc, \
             nc.psum_tensor("ps_ks", [128, 16], F32) as ps_ks:

            def wq_sl(hi, k, a, b):
                w = wqkv_h if hi else wqkv_l
                return w[:, k * 3 * C + a:k * 3 * C + b]

            def p1_sync():
                e = EY
                x3 = x_in.rearrange("(c p) t -> p c t", p=128)
                for j in range(NJ):
                    if j >= 1:
                        e.wait(("p1xn", j - 1))
                    e.dma(xbuf[:, :].rearrange("p (c t) -> p c t", t=512),
                          x3[:, :, 512 * j:512 * (j + 1)], sLD)
                    e.mark(sLD, ("p1x", j))

            def p1_scalar():
                e = EC
                for j in range(NJ):
                    e.wait(("p1x", j))
                    if j >= 1:
                        e.wait(("p1xn", j - 1))
                    i = e.activation(sh1[:, :], xbuf[:, :], AF.Square)
                    e.inc(i, sEV, 1)
                    e.mark(sEV, ("p1sq", j))
                    e.wait(("p1ss", j))
                    i = e.activation(row(R_SQ), ps_misc[0:1, 0:512], AF.Sqrt)
                    e.inc(i, sEV, 1)
                    e.mark(sEV, ("p1sqrt", j))
                    for co in range(8):
                        e.wait(("p1mmq", j, co))
                        gg = j * 8 + co
                        if gg >= 2:
                            e.wait(("p1phisp", (gg - 2) // 8, (gg - 2) % 8))
                        pq = ps_q[:, (co % 2) * 512:(co % 2) * 512 + 512]
                        e.activation(qtmp[:, :], pq, AF.Square)
                        i = e.activation(phib[:, (co % 2) * 512:(co % 2) * 512 + 512],
                                         qtmp[:, :], AF.Exp, scale=-0.5)
                        e.inc(i, sEV, 1)
                        e.mark(sEV, ("p1phi", j, co))
                    for tt in range(4):
                        for fo in range(4):
                            e.wait(("p1mmkv", j, tt, fo))
                            if fo == 0 and (tt > 0 or j > 0):
                                e.wait(("p1kvb", j if tt > 0 else j - 1,
                                        tt - 1 if tt > 0 else 3))
                            pkv = ps_kv[:, (fo % 2) * 512:(fo % 2) * 512 + 512]
                            if fo < 2:
                                e.activation(qtmp[:, :], pkv, AF.Square)
                                i = e.activation(k_sb[:, fo * 512:(fo + 1) * 512],
                                                 qtmp[:, :], AF.Exp, scale=-0.5)
                            else:
                                i = e.copy(v_sb[:, (fo - 2) * 512:(fo - 1) * 512], pkv)
                            e.inc(i, sEV, 1)
                            e.mark(sEV, ("p1kvev", j, tt, fo))

            def p1_vector():
                e = EV
                for j in range(NJ):
                    e.wait(("p1sqrt", j))
                    e.reciprocal(row(R_RC), row(R_SQ))
                    i = e.tensor_scalar(row(R_RK), row(R_RC), RMS_K, None, ALU.mult)
                    e.inc(i, sVE, 1)
                    e.mark(sVE, ("p1r1row", j))
                    e.wait(("p1r1bcmm", j))
                    i = e.tensor_copy(r1bc[:, :], ps_misc[:, :])
                    e.inc(i, sVE, 1)
                    e.mark(sVE, ("p1r1bc", j))
                    r1b3 = r1bc[:, :].unsqueeze(1).broadcast_to([128, 8, 512])
                    x4 = xbuf[:, :].rearrange("p (c t) -> p c t", t=512)
                    e.tensor_tensor(xnh[:, :].rearrange("p (c t) -> p c t", t=512),
                                    x4, r1b3, ALU.mult)
                    e.tensor_tensor(sh1[:, :].rearrange("p (c t) -> p c t", t=512),
                                    x4, r1b3, ALU.mult)
                    i = e.tensor_tensor(xnl[:, :], sh1[:, :], xnh[:, :], ALU.subtract)
                    e.inc(i, sVE, 1)
                    e.mark(sVE, ("p1xn", j))

            def p1_tensor():
                e = ET
                for j in range(NJ):
                    e.wait(("p1sq", j))
                    if j >= 1:
                        e.wait(("p1r1bc", j - 1))
                    for k in range(8):
                        i = e.matmul(ps_misc[0:1, 0:512], onesc_sb[:, :],
                                     sh1[:, k * 512:(k + 1) * 512],
                                     start=(k == 0), stop=(k == 7))
                    e.inc(i, sMM, 1)
                    e.mark(sMM, ("p1ss", j))
                    e.wait(("p1sqrt", j))
                    e.wait(("p1r1row", j))
                    i = e.matmul(ps_misc[:, :], onesr_sb[0:1, :], row(R_RK),
                                 start=True, stop=True)
                    e.inc(i, sMM, 1)
                    e.mark(sMM, ("p1r1bcmm", j))
                    e.wait(("p1xn", j))
                    for co in range(8):
                        gg = j * 8 + co
                        if gg >= 2:
                            e.wait(("p1phi", (gg - 2) // 8, (gg - 2) % 8))
                        pq = ps_q[:, (co % 2) * 512:(co % 2) * 512 + 512]
                        n = 0
                        for (ah, bh) in ((1, 1), (1, 0), (0, 1)):
                            for k in range(8):
                                xs = (xnh if bh else xnl)[:, k * 512:(k + 1) * 512]
                                i = e.matmul(pq, wq_sl(ah, k, co * 128, (co + 1) * 128),
                                             xs, start=(n == 0), stop=(n == 23))
                                n += 1
                        e.inc(i, sMM, 1)
                        e.mark(sMM, ("p1mmq", j, co))
                    for tt in range(4):
                        for fo in range(4):
                            ff = j * 16 + tt * 4 + fo
                            if ff >= 2:
                                pf = ff - 2
                                e.wait(("p1kvev", pf // 16, (pf % 16) // 4, (pf % 16) % 4))
                            pkv = ps_kv[:, (fo % 2) * 512:(fo % 2) * 512 + 512]
                            n = 0
                            for (ah, bh) in ((1, 1), (1, 0), (0, 1)):
                                for k in range(8):
                                    xs = (xnh if ah else xnl)[:, k * 512 + tt * 128:
                                                              k * 512 + tt * 128 + 128]
                                    ws_ = wq_sl(bh, k, C + fo * 512, C + (fo + 1) * 512)
                                    i = e.matmul(pkv, xs, ws_, start=(n == 0), stop=(n == 23))
                                    n += 1
                            e.inc(i, sMM, 1)
                            e.mark(sMM, ("p1mmkv", j, tt, fo))
                        for fo in range(4):
                            e.wait(("p1kvev", j, tt, fo))
                        first = (j == 0 and tt == 0)
                        last = (j == NJ - 1 and tt == 3)
                        for p in range(8):
                            e.matmul(ps_acc[:, p * 128:(p + 1) * 128],
                                     k_sb[:, p * 128:(p + 1) * 128],
                                     v_sb[:, p * 128:(p + 1) * 128],
                                     start=(first and p % 4 == 0),
                                     stop=(last and p % 4 == 3))
                            i = e.matmul(ps_ks[:, p:p + 1],
                                         k_sb[:, p * 128:(p + 1) * 128],
                                         onesc_sb[:, :], start=(first and p == 0),
                                         stop=(last and p == 7))
                        e.inc(i, sMM, 1)
                        e.mark(sMM, ("p1kvb", j, tt))

            def p1_gpsimd():
                e = EG
                for j in range(NJ):
                    for co in range(8):
                        e.wait(("p1phi", j, co))
                        e.dma(d_phiq[co * 128:(co + 1) * 128, 512 * j:512 * (j + 1)],
                              phib[:, (co % 2) * 512:(co % 2) * 512 + 512], sSP)
                        e.mark(sSP, ("p1phisp", j, co))

            run_phase([("sync", p1_sync), ("scalar", p1_scalar), ("vector", p1_vector),
                       ("tensor", p1_tensor), ("gpsimd", p1_gpsimd)], "p1")

            def p15_vector():
                e = EV
                for p in range(8):
                    e.tensor_copy(kvext[0:64, p * 65:p * 65 + 64],
                                  ps_acc[0:64, p * 128:p * 128 + 64])
                    e.tensor_copy(kvext[64:128, p * 65:p * 65 + 64],
                                  ps_acc[64:128, p * 128 + 64:p * 128 + 128])
                    e.tensor_copy(kvext[:, p * 65 + 64:p * 65 + 65], ps_ks[:, p:p + 1])

            run_phase([("vector", p15_vector)], "p15")

        sc_qkv.close()

        sc_proj = contextlib.ExitStack()
        wproj_h = sc_proj.enter_context(nc.sbuf_tensor("wproj_h", [128, 8 * C], F16))
        wproj_l = sc_proj.enter_context(nc.sbuf_tensor("wproj_l", [128, 8 * C], F16))
        with nc.sbuf_tensor("wstgp", [128, 2 * C], F32) as wstgp, \
             nc.sbuf_tensor("wtmpp", [128, C], F32) as wtmpp:
            pws, pwc, pwv = emit_wprep(
                "p2p", [(wproj_d, 8, C, None, wproj_h, wproj_l)],
                wstgp, wtmpp, C, False)
            run_phase([("sync", pws), ("scalar", pwc), ("vector", pwv)], "p2pre")

        with nc.sbuf_tensor("qbuf", [128, 8 * 512], F32) as qbuf, \
             nc.sbuf_tensor("atile", [128, 1024], F32) as atile, \
             nc.sbuf_tensor("zt", [128, 16], F32) as zt, \
             nc.sbuf_tensor("athi", [128, 8 * 512], F16) as athi, \
             nc.sbuf_tensor("atlo", [128, 8 * 512], F16) as atlo, \
             nc.sbuf_tensor("xck", [128, 2 * 512], F32) as xck, \
             nc.sbuf_tensor("ptmp", [128, 2 * 512], F32) as ptmp, \
             nc.sbuf_tensor("x2b", [128, 2 * 512], F32) as x2b, \
             nc.psum_tensor("ps_ap", [128, 1024], F32) as ps_ap, \
             nc.psum_tensor("ps_tr", [128, 1024], F32) as ps_tr, \
             nc.psum_tensor("ps_pr", [128, 2 * 512], F32) as ps_pr:

            def p2_sync():
                e = EY
                ph3 = d_phiq.rearrange("(c p) t -> p c t", p=128)
                x3 = x_in.rearrange("(c p) t -> p c t", p=128)
                for j in range(NJ):
                    if j >= 1:
                        e.wait(("p2apdone", j - 1))
                    e.dma(qbuf[:, :].rearrange("p (c t) -> p c t", t=512),
                          ph3[:, :, 512 * j:512 * (j + 1)], sLD)
                    e.mark(sLD, ("p2q", j))
                    for oc in range(8):
                        qq = j * 8 + oc
                        if qq >= 2:
                            e.wait(("p2x2v", (qq - 2) // 8, (qq - 2) % 8))
                        e.dma(xck[:, (oc % 2) * 512:(oc % 2) * 512 + 512],
                              x3[:, oc, 512 * j:512 * (j + 1)], sLD)
                        e.mark(sLD, ("p2xc", j, oc))

            def p2_tensor():
                e = ET
                for j in range(NJ):
                    e.wait(("p2q", j))
                    for tt in range(4):
                        for h in range(NH):
                            g = h // 4
                            gg = j * 16 + tt * 4 + g
                            if h % 4 == 0 and gg >= 2:
                                pz = gg - 2
                                e.wait(("p2zev", pz // 16, (pz % 16) // 4, (pz % 16) % 4))
                            pr, hf = h // 2, h % 2
                            qsl = qbuf[64 * hf:64 * (hf + 1),
                                       pr * 512 + tt * 128:pr * 512 + tt * 128 + 128]
                            ksl = kvext[64 * hf:64 * (hf + 1), pr * 65:(pr + 1) * 65]
                            if h % 4 != 0:
                                e.drain()
                            i = e.matmul(ps_ap[:, (g % 2) * 512 + (h % 4) * 65:
                                               (g % 2) * 512 + (h % 4) * 65 + 65],
                                         qsl, ksl, start=True, stop=True)
                            if h % 4 == 3:
                                e.inc(i, sMM, 1)
                                e.mark(sMM, ("p2ap", j, tt, g))
                        for p in range(8):
                            e.wait(("p2at", j, tt, p // 2))
                            pp = (j * 4 + tt) * 8 + p
                            if pp >= 2:
                                q_, p_ = divmod(pp - 2, 8)
                                j_, tt_ = divmod(q_, 4)
                                e.wait(("p2trlo", j_, tt_, p_))
                            i = e.transpose(ps_tr[:, (p % 2) * 512:(p % 2) * 512 + 128],
                                            atile[:, p * 128:(p + 1) * 128], eye32_sb[:, :])
                            e.inc(i, sMM, 1)
                            e.mark(sMM, ("p2tr", j, tt, p))
                    e.mark(sMM, ("p2apdone", j))
                    e.wait(("p2trall", j))
                    for oc in range(8):
                        qq = j * 8 + oc
                        if qq >= 2:
                            e.wait(("p2ev", (qq - 2) // 8, (qq - 2) % 8))
                        pp_ = ps_pr[:, (oc % 2) * 512:(oc % 2) * 512 + 512]
                        n = 0
                        for (ah, bh) in ((1, 1), (1, 0), (0, 1)):
                            for k in range(8):
                                w = (wproj_h if ah else wproj_l)[:, k * C + oc * 128:
                                                                 k * C + (oc + 1) * 128]
                                xs = (athi if bh else atlo)[:, k * 512:(k + 1) * 512]
                                i = e.matmul(pp_, w, xs, start=(n == 0), stop=(n == 23))
                                n += 1
                        e.inc(i, sMM, 1)
                        e.mark(sMM, ("p2pr", j, oc))

            def p2_vector():
                e = EV
                for j in range(NJ):
                    for tt in range(4):
                        for g in range(4):
                            e.wait(("p2ap", j, tt, g))
                            if not (j == 0 and tt == 0):
                                pj, ptt = (j, tt - 1) if tt > 0 else (j - 1, 3)
                                e.wait(("p2tr", pj, ptt, 2 * g + 1))
                            pa = ps_ap[:, (g % 2) * 512:(g % 2) * 512 + 260]
                            den = pa.rearrange("p (h w) -> p h w", w=65)[:, :, 64:65]
                            e.tensor_scalar(zt[:, g * 4:(g + 1) * 4],
                                            den.rearrange("p h w -> p (h w)"),
                                            1e-6, None, ALU.add)
                            e.reciprocal(zt[:, g * 4:(g + 1) * 4], zt[:, g * 4:(g + 1) * 4])
                            for hh in range(4):
                                h = g * 4 + hh
                                i = e.tensor_scalar(atile[:, h * 64:(h + 1) * 64],
                                                    pa[:, hh * 65:hh * 65 + 64],
                                                    zt[:, h:h + 1], None, ALU.mult)
                            e.inc(i, sVE, 1)
                            e.mark(sVE, ("p2zev", j, tt, g))
                            e.mark(sVE, ("p2at", j, tt, g))
                        for p in range(8):
                            e.wait(("p2trhi", j, tt, p))
                            dst = atlo[:, p * 512 + tt * 128:p * 512 + tt * 128 + 128]
                            src_h = athi[:, p * 512 + tt * 128:p * 512 + tt * 128 + 128]
                            i = e.tensor_tensor(dst,
                                                ps_tr[:, (p % 2) * 512:(p % 2) * 512 + 128],
                                                src_h, ALU.subtract)
                            e.inc(i, sVE, 1)
                            e.mark(sVE, ("p2trlo", j, tt, p))
                    e.mark(sVE, ("p2trall", j))
                    for oc in range(8):
                        e.wait(("p2ev", j, oc))
                        e.wait(("p2xc", j, oc))
                        qq = j * 8 + oc
                        if qq >= 2:
                            e.wait(("p2x2sp", (qq - 2) // 8, (qq - 2) % 8))
                        i = e.tensor_tensor(x2b[:, (oc % 2) * 512:(oc % 2) * 512 + 512],
                                            ptmp[:, (oc % 2) * 512:(oc % 2) * 512 + 512],
                                            xck[:, (oc % 2) * 512:(oc % 2) * 512 + 512],
                                            ALU.add)
                        e.inc(i, sVE, 1)
                        e.mark(sVE, ("p2x2v", j, oc))

            def p2_scalar():
                e = EC
                for j in range(NJ):
                    for tt in range(4):
                        for p in range(8):
                            e.wait(("p2tr", j, tt, p))
                            i = e.copy(athi[:, p * 512 + tt * 128:p * 512 + tt * 128 + 128],
                                       ps_tr[:, (p % 2) * 512:(p % 2) * 512 + 128])
                            e.inc(i, sEV, 1)
                            e.mark(sEV, ("p2trhi", j, tt, p))
                    for oc in range(8):
                        e.wait(("p2pr", j, oc))
                        qq = j * 8 + oc
                        if qq >= 2:
                            e.wait(("p2x2v", (qq - 2) // 8, (qq - 2) % 8))
                        i = e.activation(ptmp[:, (oc % 2) * 512:(oc % 2) * 512 + 512],
                                         ps_pr[:, (oc % 2) * 512:(oc % 2) * 512 + 512],
                                         AF.Identity, scale=vsl(2, oc),
                                         bias=gpb_sb[:, oc:oc + 1])
                        e.inc(i, sEV, 1)
                        e.mark(sEV, ("p2ev", j, oc))

            def p2_gpsimd():
                e = EG
                for j in range(NJ):
                    for oc in range(8):
                        e.wait(("p2x2v", j, oc))
                        e.dma(d_x2[oc * 128:(oc + 1) * 128, 512 * j:512 * (j + 1)],
                              x2b[:, (oc % 2) * 512:(oc % 2) * 512 + 512], sSP)
                        e.mark(sSP, ("p2x2sp", j, oc))

            run_phase([("sync", p2_sync), ("tensor", p2_tensor), ("vector", p2_vector),
                       ("scalar", p2_scalar), ("gpsimd", p2_gpsimd)], "p2")

        sc_proj.close()

        sc_c12 = contextlib.ExitStack()
        wc1_h = sc_c12.enter_context(nc.sbuf_tensor("wc1_h", [128, 8 * HID], F16))
        wc1_l = sc_c12.enter_context(nc.sbuf_tensor("wc1_l", [128, 8 * HID], F16))
        wc2_h = sc_c12.enter_context(nc.sbuf_tensor("wc2_h", [128, 16 * C], F16))
        wc2_l = sc_c12.enter_context(nc.sbuf_tensor("wc2_l", [128, 16 * C], F16))
        with nc.sbuf_tensor("wstg2", [128, 2 * HID], F32) as wstg2, \
             nc.sbuf_tensor("wtmp2", [128, HID], F32) as wtmp2:
            cw_s, cw_c, cw_v = emit_wprep(
                "p3w", [(wc1_d, 8, HID, 1, wc1_h, wc1_l),
                        (wc2_d, 16, C, None, wc2_h, wc2_l)],
                wstg2, wtmp2, HID, False)
            run_phase([("sync", cw_s), ("scalar", cw_c), ("vector", cw_v)], "p3w")

        with nc.sbuf_tensor("x2t", [128, 8 * TJ], F32) as x2t, \
             nc.sbuf_tensor("sh3", [128, 8 * TJ], F32) as sh3, \
             nc.sbuf_tensor("xn2h", [128, 8 * TJ], F16) as xn2h, \
             nc.sbuf_tensor("xn2l", [128, 8 * TJ], F16) as xn2l, \
             nc.sbuf_tensor("rbc3", [128, 2 * TJ], F32) as rbc3, \
             nc.sbuf_tensor("h1b", [128, 2 * TJ], F32) as h1b, \
             nc.sbuf_tensor("h1h", [128, 16 * TJ], F16) as h1h, \
             nc.sbuf_tensor("h1l", [128, 16 * TJ], F16) as h1l, \
             nc.sbuf_tensor("pt3", [128, 2 * TJ], F32) as pt3, \
             nc.sbuf_tensor("x3b", [128, 8 * TJ], F32) as x3b, \
             nc.sbuf_tensor("y1b", [128, 8 * TJ], F32) as y1b, \
             nc.psum_tensor("ps_c1", [128, 1024], F32) as ps_c1, \
             nc.psum_tensor("ps_c2", [128, 1024], F32) as ps_c2, \
             nc.psum_tensor("ps_ra", [128, 512], F32) as ps_ra, \
             nc.psum_tensor("ps_rb", [128, 512], F32) as ps_rb:

            def p3_sync():
                e = EY
                x23 = d_x2.rearrange("(c p) t -> p c t", p=128)
                x03 = x_in.rearrange("(c p) t -> p c t", p=128)
                for j in range(NJ2):
                    if j >= 1:
                        e.wait(("p3x3all", j - 1))
                    e.dma(x2t[:, :].rearrange("p (c t) -> p c t", t=TJ),
                          x23[:, :, TJ * j:TJ * (j + 1)], sLD)
                    e.mark(sLD, ("p3x2ld", j))
                    if j >= 1:
                        e.wait(("p3y1sp", j - 1))
                    e.dma(y1b[:, :].rearrange("p (c t) -> p c t", t=TJ),
                          x03[:, :, TJ * j:TJ * (j + 1)], sL2)
                    e.mark(sL2, ("p3x0", j))

            def p3_scalar():
                e = EC
                for j in range(NJ2):
                    e.wait(("p3x2ld", j))
                    if j >= 1:
                        e.wait(("p3xn2", j - 1))
                    i = e.activation(sh3[:, :], x2t[:, :], AF.Square)
                    e.inc(i, sEV, 1)
                    e.mark(sEV, ("p3sq", j))
                    e.wait(("p3ss", j))
                    i = e.activation(row(R_SQ, TJ), ps_ra[0:1, 0:TJ], AF.Sqrt)
                    e.inc(i, sEV, 1)
                    e.mark(sEV, ("p3sqrt", j))
                    for ho in range(16):
                        e.wait(("p3mc1", j, ho))
                        hh = j * 16 + ho
                        if hh >= 2:
                            e.wait(("p3h1lo", (hh - 2) // 16, (hh - 2) % 16))
                        pc = ps_c1[:, (ho % 2) * 512:(ho % 2) * 512 + TJ]
                        e.activation(h1b[:, (ho % 2) * TJ:(ho % 2) * TJ + TJ], pc,
                                     AF.Gelu, bias=c1b_sb[:, ho:ho + 1])
                        i = e.copy(h1h[:, ho * TJ:(ho + 1) * TJ],
                                   h1b[:, (ho % 2) * TJ:(ho % 2) * TJ + TJ])
                        e.inc(i, sEV, 1)
                        e.mark(sEV, ("p3gel", j, ho))
                    for oc in range(8):
                        e.wait(("p3mc2", j, oc))
                        qq = j * 8 + oc
                        if qq >= 2:
                            e.wait(("p3x3", (qq - 2) // 8, (qq - 2) % 8))
                        i = e.activation(pt3[:, (oc % 2) * TJ:(oc % 2) * TJ + TJ],
                                         ps_c2[:, (oc % 2) * 512:(oc % 2) * 512 + TJ],
                                         AF.Identity, scale=vsl(2, oc),
                                         bias=gpb_sb[:, 8 + oc:9 + oc])
                        e.inc(i, sEV, 1)
                        e.mark(sEV, ("p3ev2", j, oc))
                    e.wait(("p3x3all", j))
                    i = e.activation(sh3[:, :], x3b[:, :], AF.Square)
                    e.inc(i, sEV, 1)
                    e.mark(sEV, ("p3sq3", j))
                    e.wait(("p3ln1mm", j))
                    e.activation(row(R_M, TJ), ps_ra[0:1, 0:TJ], AF.Identity,
                                 scale=1.0 / 1024.0)
                    i = e.activation(row(R_EX, TJ), ps_rb[0:1, 0:TJ], AF.Identity,
                                     scale=1.0 / 1024.0)
                    e.inc(i, sEV, 1)
                    e.mark(sEV, ("p3rows", j))
                    e.wait(("p3var", j))
                    i = e.activation(row(R_SD, TJ), row(R_VAR, TJ), AF.Sqrt, bias=cbias_sb[0:1, 0:1])
                    e.inc(i, sEV, 1)
                    e.mark(sEV, ("p3sd", j))

            def p3_vector():
                e = EV
                e.memset(amaxc[:, 0:1], 0.0)
                for j in range(NJ2):
                    e.wait(("p3sqrt", j))
                    e.reciprocal(row(R_RC, TJ), row(R_SQ, TJ))
                    i = e.tensor_scalar(row(R_RK, TJ), row(R_RC, TJ), RMS_K, None,
                                        ALU.mult)
                    e.inc(i, sVE, 1)
                    e.mark(sVE, ("p3r2row", j))
                    e.wait(("p3r2bc", j))
                    i = e.tensor_copy(rbc3[:, 0:TJ], ps_ra[:, 0:TJ])
                    e.inc(i, sVE, 1)
                    e.mark(sVE, ("p3r2cp", j))
                    rb3 = rbc3[:, 0:TJ].unsqueeze(1).broadcast_to([128, 8, TJ])
                    x4 = x2t[:, :].rearrange("p (c t) -> p c t", t=TJ)
                    e.tensor_tensor(xn2h[:, :].rearrange("p (c t) -> p c t", t=TJ),
                                    x4, rb3, ALU.mult)
                    e.tensor_tensor(sh3[:, :].rearrange("p (c t) -> p c t", t=TJ),
                                    x4, rb3, ALU.mult)
                    i = e.tensor_tensor(xn2l[:, :], sh3[:, :], xn2h[:, :], ALU.subtract)
                    e.inc(i, sVE, 1)
                    e.mark(sVE, ("p3xn2", j))
                    for ho in range(16):
                        e.wait(("p3gel", j, ho))
                        i = e.tensor_tensor(h1l[:, ho * TJ:(ho + 1) * TJ],
                                            h1b[:, (ho % 2) * TJ:(ho % 2) * TJ + TJ],
                                            h1h[:, ho * TJ:(ho + 1) * TJ], ALU.subtract)
                        e.inc(i, sVE, 1)
                        e.mark(sVE, ("p3h1lo", j, ho))
                    for oc in range(8):
                        e.wait(("p3ev2", j, oc))
                        e.wait(("p3x0", j))
                        e.tensor_tensor(x3b[:, oc * TJ:(oc + 1) * TJ],
                                        pt3[:, (oc % 2) * TJ:(oc % 2) * TJ + TJ],
                                        x2t[:, oc * TJ:(oc + 1) * TJ], ALU.add)
                        i = e.tensor_tensor(x3b[:, oc * TJ:(oc + 1) * TJ],
                                            x3b[:, oc * TJ:(oc + 1) * TJ],
                                            y1b[:, oc * TJ:(oc + 1) * TJ], ALU.add)
                        e.inc(i, sVE, 1)
                        e.mark(sVE, ("p3x3", j, oc))
                    e.mark(sVE, ("p3x3all", j))
                    e.wait(("p3rows", j))
                    e.tensor_tensor(row(R_VAR, TJ), row(R_M, TJ), row(R_M, TJ), ALU.mult)
                    i = e.tensor_tensor(row(R_VAR, TJ), row(R_EX, TJ), row(R_VAR, TJ),
                                        ALU.subtract)
                    e.inc(i, sVE, 1)
                    e.mark(sVE, ("p3var", j))
                    e.wait(("p3sd", j))
                    e.reciprocal(row(R_R, TJ), row(R_SD, TJ))
                    i = e.tensor_tensor(row(R_MR, TJ), row(R_M, TJ), row(R_R, TJ), ALU.mult)
                    e.inc(i, sVE, 1)
                    e.mark(sVE, ("p3lnr", j))
                    e.wait(("p3lnbc", j))
                    if j >= 1:
                        e.wait(("p3y1sp", j - 1))
                    e.tensor_copy(rbc3[:, 0:TJ], ps_ra[:, 0:TJ])
                    i = e.tensor_copy(rbc3[:, TJ:2 * TJ], ps_rb[:, 0:TJ])
                    e.inc(i, sVE, 1)
                    e.mark(sVE, ("p3lncp", j))
                    rb = rbc3[:, 0:TJ].unsqueeze(1).broadcast_to([128, 8, TJ])
                    mrb = rbc3[:, TJ:2 * TJ].unsqueeze(1).broadcast_to([128, 8, TJ])
                    x5 = x3b[:, :].rearrange("p (c t) -> p c t", t=TJ)
                    y5 = y1b[:, :].rearrange("p (c t) -> p c t", t=TJ)
                    e.tensor_tensor(y5, x5, rb, ALU.mult)
                    e.tensor_tensor(y5, y5, mrb, ALU.subtract)
                    e.tensor_reduce(amaxc[:, 4:5], y1b[:, :], AX.X, ALU.max,
                                    apply_absolute_value=True)
                    i = e.tensor_tensor(amaxc[:, 0:1], amaxc[:, 0:1], amaxc[:, 4:5],
                                        ALU.max)
                    e.inc(i, sVE, 1)
                    e.mark(sVE, ("p3y1", j))

            def p3_tensor():
                e = ET
                for j in range(NJ2):
                    e.wait(("p3sq", j))
                    if j >= 1:
                        e.wait(("p3lncp", j - 1))
                    for k in range(8):
                        i = e.matmul(ps_ra[0:1, 0:TJ], onesc_sb[:, :],
                                     sh3[:, k * TJ:(k + 1) * TJ],
                                     start=(k == 0), stop=(k == 7))
                    e.inc(i, sMM, 1)
                    e.mark(sMM, ("p3ss", j))
                    e.wait(("p3sqrt", j))
                    e.wait(("p3r2row", j))
                    i = e.matmul(ps_ra[:, 0:TJ], onesr_sb[0:1, :],
                                 row(R_RK, TJ), start=True, stop=True)
                    e.inc(i, sMM, 1)
                    e.mark(sMM, ("p3r2bc", j))
                    e.wait(("p3xn2", j))
                    for ho in range(16):
                        hh = j * 16 + ho
                        if hh >= 2:
                            e.wait(("p3gel", (hh - 2) // 16, (hh - 2) % 16))
                        pc = ps_c1[:, (ho % 2) * 512:(ho % 2) * 512 + TJ]
                        n = 0
                        for (ah, bh) in ((1, 1), (1, 0), (0, 1)):
                            for k in range(8):
                                w = (wc1_h if ah else wc1_l)[:, k * HID + ho * 128:
                                                             k * HID + (ho + 1) * 128]
                                xs = (xn2h if bh else xn2l)[:, k * TJ:(k + 1) * TJ]
                                i = e.matmul(pc, w, xs, start=(n == 0), stop=(n == 23))
                                n += 1
                        e.inc(i, sMM, 1)
                        e.mark(sMM, ("p3mc1", j, ho))
                    for k in range(16):
                        e.wait(("p3h1lo", j, k))
                    for oc in range(8):
                        qq = j * 8 + oc
                        if qq >= 2:
                            e.wait(("p3ev2", (qq - 2) // 8, (qq - 2) % 8))
                        pc2 = ps_c2[:, (oc % 2) * 512:(oc % 2) * 512 + TJ]
                        n = 0
                        for (ah, bh) in ((1, 1), (1, 0), (0, 1)):
                            for k in range(16):
                                w = (wc2_h if ah else wc2_l)[:, k * C + oc * 128:
                                                             k * C + (oc + 1) * 128]
                                xs = (h1h if bh else h1l)[:, k * TJ:(k + 1) * TJ]
                                i = e.matmul(pc2, w, xs, start=(n == 0), stop=(n == 47))
                                n += 1
                        e.inc(i, sMM, 1)
                        e.mark(sMM, ("p3mc2", j, oc))
                    e.wait(("p3sq3", j))
                    e.wait(("p3r2cp", j))
                    for k in range(8):
                        e.matmul(ps_ra[0:1, 0:TJ], onesc_sb[:, :],
                                 x3b[:, k * TJ:(k + 1) * TJ], start=(k == 0), stop=(k == 7))
                        i = e.matmul(ps_rb[0:1, 0:TJ], onesc_sb[:, :],
                                     sh3[:, k * TJ:(k + 1) * TJ], start=(k == 0),
                                     stop=(k == 7))
                    e.inc(i, sMM, 1)
                    e.mark(sMM, ("p3ln1mm", j))
                    e.wait(("p3lnr", j))
                    e.matmul(ps_ra[:, 0:TJ], onesr_sb[0:1, :], row(R_R, TJ),
                             start=True, stop=True)
                    i = e.matmul(ps_rb[:, 0:TJ], onesr_sb[0:1, :], row(R_MR, TJ),
                                 start=True, stop=True)
                    e.inc(i, sMM, 1)
                    e.mark(sMM, ("p3lnbc", j))

            def p3_gpsimd():
                e = EG
                for j in range(NJ2):
                    e.wait(("p3y1", j))
                    e.dma(d_y1.rearrange("(c p) t -> p c t", p=128)[:, :, TJ * j:TJ * (j + 1)],
                          y1b[:, :].rearrange("p (c t) -> p c t", t=TJ), sSP)
                    e.mark(sSP, ("p3y1sp", j))

            run_phase([("sync", p3_sync), ("scalar", p3_scalar), ("vector", p3_vector),
                       ("tensor", p3_tensor), ("gpsimd", p3_gpsimd)], "p3")

        sc_c12.close()

        def emit_cc_finalize(tag, pieces, ccidx, comp_fn, bcasts, ps_cc):

            def t_fn():
                e = ET
                for n_, (acol, scol) in enumerate(pieces):
                    i = e.transpose(ps_cc[0:1, n_ * 128:n_ * 128 + 128],
                                    amaxc[:, acol:acol + 1], eye32_sb[:, :])
                    e.inc(i, sMM, 1)
                e.mark(sMM, (tag, "tr"))
                e.wait((tag, "comp"))
                for n_, (scol, bcol) in enumerate(bcasts):
                    i = e.matmul(ps_cc[:, n_ * 128:n_ * 128 + 1], onesr_sb[0:1, :],
                                 scal[0:1, scol:scol + 1], start=True, stop=True)
                    e.inc(i, sMM, 1)
                e.mark(sMM, (tag, "bc"))

            def v_fn():
                e = EV
                e.wait((tag, "tr"))
                for n_, (acol, scol) in enumerate(pieces):
                    i = e.tensor_reduce(scal[0:1, scol:scol + 1],
                                        ps_cc[0:1, n_ * 128:n_ * 128 + 128],
                                        AX.X, ALU.max, apply_absolute_value=True)
                if len(pieces) == 1:
                    i = e.memset(scal[0:1, pieces[0][1] + 1:pieces[0][1] + 2], 0.0)
                e.inc(i, sVE, 1)
                e.mark(sVE, (tag, "locmax"))
                e.wait((tag, "ccld"))
                i = comp_fn(e)
                e.inc(i, sVE, 1)
                e.mark(sVE, (tag, "comp"))
                e.wait((tag, "bc"))
                for n_, (scol, bcol) in enumerate(bcasts):
                    i = e.tensor_copy(scbc[:, bcol:bcol + 1], ps_cc[:, n_ * 128:n_ * 128 + 1])
                e.inc(i, sVE, 1)
                e.mark(sVE, (tag, "done"))

            def g_fn():
                e = EG
                e.wait((tag, "locmax"))
                src_col = pieces[0][1]
                e.dma(cc_in[ccidx][0:1, 0:2], scal[0:1, src_col:src_col + 2], sSP)
                e.wait_sem_now(sSP)
                e.cc(cc_in[ccidx][:, :], cc_out[ccidx][:, :], sGP)
                e.wait_sem_now(sGP)
                e.dma(scal[0:1, src_col + 2:src_col + 4], cc_out[ccidx][0:1, 0:2], sSP)
                e.mark(sSP, (tag, "ccld"))

            return [("tensor", t_fn), ("vector", v_fn), ("gpsimd", g_fn)]

        with nc.psum_tensor("ps_cc0", [128, 256], F32) as ps_cc0:
            def comp0(e):
                e.tensor_scalar(sc1(4), sc1(2), INV_QMAX, 1e-12, ALU.mult, ALU.max)
                return e.reciprocal(sc1(5), sc1(4))

            run_phase(emit_cc_finalize("cc0", [(0, 0)], 0, comp0, [(5, 0)], ps_cc0), "cc0")

        sc_mx = contextlib.ExitStack()
        mx_h = sc_mx.enter_context(nc.sbuf_tensor("mx_h", [128, 8 * T], F16))
        mx_l = sc_mx.enter_context(nc.sbuf_tensor("mx_l", [128, 8 * T], F16))

        def mx3(t_):
            return t_.rearrange("p (c t) -> p c t", t=T)

        mats5 = [(wup_d, wup_t, d_U, 2), (wgate_d, wgate_t, d_G, 3)]

        with nc.sbuf_tensor("y1t5", [128, 2 * 512], F32) as y1t5, \
             nc.sbuf_tensor("mxi", [128, 2 * 512], I32) as mxi, \
             nc.sbuf_tensor("wff", [128, 2 * C], F32) as wff, \
             nc.sbuf_tensor("wts", [128, 2 * 8 * 128], F32) as wts, \
             nc.sbuf_tensor("sdq", [128, 2 * 32], F32) as sdq, \
             nc.sbuf_tensor("sTb", [32, 2 * 128], F32) as sTb, \
             nc.sbuf_tensor("srep", [128, 2 * 128], F32) as srep, \
             nc.sbuf_tensor("rsrep", [128, 2 * 128], F32) as rsrep, \
             nc.sbuf_tensor("mwt", [128, 128], I32) as mwt, \
             nc.sbuf_tensor("w16t", [128, 128], F32) as w16t, \
             nc.sbuf_tensor("wqh", [128, 2 * 8 * 128], F16) as wqh, \
             nc.sbuf_tensor("wql", [128, 2 * 8 * 128], F16) as wql, \
             nc.sbuf_tensor("ub", [128, 4 * 512], F32) as ub, \
             nc.psum_tensor("ps_ug", [128, 2 * 512], F32) as ps_ug, \
             nc.psum_tensor("ps_s5", [32, 1024], F32) as ps_s5, \
             nc.psum_tensor("ps_sr", [128, 1024], F32) as ps_sr:

            ND = 48

            def p5_sync():
                e = EY
                y13 = d_y1.rearrange("(c p) t -> p c t", p=128)
                for j in range(NJ):
                    for c in range(8):
                        fc = j * 8 + c
                        if fc >= 2:
                            e.wait(("p5mxl", (fc - 2) // 8, (fc - 2) % 8))
                        e.dma(y1t5[:, (fc % 2) * 512:(fc % 2) * 512 + 512],
                              y13[:, c, 512 * j:512 * (j + 1)], sLD)
                        e.mark(sLD, ("p5y1", j, c))
                for nb in range(ND):
                    mi, oc = divmod(nb, 24)
                    if nb >= 2:
                        e.wait(("p5s", nb - 2))
                    e.dma(wff[:, (nb % 2) * C:(nb % 2) * C + C],
                          mats5[mi][0][oc * 128:(oc + 1) * 128, :], sLD)
                    e.mark(sLD, ("p5w", nb))
                    if nb >= 2:
                        e.wait(("p5dq", nb - 2, 7))
                    e.dma(wts[:, (nb % 2) * 1024:(nb % 2) * 1024 + 1024]
                          .rearrange("p (ci o) -> p ci o", o=128),
                          mats5[mi][1].rearrange("(ci p) o -> p ci o", p=128)
                          [:, :, oc * 128:(oc + 1) * 128], sL2)
                    e.mark(sL2, ("p5wt", nb))

            def p5_scalar_all():
                e = EC
                for j in range(NJ):
                    for c in range(8):
                        fc = j * 8 + c
                        e.wait(("p5y1", j, c))
                        if fc >= 2:
                            e.wait(("p5mxl", (fc - 2) // 8, (fc - 2) % 8))
                        sl = slice((fc % 2) * 512, (fc % 2) * 512 + 512)
                        e.activation(mxi[:, sl], y1t5[:, sl], AF.Identity,
                                     scale=scbc[:, 0:1])
                        i = e.copy(mx3(mx_h[:, :])[:, c, 512 * j:512 * (j + 1)],
                                   mxi[:, sl])
                        e.inc(i, sEV, 1)
                        e.mark(sEV, ("p5mxh", j, c))
                for nb0 in range(2):
                    e.wait(("p5strd", nb0))
                    i = e.copy(sTb[0:32, (nb0 % 2) * 128:(nb0 % 2) * 128 + 128],
                               ps_s5[0:32, (nb0 % 2) * 512:(nb0 % 2) * 512 + 128])
                    e.inc(i, sEV, 1)
                    e.mark(sEV, ("p5stcp", nb0))
                for nb in range(ND):
                    for j in range(NJ):
                        fb = nb * NJ + j
                        e.wait(("p5mm", nb, j))
                        if fb >= 4:
                            e.wait(("p5sp", fb - 4))
                            e.wait(("p5am", fb - 4))
                        i = e.copy(ub[:, (fb % 4) * 512:(fb % 4) * 512 + 512],
                                   ps_ug[:, (fb % 2) * 512:(fb % 2) * 512 + 512])
                        e.inc(i, sEV, 1)
                        e.mark(sEV, ("p5ev", fb))
                    if nb + 2 < ND:
                        e.wait(("p5strd", nb + 2))
                        e.wait(("p5sr", nb, 7))
                        i = e.copy(sTb[0:32, ((nb + 2) % 2) * 128:((nb + 2) % 2) * 128 + 128],
                                   ps_s5[0:32, ((nb + 2) % 2) * 512:((nb + 2) % 2) * 512 + 128])
                        e.inc(i, sEV, 1)
                        e.mark(sEV, ("p5stcp", nb + 2))

            def p5_dequant_head(e, nb):
                e.wait(("p5w", nb))
                if nb >= 2:
                    e.wait(("p5strd", nb - 2))
                stg = wff[:, (nb % 2) * C:(nb % 2) * C + C]
                w3 = stg.rearrange("p (b i) -> p b i", i=32)
                ss_ = sdq[:, (nb % 2) * 32:(nb % 2) * 32 + 32]
                e.tensor_reduce(ss_, w3, AX.X, ALU.max, apply_absolute_value=True)
                i = e.tensor_scalar(ss_, ss_, 1.0 / 7.0, 1e-12, ALU.mult, ALU.max)
                e.inc(i, sVE, 1)
                e.mark(sVE, ("p5s", nb))

            def p5_dequant_tail(e, nb):
                e.wait(("p5wt", nb))
                for ci in range(8):
                    e.wait(("p5sr", nb, ci))
                    sl2 = slice((ci % 2) * 128, (ci % 2) * 128 + 128)
                    e.tensor_copy(srep[:, sl2], ps_sr[:, (ci % 2) * 512:(ci % 2) * 512 + 128])
                    i = e.reciprocal(rsrep[:, sl2], ps_sr[:, (ci % 2) * 512:(ci % 2) * 512 + 128])
                    e.inc(i, sVE, 1)
                    e.mark(sVE, ("p5srcp", nb, ci))
                    wt_sl = wts[:, (nb % 2) * 1024 + ci * 128:(nb % 2) * 1024 + (ci + 1) * 128]
                    e.tensor_tensor(mwt[:, :], wt_sl, rsrep[:, sl2], ALU.mult)
                    sl = slice((nb % 2) * 1024 + ci * 128, (nb % 2) * 1024 + (ci + 1) * 128)
                    e.tensor_tensor(wqh[:, sl], mwt[:, :], srep[:, sl2], ALU.mult)
                    e.tensor_tensor(w16t[:, :], mwt[:, :], srep[:, sl2], ALU.mult)
                    i = e.tensor_tensor(wql[:, sl], w16t[:, :], wqh[:, sl], ALU.subtract)
                    e.inc(i, sVE, 1)
                    e.mark(sVE, ("p5dq", nb, ci))

            def p5_vector():
                e = EV
                e.memset(amaxc[:, 2:4], 0.0)
                i = e.memset(amaxc[:, 5:7], 0.0)
                e.inc(i, sVE, 1)
                e.mark(sVE, ("p5amz",))
                for j in range(NJ):
                    for c in range(8):
                        fc = j * 8 + c
                        e.wait(("p5mxh", j, c))
                        sl = slice((fc % 2) * 512, (fc % 2) * 512 + 512)
                        i = e.tensor_tensor(mx3(mx_l[:, :])[:, c, 512 * j:512 * (j + 1)],
                                            mxi[:, sl],
                                            mx3(mx_h[:, :])[:, c, 512 * j:512 * (j + 1)],
                                            ALU.subtract)
                        e.inc(i, sVE, 1)
                        e.mark(sVE, ("p5mxl", j, c))
                p5_dequant_head(e, 0)
                p5_dequant_tail(e, 0)
                p5_dequant_head(e, 1)
                p5_dequant_tail(e, 1)
                for nb in range(ND):
                    mi, oc = divmod(nb, 24)
                    for j in range(NJ):
                        fb = nb * NJ + j
                        e.wait(("p5ev", fb))
                        e.tensor_reduce(amaxc[:, 4:5],
                                        ub[:, (fb % 4) * 512:(fb % 4) * 512 + 512],
                                        AX.X, ALU.max, apply_absolute_value=True)
                        i = e.tensor_tensor(amaxc[:, 2 + mi:3 + mi], amaxc[:, 2 + mi:3 + mi],
                                            amaxc[:, 4:5], ALU.max)
                        e.inc(i, sVE, 1)
                        e.mark(sVE, ("p5am", fb))
                    if nb + 2 < ND:
                        p5_dequant_head(e, nb + 2)
                        p5_dequant_tail(e, nb + 2)

            def p5_gpsimd():
                e = EG
                for nb in range(ND):
                    mi, oc = divmod(nb, 24)
                    ddst = mats5[mi][2]
                    for j in range(NJ):
                        fb = nb * NJ + j
                        e.wait(("p5ev", fb))
                        e.dma(ddst[oc * 128:(oc + 1) * 128, 512 * j:512 * (j + 1)],
                              ub[:, (fb % 4) * 512:(fb % 4) * 512 + 512], sSP)
                        e.mark(sSP, ("p5sp", fb))

            def p5_tensor_dq(e, nb):
                e.wait(("p5s", nb))
                i = e.transpose(ps_s5[0:32, (nb % 2) * 512:(nb % 2) * 512 + 128],
                                sdq[:, (nb % 2) * 32:(nb % 2) * 32 + 32], eye32_sb[:, :])
                e.inc(i, sMM, 1)
                e.mark(sMM, ("p5strd", nb))
                e.wait(("p5stcp", nb))
                for ci in range(8):
                    cc_ = nb * 8 + ci
                    if cc_ >= 2:
                        e.wait(("p5srcp", (cc_ - 2) // 8, (cc_ - 2) % 8))
                    i = e.matmul(ps_sr[:, (ci % 2) * 512:(ci % 2) * 512 + 128],
                                 eexpA_sb[:, ci * 128:(ci + 1) * 128],
                                 sTb[:, (nb % 2) * 128:(nb % 2) * 128 + 128],
                                 start=True, stop=True)
                    e.inc(i, sMM, 1)
                    e.mark(sMM, ("p5sr", nb, ci))

            def p5_tensor():
                e = ET
                p5_tensor_dq(e, 0)
                p5_tensor_dq(e, 1)
                for nb in range(ND):
                    for ci in range(8):
                        e.wait(("p5dq", nb, ci))
                    for j in range(NJ):
                        fb = nb * NJ + j
                        for c in range(8):
                            e.wait(("p5mxl", j, c))
                        if fb >= 2:
                            e.wait(("p5ev", fb - 2))
                        pu = ps_ug[:, (fb % 2) * 512:(fb % 2) * 512 + 512]
                        n = 0
                        for (ah, bh) in ((1, 1), (1, 0), (0, 1)):
                            for k in range(8):
                                sl = slice((nb % 2) * 1024 + k * 128,
                                           (nb % 2) * 1024 + (k + 1) * 128)
                                w = (wqh if ah else wql)[:, sl]
                                xs = mx3((mx_h if bh else mx_l)[:, :])[:, k,
                                                                       512 * j:512 * (j + 1)]
                                i = e.matmul(pu, w, xs, start=(n == 0), stop=(n == 23))
                                n += 1
                        e.inc(i, sMM, 1)
                        e.mark(sMM, ("p5mm", nb, j))
                    if nb + 2 < ND:
                        p5_tensor_dq(e, nb + 2)

            run_phase([("sync", p5_sync), ("scalar", p5_scalar_all),
                       ("vector", p5_vector), ("gpsimd", p5_gpsimd),
                       ("tensor", p5_tensor)], "p5")

        with nc.psum_tensor("ps_cc1", [128, 384], F32) as ps_cc1:
            def comp1(e):
                e.tensor_tensor(sc1(11), sc1(8), sc1(4), ALU.mult)
                e.tensor_scalar(sc1(10), sc1(11), INV_QMAX, 1e-12, ALU.mult, ALU.max)
                e.reciprocal(sc1(11), sc1(10))
                e.tensor_tensor(sc1(12), sc1(4), sc1(11), ALU.mult)
                e.tensor_tensor(sc1(11), sc1(9), sc1(4), ALU.mult)
                e.tensor_scalar(sc1(13), sc1(11), INV_QMAX, 1e-12, ALU.mult, ALU.max)
                e.reciprocal(sc1(11), sc1(13))
                e.tensor_tensor(sc1(14), sc1(4), sc1(11), ALU.mult)
                return e.tensor_scalar(sc1(15), sc1(13), 1.0 / 65536.0, None, ALU.mult)

            run_phase(emit_cc_finalize("cc1", [(2, 6), (3, 7)], 1, comp1,
                                       [(12, 1), (14, 2), (13, 3)], ps_cc1), "cc1")
        sc_mx.close()

        with nc.sbuf_tensor("u6", [128, 2 * 2048], F32) as u6, \
             nc.sbuf_tensor("g6", [128, 2 * 2048], F32) as g6, \
             nc.sbuf_tensor("mu6", [128, 2 * 2048], I32) as mu6, \
             nc.sbuf_tensor("mg6", [128, 2 * 2048], I32) as mg6, \
             nc.sbuf_tensor("sg6", [128, 2 * 2048], F32) as sg6, \
             nc.sbuf_tensor("ms6", [128, 2 * 2048], I32) as ms6, \
             nc.sbuf_tensor("p6", [128, 2 * 2048], F32) as p6:

            NJ4 = T // 2048 if T >= 2048 else 1
            W6 = min(T, 2048)
            NB6 = 24 * NJ4

            def p6a_sync():
                e = EY
                for fb in range(NB6):
                    oc, j = divmod(fb, NJ4)
                    if fb >= 2:
                        e.wait(("p6musp", fb - 2))
                        e.wait(("p6psp", fb - 2))
                    e.dma(u6[:, (fb % 2) * W6:(fb % 2) * W6 + W6],
                          d_U[oc * 128:(oc + 1) * 128, W6 * j:W6 * (j + 1)], sLD)
                    e.dma(g6[:, (fb % 2) * W6:(fb % 2) * W6 + W6],
                          d_G[oc * 128:(oc + 1) * 128, W6 * j:W6 * (j + 1)], sL2)
                    e.mark(sLD, ("p6ldu", fb))
                    e.mark(sL2, ("p6ldg", fb))

            def p6a_scalar():
                e = EC
                for fb in range(NB6):
                    e.wait(("p6ldu", fb))
                    e.wait(("p6ldg", fb))
                    if fb >= 2:
                        e.wait(("p6p", fb - 2))
                        e.wait(("p6musp", fb - 2))
                    sl = slice((fb % 2) * W6, (fb % 2) * W6 + W6)
                    e.activation(mu6[:, sl], u6[:, sl], AF.Identity, scale=scbc[:, 1:2])
                    e.activation(mg6[:, sl], g6[:, sl], AF.Identity, scale=scbc[:, 2:3])
                    i = e.activation(sg6[:, sl], mg6[:, sl], AF.Sigmoid, scale=scbc[:, 3:4])
                    e.inc(i, sEV, 1)
                    e.mark(sEV, ("p6s", fb))

            def p6a_vector():
                e = EV
                for fb in range(NB6):
                    e.wait(("p6s", fb))
                    if fb >= 2:
                        e.wait(("p6psp", fb - 2))
                    sl = slice((fb % 2) * W6, (fb % 2) * W6 + W6)
                    e.tensor_scalar(ms6[:, sl], sg6[:, sl], 65536.0, 65535.0,
                                    ALU.mult, ALU.min)
                    e.tensor_tensor(p6[:, sl], mg6[:, sl], ms6[:, sl], ALU.mult)
                    e.tensor_reduce(amaxc[:, 4:5], p6[:, sl], AX.X, ALU.max,
                                    apply_absolute_value=True)
                    i = e.tensor_tensor(amaxc[:, 5:6], amaxc[:, 5:6], amaxc[:, 4:5],
                                        ALU.max)
                    e.inc(i, sVE, 1)
                    e.mark(sVE, ("p6p", fb))

            def p6a_gpsimd():
                e = EG
                for fb in range(NB6):
                    oc, j = divmod(fb, NJ4)
                    sl = slice((fb % 2) * W6, (fb % 2) * W6 + W6)
                    e.wait(("p6s", fb))
                    e.dma(d_mu[oc * 128:(oc + 1) * 128, W6 * j:W6 * (j + 1)],
                          mu6[:, sl], sSP)
                    e.mark(sSP, ("p6musp", fb))
                    e.wait(("p6p", fb))
                    e.dma(d_P[oc * 128:(oc + 1) * 128, W6 * j:W6 * (j + 1)],
                          p6[:, sl], sS2)
                    e.mark(sS2, ("p6psp", fb))

            run_phase([("sync", p6a_sync), ("scalar", p6a_scalar),
                       ("vector", p6a_vector), ("gpsimd", p6a_gpsimd)], "p6a")

        with nc.psum_tensor("ps_cc2", [128, 256], F32) as ps_cc2:
            def comp2(e):
                e.tensor_tensor(sc1(22), sc1(19), sc1(15), ALU.mult)
                e.tensor_scalar(sc1(21), sc1(22), INV_QMAX, 1e-12, ALU.mult, ALU.max)
                e.reciprocal(sc1(22), sc1(21))
                return e.tensor_tensor(sc1(23), sc1(15), sc1(22), ALU.mult)

            run_phase(emit_cc_finalize("cc2", [(5, 17)], 2, comp2, [(23, 4)], ps_cc2), "cc2")

        with nc.sbuf_tensor("p6b", [128, 2 * 2048], F32) as p6b, \
             nc.sbuf_tensor("mu6b", [128, 2 * 2048], I32) as mu6b, \
             nc.sbuf_tensor("mg2b", [128, 2 * 2048], I32) as mg2b, \
             nc.sbuf_tensor("q6", [128, 2 * 2048], F32) as q6:

            NJ4 = T // 2048 if T >= 2048 else 1
            W6 = min(T, 2048)
            NB6 = 24 * NJ4

            def p6b_sync():
                e = EY
                for fb in range(NB6):
                    oc, j = divmod(fb, NJ4)
                    if fb >= 2:
                        e.wait(("p6qsp", fb - 2))
                    e.dma(p6b[:, (fb % 2) * W6:(fb % 2) * W6 + W6],
                          d_P[oc * 128:(oc + 1) * 128, W6 * j:W6 * (j + 1)], sLD)
                    e.dma(mu6b[:, (fb % 2) * W6:(fb % 2) * W6 + W6],
                          d_mu[oc * 128:(oc + 1) * 128, W6 * j:W6 * (j + 1)], sL2)
                    e.mark(sLD, ("p6bldp", fb))
                    e.mark(sL2, ("p6bldm", fb))

            def p6b_scalar():
                e = EC
                for fb in range(NB6):
                    e.wait(("p6bldp", fb))
                    if fb >= 2:
                        e.wait(("p6q", fb - 2))
                    sl = slice((fb % 2) * W6, (fb % 2) * W6 + W6)
                    i = e.activation(mg2b[:, sl], p6b[:, sl], AF.Identity,
                                     scale=scbc[:, 4:5])
                    e.inc(i, sEV, 1)
                    e.mark(sEV, ("p6mg2", fb))

            def p6b_vector():
                e = EV
                for fb in range(NB6):
                    e.wait(("p6mg2", fb))
                    e.wait(("p6bldm", fb))
                    if fb >= 2:
                        e.wait(("p6qsp", fb - 2))
                    sl = slice((fb % 2) * W6, (fb % 2) * W6 + W6)
                    e.tensor_tensor(q6[:, sl], mg2b[:, sl], mu6b[:, sl], ALU.mult)
                    e.tensor_reduce(amaxc[:, 4:5], q6[:, sl], AX.X, ALU.max,
                                    apply_absolute_value=True)
                    i = e.tensor_tensor(amaxc[:, 6:7], amaxc[:, 6:7], amaxc[:, 4:5],
                                        ALU.max)
                    e.inc(i, sVE, 1)
                    e.mark(sVE, ("p6q", fb))

            def p6b_gpsimd():
                e = EG
                for fb in range(NB6):
                    oc, j = divmod(fb, NJ4)
                    e.wait(("p6q", fb))
                    e.dma(d_Q[oc * 128:(oc + 1) * 128, W6 * j:W6 * (j + 1)],
                          q6[:, (fb % 2) * W6:(fb % 2) * W6 + W6], sSP)
                    e.mark(sSP, ("p6qsp", fb))

            run_phase([("sync", p6b_sync), ("scalar", p6b_scalar),
                       ("vector", p6b_vector), ("gpsimd", p6b_gpsimd)], "p6b")

        with nc.psum_tensor("ps_cc3", [128, 256], F32) as ps_cc3:
            def comp3(e):
                e.tensor_tensor(sc1(28), sc1(21), sc1(10), ALU.mult)
                e.tensor_tensor(sc1(29), sc1(26), sc1(28), ALU.mult)
                e.tensor_scalar(sc1(30), sc1(29), INV_QMAX, 1e-12, ALU.mult, ALU.max)
                e.reciprocal(sc1(31), sc1(30))
                return e.tensor_tensor(sc1(29), sc1(28), sc1(31), ALU.mult)

            run_phase(emit_cc_finalize("cc3", [(6, 24)], 0, comp3,
                                       [(29, 5), (30, 6)], ps_cc3), "cc3")

        sc_wd = contextlib.ExitStack()
        wdq_h = sc_wd.enter_context(nc.sbuf_tensor("wdq_h", [128, 8 * FF], F16))
        wdq_l = sc_wd.enter_context(nc.sbuf_tensor("wdq_l", [128, 8 * FF], F16))

        with nc.sbuf_tensor("eexpB_sb", [96, 24 * 128], F32) as eexpB_sb, \
             nc.sbuf_tensor("wff7", [128, 2 * FF], F32) as wff7, \
             nc.sbuf_tensor("wts7", [128, 2 * 24 * 128], F32) as wts7, \
             nc.sbuf_tensor("sdq7", [128, 2 * 96], F32) as sdq7, \
             nc.sbuf_tensor("sTb7", [96, 2 * 128], F32) as sTb7, \
             nc.sbuf_tensor("srep7", [128, 2 * 128], F32) as srep7, \
             nc.sbuf_tensor("rsrep7", [128, 2 * 128], F32) as rsrep7, \
             nc.sbuf_tensor("mwt7", [128, 128], I32) as mwt7, \
             nc.sbuf_tensor("w16t7", [128, 128], F32) as w16t7, \
             nc.psum_tensor("ps_s57", [96, 1024], F32) as ps_s57, \
             nc.psum_tensor("ps_sr7", [128, 1024], F32) as ps_sr7:

            def p7a_sync():
                e = EY
                e.dma(eexpB_sb[:, :], eexpB_c[:, :], sLD)
                for nb in range(8):
                    if nb >= 2:
                        e.wait(("p7s", nb - 2))
                    e.dma(wff7[:, (nb % 2) * FF:(nb % 2) * FF + FF],
                          wdown_d[nb * 128:(nb + 1) * 128, :], sLD)
                    e.mark(sLD, ("p7w", nb))
                    if nb >= 2:
                        e.wait(("p7dq", nb - 2, 23))
                    e.dma(wts7[:, (nb % 2) * 3072:(nb % 2) * 3072 + 3072]
                          .rearrange("p (ci o) -> p ci o", o=128),
                          wdown_t.rearrange("(ci p) o -> p ci o", p=128)
                          [:, :, nb * 128:(nb + 1) * 128], sL2)
                    e.mark(sL2, ("p7wt_ld", nb))

            def p7a_vector():
                e = EV
                for nb in range(8):
                    e.wait(("p7w", nb))
                    if nb >= 2:
                        e.wait(("p7strd", nb - 2))
                    stg = wff7[:, (nb % 2) * FF:(nb % 2) * FF + FF]
                    w3 = stg.rearrange("p (b i) -> p b i", i=32)
                    ss_ = sdq7[:, (nb % 2) * 96:(nb % 2) * 96 + 96]
                    e.tensor_reduce(ss_, w3, AX.X, ALU.max, apply_absolute_value=True)
                    i = e.tensor_scalar(ss_, ss_, 1.0 / 7.0, 1e-12, ALU.mult, ALU.max)
                    e.inc(i, sVE, 1)
                    e.mark(sVE, ("p7s", nb))
                    e.wait(("p7wt_ld", nb))
                    for ci in range(24):
                        e.wait(("p7sr", nb, ci))
                        sl2 = slice((ci % 2) * 128, (ci % 2) * 128 + 128)
                        e.tensor_copy(srep7[:, sl2],
                                      ps_sr7[:, (ci % 2) * 512:(ci % 2) * 512 + 128])
                        i = e.reciprocal(rsrep7[:, sl2],
                                         ps_sr7[:, (ci % 2) * 512:(ci % 2) * 512 + 128])
                        e.inc(i, sVE, 1)
                        e.mark(sVE, ("p7srcp", nb, ci))
                        wt_sl = wts7[:, (nb % 2) * 3072 + ci * 128:
                                     (nb % 2) * 3072 + (ci + 1) * 128]
                        e.tensor_tensor(mwt7[:, :], wt_sl, rsrep7[:, sl2], ALU.mult)
                        sl = slice(nb * FF + ci * 128, nb * FF + (ci + 1) * 128)
                        e.tensor_tensor(wdq_h[:, sl], mwt7[:, :], srep7[:, sl2], ALU.mult)
                        e.tensor_tensor(w16t7[:, :], mwt7[:, :], srep7[:, sl2], ALU.mult)
                        i = e.tensor_tensor(wdq_l[:, sl], w16t7[:, :], wdq_h[:, sl],
                                            ALU.subtract)
                        e.inc(i, sVE, 1)
                        e.mark(sVE, ("p7dq", nb, ci))

            def p7a_scalar():
                e = EC
                for nb in range(8):
                    e.wait(("p7strd", nb))
                    if nb >= 2:
                        e.wait(("p7sr", nb - 2, 23))
                    i = e.copy(sTb7[0:96, (nb % 2) * 128:(nb % 2) * 128 + 128],
                               ps_s57[0:96, (nb % 2) * 512:(nb % 2) * 512 + 128])
                    e.inc(i, sEV, 1)
                    e.mark(sEV, ("p7stcp", nb))

            def p7a_tensor():
                e = ET
                for nb in range(8):
                    e.wait(("p7s", nb))
                    i = e.transpose(ps_s57[0:96, (nb % 2) * 512:(nb % 2) * 512 + 128],
                                    sdq7[:, (nb % 2) * 96:(nb % 2) * 96 + 96],
                                    eye32_sb[:, :])
                    e.inc(i, sMM, 1)
                    e.mark(sMM, ("p7strd", nb))
                    e.wait(("p7stcp", nb))
                    for ci in range(24):
                        cc_ = nb * 24 + ci
                        if cc_ >= 2:
                            e.wait(("p7srcp", (cc_ - 2) // 24, (cc_ - 2) % 24))
                        i = e.matmul(ps_sr7[:, (ci % 2) * 512:(ci % 2) * 512 + 128],
                                     eexpB_sb[:, ci * 128:(ci + 1) * 128],
                                     sTb7[:, (nb % 2) * 128:(nb % 2) * 128 + 128],
                                     start=True, stop=True)
                        e.inc(i, sMM, 1)
                        e.mark(sMM, ("p7sr", nb, ci))

            run_phase([("sync", p7a_sync), ("vector", p7a_vector), ("scalar", p7a_scalar),
                       ("tensor", p7a_tensor)], "p7a")

        with nc.sbuf_tensor("q7", [128, 2 * TJ], F32) as q7, \
             nc.sbuf_tensor("moi", [128, 2 * TJ], I32) as moi, \
             nc.sbuf_tensor("mo_h", [128, 24 * TJ], F16) as mo_h, \
             nc.sbuf_tensor("mo_l", [128, 24 * TJ], F16) as mo_l, \
             nc.sbuf_tensor("y1t7", [128, 8 * TJ], F32) as y1t7, \
             nc.sbuf_tensor("pt7", [128, 2 * TJ], F32) as pt7, \
             nc.sbuf_tensor("x4b", [128, 8 * TJ], F32) as x4b, \
             nc.sbuf_tensor("sh7", [128, 8 * TJ], F32) as sh7, \
             nc.sbuf_tensor("rbc7", [128, 2 * TJ], F32) as rbc7, \
             nc.sbuf_tensor("yb", [128, 8 * TJ], F32) as yb, \
             nc.psum_tensor("ps_dn", [128, 1024], F32) as ps_dn, \
             nc.psum_tensor("ps_r7a", [128, 512], F32) as ps_r7a, \
             nc.psum_tensor("ps_r7b", [128, 512], F32) as ps_r7b:

            def p7b_sync():
                e = EY
                q3 = d_Q.rearrange("(c p) t -> p c t", p=128)
                y13 = d_y1.rearrange("(c p) t -> p c t", p=128)
                for j in range(NJ2):
                    if j >= 1:
                        e.wait(("p7x4all", j - 1))
                    e.dma(y1t7[:, :].rearrange("p (c t) -> p c t", t=TJ),
                          y13[:, :, TJ * j:TJ * (j + 1)], sLD)
                    e.mark(sLD, ("p7y1", j))
                    for ci in range(24):
                        fb = j * 24 + ci
                        if fb >= 2:
                            pf = fb - 2
                            e.wait(("p7mol", pf // 24, pf % 24))
                        e.dma(q7[:, (ci % 2) * TJ:(ci % 2) * TJ + TJ],
                              q3[:, ci, TJ * j:TJ * (j + 1)], sLD)
                        e.mark(sLD, ("p7q", j, ci))

            def p7b_scalar():
                e = EC
                for j in range(NJ2):
                    for ci in range(24):
                        e.wait(("p7q", j, ci))
                        if j * 24 + ci >= 2:
                            pf = j * 24 + ci - 2
                            e.wait(("p7mol", pf // 24, pf % 24))
                        sl = slice((ci % 2) * TJ, (ci % 2) * TJ + TJ)
                        e.activation(moi[:, sl], q7[:, sl], AF.Identity,
                                     scale=scbc[:, 5:6])
                        i = e.copy(mo_h[:, ci * TJ:(ci + 1) * TJ], moi[:, sl])
                        e.inc(i, sEV, 1)
                        e.mark(sEV, ("p7moh", j, ci))
                    for oc in range(8):
                        e.wait(("p7dn", j, oc))
                        qq = j * 8 + oc
                        if qq >= 2:
                            e.wait(("p7x4", (qq - 2) // 8, (qq - 2) % 8))
                        i = e.activation(pt7[:, (oc % 2) * TJ:(oc % 2) * TJ + TJ],
                                         ps_dn[:, (oc % 2) * 512:(oc % 2) * 512 + TJ],
                                         AF.Identity, scale=scbc[:, 6:7])
                        e.inc(i, sEV, 1)
                        e.mark(sEV, ("p7ev", j, oc))
                    e.wait(("p7x4all", j))
                    i = e.activation(sh7[:, :], x4b[:, :], AF.Square)
                    e.inc(i, sEV, 1)
                    e.mark(sEV, ("p7sq", j))
                    e.wait(("p7lnmm", j))
                    e.activation(row(R_M, TJ), ps_r7a[0:1, 0:TJ], AF.Identity,
                                 scale=1.0 / 1024.0)
                    i = e.activation(row(R_EX, TJ), ps_r7b[0:1, 0:TJ], AF.Identity,
                                     scale=1.0 / 1024.0)
                    e.inc(i, sEV, 1)
                    e.mark(sEV, ("p7rows", j))
                    e.wait(("p7var", j))
                    i = e.activation(row(R_SD, TJ), row(R_VAR, TJ), AF.Sqrt, bias=cbias_sb[0:1, 0:1])
                    e.inc(i, sEV, 1)
                    e.mark(sEV, ("p7sd", j))

            def p7b_vector():
                e = EV
                for j in range(NJ2):
                    for ci in range(24):
                        e.wait(("p7moh", j, ci))
                        sl = slice((ci % 2) * TJ, (ci % 2) * TJ + TJ)
                        i = e.tensor_tensor(mo_l[:, ci * TJ:(ci + 1) * TJ],
                                            moi[:, sl], mo_h[:, ci * TJ:(ci + 1) * TJ],
                                            ALU.subtract)
                        e.inc(i, sVE, 1)
                        e.mark(sVE, ("p7mol", j, ci))
                    for oc in range(8):
                        e.wait(("p7ev", j, oc))
                        e.wait(("p7y1", j))
                        i = e.tensor_tensor(x4b[:, oc * TJ:(oc + 1) * TJ],
                                            pt7[:, (oc % 2) * TJ:(oc % 2) * TJ + TJ],
                                            y1t7[:, oc * TJ:(oc + 1) * TJ], ALU.add)
                        e.inc(i, sVE, 1)
                        e.mark(sVE, ("p7x4", j, oc))
                    e.mark(sVE, ("p7x4all", j))
                    e.wait(("p7rows", j))
                    e.tensor_tensor(row(R_VAR, TJ), row(R_M, TJ), row(R_M, TJ), ALU.mult)
                    i = e.tensor_tensor(row(R_VAR, TJ), row(R_EX, TJ), row(R_VAR, TJ),
                                        ALU.subtract)
                    e.inc(i, sVE, 1)
                    e.mark(sVE, ("p7var", j))
                    e.wait(("p7sd", j))
                    e.reciprocal(row(R_R, TJ), row(R_SD, TJ))
                    i = e.tensor_tensor(row(R_MR, TJ), row(R_M, TJ), row(R_R, TJ),
                                        ALU.mult)
                    e.inc(i, sVE, 1)
                    e.mark(sVE, ("p7lnr", j))
                    e.wait(("p7lnbc", j))
                    e.tensor_copy(rbc7[:, 0:TJ], ps_r7a[:, 0:TJ])
                    i = e.tensor_copy(rbc7[:, TJ:2 * TJ], ps_r7b[:, 0:TJ])
                    e.inc(i, sVE, 1)
                    e.mark(sVE, ("p7lncp", j))
                    if j >= 1:
                        e.wait(("p7ysp", j - 1))
                    rb = rbc7[:, 0:TJ].unsqueeze(1).broadcast_to([128, 8, TJ])
                    mrb = rbc7[:, TJ:2 * TJ].unsqueeze(1).broadcast_to([128, 8, TJ])
                    x5 = x4b[:, :].rearrange("p (c t) -> p c t", t=TJ)
                    y5 = yb[:, :].rearrange("p (c t) -> p c t", t=TJ)
                    e.tensor_tensor(y5, x5, rb, ALU.mult)
                    i = e.tensor_tensor(y5, y5, mrb, ALU.subtract)
                    e.inc(i, sVE, 1)
                    e.mark(sVE, ("p7y", j))

            def p7b_tensor():
                e = ET
                for j in range(NJ2):
                    for ci in range(24):
                        e.wait(("p7mol", j, ci))
                    for oc in range(8):
                        qq = j * 8 + oc
                        if qq >= 2:
                            e.wait(("p7ev", (qq - 2) // 8, (qq - 2) % 8))
                        pd = ps_dn[:, (oc % 2) * 512:(oc % 2) * 512 + TJ]
                        n = 0
                        for (ah, bh) in ((1, 1), (1, 0), (0, 1)):
                            for k in range(24):
                                w = (wdq_h if ah else wdq_l)[:, oc * FF + k * 128:
                                                             oc * FF + (k + 1) * 128]
                                xs = (mo_h if bh else mo_l)[:, k * TJ:(k + 1) * TJ]
                                i = e.matmul(pd, w, xs, start=(n == 0), stop=(n == 71))
                                n += 1
                        e.inc(i, sMM, 1)
                        e.mark(sMM, ("p7dn", j, oc))
                    e.wait(("p7sq", j))
                    if j >= 1:
                        e.wait(("p7lncp", j - 1))
                    e.wait(("p7x4all", j))
                    for k in range(8):
                        e.matmul(ps_r7a[0:1, 0:TJ], onesc_sb[:, :],
                                 x4b[:, k * TJ:(k + 1) * TJ], start=(k == 0), stop=(k == 7))
                        i = e.matmul(ps_r7b[0:1, 0:TJ], onesc_sb[:, :],
                                     sh7[:, k * TJ:(k + 1) * TJ], start=(k == 0),
                                     stop=(k == 7))
                    e.inc(i, sMM, 1)
                    e.mark(sMM, ("p7lnmm", j))
                    e.wait(("p7lnr", j))
                    e.matmul(ps_r7a[:, 0:TJ], onesr_sb[0:1, :], row(R_R, TJ),
                             start=True, stop=True)
                    i = e.matmul(ps_r7b[:, 0:TJ], onesr_sb[0:1, :], row(R_MR, TJ),
                                 start=True, stop=True)
                    e.inc(i, sMM, 1)
                    e.mark(sMM, ("p7lnbc", j))

            def p7b_gpsimd():
                e = EG
                for j in range(NJ2):
                    e.wait(("p7y", j))
                    e.dma(y_out.rearrange("(c p) t -> p c t", p=128)
                          [:, :, TJ * j:TJ * (j + 1)],
                          yb[:, :].rearrange("p (c t) -> p c t", t=TJ), sSP)
                    e.mark(sSP, ("p7ysp", j))
                e.wait_sem_now(sSP)

            run_phase([("sync", p7b_sync), ("scalar", p7b_scalar), ("vector", p7b_vector),
                       ("tensor", p7b_tensor), ("gpsimd", p7b_gpsimd)], "p7b")

        sc_wd.close()

    return nc


_PROG_CACHE = {}


def _get_prog(T):
    if T not in _PROG_CACHE:
        _PROG_CACHE[T] = build_program(T)
    return _PROG_CACHE[T]


def make_inmaps(inputs, T):
    x = np.ascontiguousarray(np.asarray(inputs["L0_attn_x"]))
    B = x.shape[0]
    wqkv = np.ascontiguousarray(np.asarray(inputs["L0_attn_qkv_w"]).T)
    wproj = np.ascontiguousarray(np.asarray(inputs["L0_attn_proj_w"]).T)
    wc1 = np.ascontiguousarray(np.asarray(inputs["L0_attn_conv1_w"]).T)
    wc2 = np.ascontiguousarray(np.asarray(inputs["L0_attn_conv2_w"]).T)
    wup = np.ascontiguousarray(np.asarray(inputs["L0_ffn_w_gate" if False else "L0_ffn_w_up"]))
    wgate = np.ascontiguousarray(np.asarray(inputs["L0_ffn_w_gate"]))
    wdown = np.ascontiguousarray(np.asarray(inputs["L0_ffn_w_down"]))
    vecs = np.zeros((6, C), np.float32)
    vecs[0] = np.asarray(inputs["L0_attn_scale1"])
    vecs[1] = np.asarray(inputs["L0_attn_scale2"])
    vecs[2] = np.asarray(inputs["L0_attn_gamma"])
    vecs[3] = np.asarray(inputs["L0_attn_proj_b"])
    vecs[4] = np.asarray(inputs["L0_attn_conv2_b"])
    vecs = np.ascontiguousarray(vecs.reshape(6, 8, 128).transpose(2, 0, 1).reshape(128, 48))
    c1b = np.asarray(inputs["L0_attn_conv1_b"], dtype=np.float32)
    c1b = np.ascontiguousarray(c1b.reshape(16, 128).T)
    shared = dict(wqkv=wqkv, wproj=wproj, wc1=wc1, wc2=wc2, wup=wup,
                  wgate=wgate, wdown=wdown, vecs=vecs, c1b=c1b,
                  wup_t=np.ascontiguousarray(wup.T),
                  wgate_t=np.ascontiguousarray(wgate.T),
                  wdown_t=np.ascontiguousarray(wdown.T))
    in_maps = []
    for b in range(B):
        m = dict(shared)
        m["x"] = np.ascontiguousarray(x[b].T)
        in_maps.append(m)
    return in_maps


def kernel(**inputs):
    T = int(np.asarray(inputs["L0_attn_x"]).shape[1])
    nc = _get_prog(T)
    in_maps = make_inmaps(inputs, T)
    res = run_bass_kernel_spmd(nc, in_maps, core_ids=list(range(8)))
    out = np.stack([np.asarray(r["y"]).T for r in res.results], axis=0)
    return out.astype(np.float32)


# version 05 (final)
# speedup vs baseline: 2.0858x; 1.0189x over previous
import contextlib
import os
import numpy as np
import concourse.bass as bass
import concourse.mybir as mybir
from concourse.bass_utils import run_bass_kernel_spmd
from concourse.bass_types import AP

F32 = mybir.dt.float32
F16 = mybir.dt.float16
I32 = mybir.dt.int32
AF = mybir.ActivationFunctionType
ALU = mybir.AluOpType
AX = mybir.AxisListType

C = 1024
NH = 16
HID = 2048
FF = 3072
INV_QMAX = 1.0 / 32767.0
RMS_K = float(1024 ** 0.5 + 1e-06)
DEBUG = False

_ENG_OPS = {
    "tensor_copy", "tensor_tensor", "tensor_add", "tensor_sub", "tensor_mul",
    "tensor_max", "tensor_scalar", "tensor_single_scalar", "tensor_scalar_add",
    "tensor_scalar_sub", "tensor_scalar_mul", "tensor_scalar_max",
    "tensor_scalar_min", "tensor_reduce", "reduce_max", "reduce_sum",
    "scalar_tensor_tensor", "select", "transpose", "bn_stats", "bn_aggr",
    "reciprocal", "memset", "memzero", "activation", "copy", "add", "mul",
    "sqrt", "square", "sign", "iota", "matmul", "collective_compute",
}


def _ap_names(args, kwargs):
    ins, outs = [], []

    def walk(v, into):
        if isinstance(v, AP):
            into.append(v.tensor.name)
        elif isinstance(v, (list, tuple)):
            for x in v:
                walk(x, into)

    for i, v in enumerate(args):
        walk(v, ins if i > 0 else outs)
    for k, v in kwargs.items():
        walk(v, outs if k in ("out", "accum_out", "outs") else ins)
    return ins, outs


class Sem:
    def __init__(self, h, name):
        self.h = h
        self.name = name
        self.val = 0


class Eng:

    def __init__(self, eng, ctx, no_hazard=False):
        self._eng = eng
        self._ctx = ctx
        self._no_hazard = no_hazard
        self._touched = set()
        self._written = set()

    @property
    def counting(self):
        return self._ctx["counting"]

    def drain(self):
        if not self.counting:
            self._eng.drain()
        self._touched = set()
        self._written = set()

    def fence(self):
        self._touched = set()
        self._written = set()

    def mark(self, sem, key):
        if self.counting:
            assert key not in self._ctx["marks"], f"duplicate mark {key}"
            self._ctx["marks"][key] = (sem, sem.val)
        else:
            rec = self._ctx["marks"].get(key)
            assert rec is not None and rec[0] is sem and rec[1] == sem.val, \
                f"mark drift at {key}: recorded {rec and rec[1]} vs emit {sem.val}"

    def inc(self, inst, sem, n=1):
        if not self.counting:
            if self._no_hazard:
                inst = self._eng.drain()
            inst.then_inc(sem.h, n)
        sem.val += n

    def wait(self, key):
        if self.counting:
            return
        sem, v = self._ctx["marks"][key]
        self._eng.wait_ge(sem.h, v)

    def wait_sem_now(self, sem):
        if self.counting:
            return
        self._eng.wait_ge(sem.h, sem.val)

    def dma(self, out, in_, sem):
        if self.counting:
            sem.val += 16
            return None
        self._eng.wait_ge(sem.h, sem.val)
        inst = self._eng.dma_start(out=out, in_=in_)
        inst.then_inc(sem.h, 16)
        sem.val += 16
        return inst

    def cc(self, in_ap, out_ap, sem):
        if self.counting:
            sem.val += 1
            return None
        inst = self._eng.collective_compute(
            "AllReduce", ALU.max, replica_groups=[list(range(8))],
            ins=[in_ap], outs=[out_ap])
        inst.then_inc(sem.h, 1)
        sem.val += 1
        return inst

    def __getattr__(self, name):
        if name.startswith("_"):
            raise AttributeError(name)
        if name not in _ENG_OPS:
            fn = getattr(self._eng, name)
            if self.counting:
                return lambda *a, **k: None
            return fn

        def wrapped(*args, **kwargs):
            if self.counting:
                return None
            if not self._no_hazard:
                ins, outs = _ap_names(args, kwargs)
                if any(n in self._written for n in ins) or any(n in self._touched for n in outs):
                    self.drain()
                r = getattr(self._eng, name)(*args, **kwargs)
                self._touched.update(ins + outs)
                self._written.update(outs)
                return r
            return getattr(self._eng, name)(*args, **kwargs)

        return wrapped


R_SQ, R_RC, R_RK, R_M, R_EX = 0, 512, 1024, 1536, 2048
R_VAR, R_SD, R_R, R_MR = R_SQ, R_RC, R_RK, R_EX


def build_program(T):
    NJ = T // 512
    NJ2 = T // 256
    TJ = 256
    nc = bass.Bass("TRN2", target_bir_lowering=False, debug=False)
    scratch_kind = "ExternalOutput" if DEBUG else "Internal"

    x_in = nc.dram_tensor("x", [C, T], F32, kind="ExternalInput").ap()
    wqkv_d = nc.dram_tensor("wqkv", [C, 3 * C], F32, kind="ExternalInput").ap()
    wproj_d = nc.dram_tensor("wproj", [C, C], F32, kind="ExternalInput").ap()
    wc1_d = nc.dram_tensor("wc1", [C, HID], F32, kind="ExternalInput").ap()
    wc2_d = nc.dram_tensor("wc2", [HID, C], F32, kind="ExternalInput").ap()
    wup_d = nc.dram_tensor("wup", [FF, C], F32, kind="ExternalInput").ap()
    wgate_d = nc.dram_tensor("wgate", [FF, C], F32, kind="ExternalInput").ap()
    wdown_d = nc.dram_tensor("wdown", [C, FF], F32, kind="ExternalInput").ap()
    wup_t = nc.dram_tensor("wup_t", [C, FF], F32, kind="ExternalInput").ap()
    wgate_t = nc.dram_tensor("wgate_t", [C, FF], F32, kind="ExternalInput").ap()
    wdown_t = nc.dram_tensor("wdown_t", [FF, C], F32, kind="ExternalInput").ap()
    vecs_d = nc.dram_tensor("vecs", [128, 48], F32, kind="ExternalInput").ap()
    c1b_d = nc.dram_tensor("c1b", [128, 16], F32, kind="ExternalInput").ap()
    y_out = nc.dram_tensor("y", [C, T], F32, kind="ExternalOutput").ap()

    def scratch(name, shape, dt):
        return nc.dram_tensor(name, shape, dt, kind=scratch_kind).ap()

    d_phiq = scratch("d_phiq", [C, T], F32)
    d_x2 = scratch("d_x2", [C, T], F32)
    d_y1 = scratch("d_y1", [C, T], F32)
    d_U = scratch("d_U", [FF, T], F32)
    d_G = scratch("d_G", [FF, T], F32)
    d_mu = scratch("d_mu", [FF, T], I32)
    d_P = scratch("d_P", [FF, T], F32)
    d_Q = scratch("d_Q", [FF, T], F32)
    cc_in = [nc.dram_tensor(f"cc_in{i}", [1, 2], F32).ap() for i in range(3)]
    cc_out = [nc.dram_tensor(f"cc_out{i}", [1, 2], F32).ap() for i in range(3)]

    eye32_c = nc.inline_tensor(np.eye(128, dtype=np.float32), name="eye32c").ap()
    eye16_c = nc.inline_tensor(np.eye(128, dtype=np.float16), name="eye16c").ap()
    onesr_c = nc.inline_tensor(np.ones((1, 128), np.float32), name="onesrc").ap()
    onesc_c = nc.inline_tensor(np.ones((128, 1), np.float32), name="onescc").ap()
    cbias_c = nc.inline_tensor(np.full((128, 1), 1e-5, np.float32), name="cbiasc").ap()
    eA = np.zeros((32, 8 * 128), np.float32)
    for ci in range(8):
        for p in range(128):
            eA[4 * ci + p // 32, ci * 128 + p] = 1.0
    eexpA_c = nc.inline_tensor(eA, name="eexpAc").ap()
    eB = np.zeros((96, 24 * 128), np.float32)
    for ci in range(24):
        for p in range(128):
            eB[4 * ci + p // 32, ci * 128 + p] = 1.0
    eexpB_c = nc.inline_tensor(eB, name="eexpBc").ap()

    es = contextlib.ExitStack()

    def sbuf(name, shape, dt):
        return es.enter_context(nc.sbuf_tensor(name, shape, dt))

    def mksem(name):
        return Sem(es.enter_context(nc.semaphore(name)), name)

    with es, nc.Block() as block:
        vec_sb = sbuf("vec_sb", [128, 48], F32)
        gpb_sb = sbuf("gpb_sb", [128, 16], F32)
        c1b_sb = sbuf("c1b_sb", [128, 16], F32)
        eye32_sb = sbuf("eye32_sb", [128, 128], F32)
        eye16_sb = sbuf("eye16_sb", [128, 128], F16)
        onesr_sb = sbuf("onesr_sb", [1, 128], F32)
        onesc_sb = sbuf("onesc_sb", [128, 1], F32)
        eexpA_sb = sbuf("eexpA_sb", [32, 8 * 128], F32)
        cbias_sb = sbuf("cbias_sb", [128, 1], F32)
        kvext = sbuf("kvext", [128, 8 * 65], F32)
        scal = sbuf("scal", [1, 32], F32)
        scbc = sbuf("scbc", [128, 8], F32)
        rowb = sbuf("rowb", [1, 5 * 512], F32)
        amaxc = sbuf("amaxc", [128, 8], F32)

        sLD = mksem("sLD")
        sL2 = mksem("sL2")
        sSP = mksem("sSP")
        sS2 = mksem("sS2")
        sMM = mksem("sMM")
        sEV = mksem("sEV")
        sVE = mksem("sVE")
        sGP = mksem("sGP")
        sPB = mksem("sPB")
        ALL_SEMS = [sLD, sL2, sSP, sS2, sMM, sEV, sVE, sGP, sPB]

        ctx = {"counting": False, "marks": {}}
        EV = Eng(nc.vector, ctx)
        EC = Eng(nc.scalar, ctx)
        EG = Eng(nc.gpsimd, ctx)
        ET = Eng(nc.tensor, ctx, no_hazard=True)
        EY = Eng(nc.sync, ctx)

        def vsl(row, ci):
            return vec_sb[:, row * 8 + ci: row * 8 + ci + 1]

        def row(slot, n=512):
            return rowb[0:1, slot:slot + n]

        def sc1(i):
            return scal[0:1, i:i + 1]

        _phase_no = [0]

        def run_phase(fns, label=None):
            label = label or f"ph{_phase_no[0]}"
            _phase_no[0] += 1
            nh = os.environ.get("NH_PHASES")
            if nh is not None:
                ET._no_hazard = label in nh.split(",")
            snap = {s.name: s.val for s in ALL_SEMS}
            ctx["counting"] = True
            for _, fn in fns:
                fn()
                for e in (EV, EC, EG, ET, EY):
                    e.fence()
            for s in ALL_SEMS:
                s.val = snap[s.name]
            ctx["counting"] = False
            deco = {"sync": block.sync, "vector": block.vector, "scalar": block.scalar,
                    "gpsimd": block.gpsimd, "tensor": block.tensor}
            with nc.named_scope(label):
                for name, fn in fns:
                    deco[name](lambda e, fn=fn: fn())
            tgt = sPB.val + 5
            for dec, eng, qsems in ((block.sync, nc.sync, (sLD, sL2)),
                                    (block.vector, nc.vector, ()),
                                    (block.scalar, nc.scalar, ()),
                                    (block.gpsimd, nc.gpsimd, (sSP, sS2)),
                                    (block.tensor, nc.tensor, ())):
                def bfn(e, eng=eng, tgt=tgt, qsems=qsems):
                    for q in qsems:
                        eng.wait_ge(q.h, q.val)
                    if eng is not nc.sync:
                        eng.drain()
                    eng.nop().then_inc(sPB.h, 1)
                    eng.wait_ge(sPB.h, tgt)
                dec(bfn)
            sPB.val = tgt
            for e in (EV, EC, EG, ET, EY):
                e.fence()
            ctx["marks"] = {}

        sc_qkv = contextlib.ExitStack()
        wqkv_h = sc_qkv.enter_context(nc.sbuf_tensor("wqkv_h", [128, 8 * 3 * C], F16))
        wqkv_l = sc_qkv.enter_context(nc.sbuf_tensor("wqkv_l", [128, 8 * 3 * C], F16))
        def emit_wprep(tag, plan, wstg, wtmp, stg_cols, after_consts):
            items = []
            for (src, nch, ncols, fold, dh, dl) in plan:
                for ci in range(nch):
                    items.append((src, ci, ncols, fold, dh, dl))

            def w_sync():
                e = EY
                for nb, (src, ci, ncols, fold, dh, dl) in enumerate(items):
                    if nb >= 2:
                        e.wait((tag, "lo", nb - 2))
                    e.dma(wstg[:, (nb % 2) * stg_cols:(nb % 2) * stg_cols + ncols],
                          src[ci * 128:(ci + 1) * 128, :], sLD)
                    e.mark(sLD, (tag, "w", nb))

            def w_scalar():
                e = EC
                if after_consts:
                    e.wait("p0consts")
                for nb, (src, ci, ncols, fold, dh, dl) in enumerate(items):
                    e.wait((tag, "w", nb))
                    if nb >= 1:
                        e.wait((tag, "lo", nb - 1))
                    stg = wstg[:, (nb % 2) * stg_cols:(nb % 2) * stg_cols + ncols]
                    sc = vsl(fold, ci) if fold is not None else 1.0
                    e.activation(dh[:, ci * ncols:(ci + 1) * ncols], stg,
                                 AF.Identity, scale=sc)
                    i = e.activation(wtmp[:, 0:ncols], stg, AF.Identity, scale=sc)
                    e.inc(i, sEV, 1)
                    e.mark(sEV, (tag, "hi", nb))

            def w_vector():
                e = EV
                for nb, (src, ci, ncols, fold, dh, dl) in enumerate(items):
                    e.wait((tag, "hi", nb))
                    i = e.tensor_tensor(dl[:, ci * ncols:(ci + 1) * ncols],
                                        wtmp[:, 0:ncols],
                                        dh[:, ci * ncols:(ci + 1) * ncols],
                                        ALU.subtract)
                    e.inc(i, sVE, 1)
                    e.mark(sVE, (tag, "lo", nb))

            return w_sync, w_scalar, w_vector

        with nc.sbuf_tensor("wstg", [128, 2 * 3 * C], F32) as wstg, \
             nc.sbuf_tensor("wtmp", [128, 3 * C], F32) as wtmp:

            ws0, wc0, wv0 = emit_wprep(
                "p0", [(wqkv_d, 8, 3 * C, 0, wqkv_h, wqkv_l)],
                wstg, wtmp, 3 * C, True)

            def p0_sync():
                e = EY
                e.dma(eye32_sb[:, :], eye32_c[:, :], sLD)
                e.dma(eye16_sb[:, :], eye16_c[:, :], sLD)
                e.dma(onesr_sb[:, :], onesr_c[:, :], sLD)
                e.dma(onesc_sb[:, :], onesc_c[:, :], sLD)
                e.dma(eexpA_sb[:, :], eexpA_c[:, :], sLD)
                e.dma(cbias_sb[:, :], cbias_c[:, :], sLD)
                e.dma(vec_sb[:, :], vecs_d[:, :], sLD)
                e.dma(c1b_sb[:, :], c1b_d[:, :], sLD)
                e.mark(sLD, "p0consts")
                ws0()

            def p0_vector():
                e = EV
                e.wait("p0consts")
                e.tensor_tensor(gpb_sb[:, 0:8], vec_sb[:, 16:24], vec_sb[:, 24:32], ALU.mult)
                e.tensor_tensor(gpb_sb[:, 8:16], vec_sb[:, 16:24], vec_sb[:, 32:40], ALU.mult)
                wv0()

            run_phase([("sync", p0_sync), ("scalar", wc0), ("vector", p0_vector)], "p0")

        with nc.sbuf_tensor("xbuf", [128, 8 * 512], F32) as xbuf, \
             nc.sbuf_tensor("sh1", [128, 8 * 512], F32) as sh1, \
             nc.sbuf_tensor("r1bc", [128, 512], F32) as r1bc, \
             nc.sbuf_tensor("xnh", [128, 8 * 512], F16) as xnh, \
             nc.sbuf_tensor("xnl", [128, 8 * 512], F16) as xnl, \
             nc.sbuf_tensor("phib", [128, 2 * 512], F32) as phib, \
             nc.sbuf_tensor("k_sb", [128, 1024], F32) as k_sb, \
             nc.sbuf_tensor("v_sb", [128, 1024], F32) as v_sb, \
             nc.sbuf_tensor("qtmp", [128, 512], F32) as qtmp, \
             nc.psum_tensor("ps_misc", [128, 512], F32) as ps_misc, \
             nc.psum_tensor("ps_q", [128, 1024], F32) as ps_q, \
             nc.psum_tensor("ps_kv", [128, 1024], F32) as ps_kv, \
             nc.psum_tensor("ps_acc", [128, 1024], F32) as ps_acc, \
             nc.psum_tensor("ps_ks", [128, 16], F32) as ps_ks:

            def wq_sl(hi, k, a, b):
                w = wqkv_h if hi else wqkv_l
                return w[:, k * 3 * C + a:k * 3 * C + b]

            def p1_sync():
                e = EY
                x3 = x_in.rearrange("(c p) t -> p c t", p=128)
                for j in range(NJ):
                    if j >= 1:
                        e.wait(("p1xn", j - 1))
                    e.dma(xbuf[:, :].rearrange("p (c t) -> p c t", t=512),
                          x3[:, :, 512 * j:512 * (j + 1)], sLD)
                    e.mark(sLD, ("p1x", j))

            def p1_scalar():
                e = EC
                for j in range(NJ):
                    e.wait(("p1x", j))
                    if j >= 1:
                        e.wait(("p1xn", j - 1))
                    i = e.activation(sh1[:, :], xbuf[:, :], AF.Square)
                    e.inc(i, sEV, 1)
                    e.mark(sEV, ("p1sq", j))
                    e.wait(("p1ss", j))
                    i = e.activation(row(R_SQ), ps_misc[0:1, 0:512], AF.Sqrt)
                    e.inc(i, sEV, 1)
                    e.mark(sEV, ("p1sqrt", j))
                    for co in range(8):
                        e.wait(("p1mmq", j, co))
                        gg = j * 8 + co
                        if gg >= 2:
                            e.wait(("p1phisp", (gg - 2) // 8, (gg - 2) % 8))
                        pq = ps_q[:, (co % 2) * 512:(co % 2) * 512 + 512]
                        e.activation(qtmp[:, :], pq, AF.Square)
                        i = e.activation(phib[:, (co % 2) * 512:(co % 2) * 512 + 512],
                                         qtmp[:, :], AF.Exp, scale=-0.5)
                        e.inc(i, sEV, 1)
                        e.mark(sEV, ("p1phi", j, co))
                    for tt in range(4):
                        for fo in range(4):
                            e.wait(("p1mmkv", j, tt, fo))
                            if fo == 0 and (tt > 0 or j > 0):
                                e.wait(("p1kvb", j if tt > 0 else j - 1,
                                        tt - 1 if tt > 0 else 3))
                            pkv = ps_kv[:, (fo % 2) * 512:(fo % 2) * 512 + 512]
                            if fo < 2:
                                e.activation(qtmp[:, :], pkv, AF.Square)
                                i = e.activation(k_sb[:, fo * 512:(fo + 1) * 512],
                                                 qtmp[:, :], AF.Exp, scale=-0.5)
                            else:
                                i = e.copy(v_sb[:, (fo - 2) * 512:(fo - 1) * 512], pkv)
                            e.inc(i, sEV, 1)
                            e.mark(sEV, ("p1kvev", j, tt, fo))

            def p1_vector():
                e = EV
                for j in range(NJ):
                    e.wait(("p1sqrt", j))
                    e.reciprocal(row(R_RC), row(R_SQ))
                    i = e.tensor_scalar(row(R_RK), row(R_RC), RMS_K, None, ALU.mult)
                    e.inc(i, sVE, 1)
                    e.mark(sVE, ("p1r1row", j))
                    e.wait(("p1r1bcmm", j))
                    i = e.tensor_copy(r1bc[:, :], ps_misc[:, :])
                    e.inc(i, sVE, 1)
                    e.mark(sVE, ("p1r1bc", j))
                    r1b3 = r1bc[:, :].unsqueeze(1).broadcast_to([128, 8, 512])
                    x4 = xbuf[:, :].rearrange("p (c t) -> p c t", t=512)
                    e.tensor_tensor(xnh[:, :].rearrange("p (c t) -> p c t", t=512),
                                    x4, r1b3, ALU.mult)
                    e.tensor_tensor(sh1[:, :].rearrange("p (c t) -> p c t", t=512),
                                    x4, r1b3, ALU.mult)
                    i = e.tensor_tensor(xnl[:, :], sh1[:, :], xnh[:, :], ALU.subtract)
                    e.inc(i, sVE, 1)
                    e.mark(sVE, ("p1xn", j))

            def p1_tensor():
                e = ET
                for j in range(NJ):
                    e.wait(("p1sq", j))
                    if j >= 1:
                        e.wait(("p1r1bc", j - 1))
                    for k in range(8):
                        i = e.matmul(ps_misc[0:1, 0:512], onesc_sb[:, :],
                                     sh1[:, k * 512:(k + 1) * 512],
                                     start=(k == 0), stop=(k == 7))
                    e.inc(i, sMM, 1)
                    e.mark(sMM, ("p1ss", j))
                    e.wait(("p1sqrt", j))
                    e.wait(("p1r1row", j))
                    i = e.matmul(ps_misc[:, :], onesr_sb[0:1, :], row(R_RK),
                                 start=True, stop=True)
                    e.inc(i, sMM, 1)
                    e.mark(sMM, ("p1r1bcmm", j))
                    e.wait(("p1xn", j))
                    for co in range(8):
                        gg = j * 8 + co
                        if gg >= 2:
                            e.wait(("p1phi", (gg - 2) // 8, (gg - 2) % 8))
                        pq = ps_q[:, (co % 2) * 512:(co % 2) * 512 + 512]
                        n = 0
                        for (ah, bh) in ((1, 1), (1, 0), (0, 1)):
                            for k in range(8):
                                xs = (xnh if bh else xnl)[:, k * 512:(k + 1) * 512]
                                i = e.matmul(pq, wq_sl(ah, k, co * 128, (co + 1) * 128),
                                             xs, start=(n == 0), stop=(n == 23))
                                n += 1
                        e.inc(i, sMM, 1)
                        e.mark(sMM, ("p1mmq", j, co))
                    for tt in range(4):
                        for fo in range(4):
                            ff = j * 16 + tt * 4 + fo
                            if ff >= 2:
                                pf = ff - 2
                                e.wait(("p1kvev", pf // 16, (pf % 16) // 4, (pf % 16) % 4))
                            pkv = ps_kv[:, (fo % 2) * 512:(fo % 2) * 512 + 512]
                            n = 0
                            for (ah, bh) in ((1, 1), (1, 0), (0, 1)):
                                for k in range(8):
                                    xs = (xnh if ah else xnl)[:, k * 512 + tt * 128:
                                                              k * 512 + tt * 128 + 128]
                                    ws_ = wq_sl(bh, k, C + fo * 512, C + (fo + 1) * 512)
                                    i = e.matmul(pkv, xs, ws_, start=(n == 0), stop=(n == 23))
                                    n += 1
                            e.inc(i, sMM, 1)
                            e.mark(sMM, ("p1mmkv", j, tt, fo))
                        for fo in range(4):
                            e.wait(("p1kvev", j, tt, fo))
                        first = (j == 0 and tt == 0)
                        last = (j == NJ - 1 and tt == 3)
                        for p in range(8):
                            e.matmul(ps_acc[:, p * 128:(p + 1) * 128],
                                     k_sb[:, p * 128:(p + 1) * 128],
                                     v_sb[:, p * 128:(p + 1) * 128],
                                     start=(first and p % 4 == 0),
                                     stop=(last and p % 4 == 3))
                            i = e.matmul(ps_ks[:, p:p + 1],
                                         k_sb[:, p * 128:(p + 1) * 128],
                                         onesc_sb[:, :], start=(first and p == 0),
                                         stop=(last and p == 7))
                        e.inc(i, sMM, 1)
                        e.mark(sMM, ("p1kvb", j, tt))

            def p1_gpsimd():
                e = EG
                for j in range(NJ):
                    for co in range(8):
                        e.wait(("p1phi", j, co))
                        e.dma(d_phiq[co * 128:(co + 1) * 128, 512 * j:512 * (j + 1)],
                              phib[:, (co % 2) * 512:(co % 2) * 512 + 512], sSP)
                        e.mark(sSP, ("p1phisp", j, co))

            run_phase([("sync", p1_sync), ("scalar", p1_scalar), ("vector", p1_vector),
                       ("tensor", p1_tensor), ("gpsimd", p1_gpsimd)], "p1")

            def p15_vector():
                e = EV
                for p in range(8):
                    e.tensor_copy(kvext[0:64, p * 65:p * 65 + 64],
                                  ps_acc[0:64, p * 128:p * 128 + 64])
                    e.tensor_copy(kvext[64:128, p * 65:p * 65 + 64],
                                  ps_acc[64:128, p * 128 + 64:p * 128 + 128])
                    e.tensor_copy(kvext[:, p * 65 + 64:p * 65 + 65], ps_ks[:, p:p + 1])

            run_phase([("vector", p15_vector)], "p15")

        sc_qkv.close()

        sc_proj = contextlib.ExitStack()
        wproj_h = sc_proj.enter_context(nc.sbuf_tensor("wproj_h", [128, 8 * C], F16))
        wproj_l = sc_proj.enter_context(nc.sbuf_tensor("wproj_l", [128, 8 * C], F16))
        with nc.sbuf_tensor("wstgp", [128, 2 * C], F32) as wstgp, \
             nc.sbuf_tensor("wtmpp", [128, C], F32) as wtmpp:
            pws, pwc, pwv = emit_wprep(
                "p2p", [(wproj_d, 8, C, None, wproj_h, wproj_l)],
                wstgp, wtmpp, C, False)
            run_phase([("sync", pws), ("scalar", pwc), ("vector", pwv)], "p2pre")

        with nc.sbuf_tensor("qbuf", [128, 8 * 512], F32) as qbuf, \
             nc.sbuf_tensor("atile", [128, 1024], F32) as atile, \
             nc.sbuf_tensor("zt", [128, 16], F32) as zt, \
             nc.sbuf_tensor("athi", [128, 8 * 512], F16) as athi, \
             nc.sbuf_tensor("atlo", [128, 8 * 512], F16) as atlo, \
             nc.sbuf_tensor("xck", [128, 2 * 512], F32) as xck, \
             nc.sbuf_tensor("ptmp", [128, 2 * 512], F32) as ptmp, \
             nc.sbuf_tensor("x2b", [128, 2 * 512], F32) as x2b, \
             nc.psum_tensor("ps_ap", [128, 1024], F32) as ps_ap, \
             nc.psum_tensor("ps_tr", [128, 1024], F32) as ps_tr, \
             nc.psum_tensor("ps_pr", [128, 2 * 512], F32) as ps_pr:

            def p2_sync():
                e = EY
                ph3 = d_phiq.rearrange("(c p) t -> p c t", p=128)
                x3 = x_in.rearrange("(c p) t -> p c t", p=128)
                for j in range(NJ):
                    if j >= 1:
                        e.wait(("p2apdone", j - 1))
                    e.dma(qbuf[:, :].rearrange("p (c t) -> p c t", t=512),
                          ph3[:, :, 512 * j:512 * (j + 1)], sLD)
                    e.mark(sLD, ("p2q", j))
                    for oc in range(8):
                        qq = j * 8 + oc
                        if qq >= 2:
                            e.wait(("p2x2v", (qq - 2) // 8, (qq - 2) % 8))
                        e.dma(xck[:, (oc % 2) * 512:(oc % 2) * 512 + 512],
                              x3[:, oc, 512 * j:512 * (j + 1)], sLD)
                        e.mark(sLD, ("p2xc", j, oc))

            def p2_tensor():
                e = ET
                for j in range(NJ):
                    e.wait(("p2q", j))
                    for tt in range(4):
                        for h in range(NH):
                            g = h // 4
                            gg = j * 16 + tt * 4 + g
                            if h % 4 == 0 and gg >= 2:
                                pz = gg - 2
                                e.wait(("p2zev", pz // 16, (pz % 16) // 4, (pz % 16) % 4))
                            pr, hf = h // 2, h % 2
                            qsl = qbuf[64 * hf:64 * (hf + 1),
                                       pr * 512 + tt * 128:pr * 512 + tt * 128 + 128]
                            ksl = kvext[64 * hf:64 * (hf + 1), pr * 65:(pr + 1) * 65]
                            if h % 4 != 0:
                                e.drain()
                            i = e.matmul(ps_ap[:, (g % 2) * 512 + (h % 4) * 65:
                                               (g % 2) * 512 + (h % 4) * 65 + 65],
                                         qsl, ksl, start=True, stop=True)
                            if h % 4 == 3:
                                e.inc(i, sMM, 1)
                                e.mark(sMM, ("p2ap", j, tt, g))
                        for p in range(8):
                            e.wait(("p2at", j, tt, p // 2))
                            pp = (j * 4 + tt) * 8 + p
                            if pp >= 2:
                                q_, p_ = divmod(pp - 2, 8)
                                j_, tt_ = divmod(q_, 4)
                                e.wait(("p2trlo", j_, tt_, p_))
                            i = e.transpose(ps_tr[:, (p % 2) * 512:(p % 2) * 512 + 128],
                                            atile[:, p * 128:(p + 1) * 128], eye32_sb[:, :])
                            e.inc(i, sMM, 1)
                            e.mark(sMM, ("p2tr", j, tt, p))
                    e.mark(sMM, ("p2apdone", j))
                    e.wait(("p2trall", j))
                    for oc in range(8):
                        qq = j * 8 + oc
                        if qq >= 2:
                            e.wait(("p2ev", (qq - 2) // 8, (qq - 2) % 8))
                        pp_ = ps_pr[:, (oc % 2) * 512:(oc % 2) * 512 + 512]
                        n = 0
                        for (ah, bh) in ((1, 1), (1, 0), (0, 1)):
                            for k in range(8):
                                w = (wproj_h if ah else wproj_l)[:, k * C + oc * 128:
                                                                 k * C + (oc + 1) * 128]
                                xs = (athi if bh else atlo)[:, k * 512:(k + 1) * 512]
                                i = e.matmul(pp_, w, xs, start=(n == 0), stop=(n == 23))
                                n += 1
                        e.inc(i, sMM, 1)
                        e.mark(sMM, ("p2pr", j, oc))

            def p2_vector():
                e = EV
                for j in range(NJ):
                    for tt in range(4):
                        for g in range(4):
                            e.wait(("p2ap", j, tt, g))
                            if not (j == 0 and tt == 0):
                                pj, ptt = (j, tt - 1) if tt > 0 else (j - 1, 3)
                                e.wait(("p2tr", pj, ptt, 2 * g + 1))
                            pa = ps_ap[:, (g % 2) * 512:(g % 2) * 512 + 260]
                            den = pa.rearrange("p (h w) -> p h w", w=65)[:, :, 64:65]
                            e.tensor_scalar(zt[:, g * 4:(g + 1) * 4],
                                            den.rearrange("p h w -> p (h w)"),
                                            1e-6, None, ALU.add)
                            e.reciprocal(zt[:, g * 4:(g + 1) * 4], zt[:, g * 4:(g + 1) * 4])
                            for hh in range(4):
                                h = g * 4 + hh
                                i = e.tensor_scalar(atile[:, h * 64:(h + 1) * 64],
                                                    pa[:, hh * 65:hh * 65 + 64],
                                                    zt[:, h:h + 1], None, ALU.mult)
                            e.inc(i, sVE, 1)
                            e.mark(sVE, ("p2zev", j, tt, g))
                            e.mark(sVE, ("p2at", j, tt, g))
                        for p in range(8):
                            e.wait(("p2trhi", j, tt, p))
                            dst = atlo[:, p * 512 + tt * 128:p * 512 + tt * 128 + 128]
                            src_h = athi[:, p * 512 + tt * 128:p * 512 + tt * 128 + 128]
                            i = e.tensor_tensor(dst,
                                                ps_tr[:, (p % 2) * 512:(p % 2) * 512 + 128],
                                                src_h, ALU.subtract)
                            e.inc(i, sVE, 1)
                            e.mark(sVE, ("p2trlo", j, tt, p))
                    e.mark(sVE, ("p2trall", j))
                    for oc in range(8):
                        e.wait(("p2ev", j, oc))
                        e.wait(("p2xc", j, oc))
                        qq = j * 8 + oc
                        if qq >= 2:
                            e.wait(("p2x2sp", (qq - 2) // 8, (qq - 2) % 8))
                        i = e.tensor_tensor(x2b[:, (oc % 2) * 512:(oc % 2) * 512 + 512],
                                            ptmp[:, (oc % 2) * 512:(oc % 2) * 512 + 512],
                                            xck[:, (oc % 2) * 512:(oc % 2) * 512 + 512],
                                            ALU.add)
                        e.inc(i, sVE, 1)
                        e.mark(sVE, ("p2x2v", j, oc))

            def p2_scalar():
                e = EC
                for j in range(NJ):
                    for tt in range(4):
                        for p in range(8):
                            e.wait(("p2tr", j, tt, p))
                            i = e.copy(athi[:, p * 512 + tt * 128:p * 512 + tt * 128 + 128],
                                       ps_tr[:, (p % 2) * 512:(p % 2) * 512 + 128])
                            e.inc(i, sEV, 1)
                            e.mark(sEV, ("p2trhi", j, tt, p))
                    for oc in range(8):
                        e.wait(("p2pr", j, oc))
                        qq = j * 8 + oc
                        if qq >= 2:
                            e.wait(("p2x2v", (qq - 2) // 8, (qq - 2) % 8))
                        i = e.activation(ptmp[:, (oc % 2) * 512:(oc % 2) * 512 + 512],
                                         ps_pr[:, (oc % 2) * 512:(oc % 2) * 512 + 512],
                                         AF.Identity, scale=vsl(2, oc),
                                         bias=gpb_sb[:, oc:oc + 1])
                        e.inc(i, sEV, 1)
                        e.mark(sEV, ("p2ev", j, oc))

            def p2_gpsimd():
                e = EG
                for j in range(NJ):
                    for oc in range(8):
                        e.wait(("p2x2v", j, oc))
                        e.dma(d_x2[oc * 128:(oc + 1) * 128, 512 * j:512 * (j + 1)],
                              x2b[:, (oc % 2) * 512:(oc % 2) * 512 + 512], sSP)
                        e.mark(sSP, ("p2x2sp", j, oc))

            run_phase([("sync", p2_sync), ("tensor", p2_tensor), ("vector", p2_vector),
                       ("scalar", p2_scalar), ("gpsimd", p2_gpsimd)], "p2")

        sc_proj.close()

        sc_c12 = contextlib.ExitStack()
        wc1_h = sc_c12.enter_context(nc.sbuf_tensor("wc1_h", [128, 8 * HID], F16))
        wc1_l = sc_c12.enter_context(nc.sbuf_tensor("wc1_l", [128, 8 * HID], F16))
        wc2_h = sc_c12.enter_context(nc.sbuf_tensor("wc2_h", [128, 16 * C], F16))
        wc2_l = sc_c12.enter_context(nc.sbuf_tensor("wc2_l", [128, 16 * C], F16))
        with nc.sbuf_tensor("wstg2", [128, 2 * HID], F32) as wstg2, \
             nc.sbuf_tensor("wtmp2", [128, HID], F32) as wtmp2:
            cw_s, cw_c, cw_v = emit_wprep(
                "p3w", [(wc1_d, 8, HID, 1, wc1_h, wc1_l),
                        (wc2_d, 16, C, None, wc2_h, wc2_l)],
                wstg2, wtmp2, HID, False)
            run_phase([("sync", cw_s), ("scalar", cw_c), ("vector", cw_v)], "p3w")

        with nc.sbuf_tensor("x2t", [128, 8 * TJ], F32) as x2t, \
             nc.sbuf_tensor("sh3", [128, 8 * TJ], F32) as sh3, \
             nc.sbuf_tensor("xn2h", [128, 8 * TJ], F16) as xn2h, \
             nc.sbuf_tensor("xn2l", [128, 8 * TJ], F16) as xn2l, \
             nc.sbuf_tensor("rbc3", [128, 2 * TJ], F32) as rbc3, \
             nc.sbuf_tensor("h1b", [128, 2 * TJ], F32) as h1b, \
             nc.sbuf_tensor("h1h", [128, 16 * TJ], F16) as h1h, \
             nc.sbuf_tensor("h1l", [128, 16 * TJ], F16) as h1l, \
             nc.sbuf_tensor("pt3", [128, 2 * TJ], F32) as pt3, \
             nc.sbuf_tensor("x3b", [128, 8 * TJ], F32) as x3b, \
             nc.sbuf_tensor("y1b", [128, 8 * TJ], F32) as y1b, \
             nc.psum_tensor("ps_c1", [128, 1024], F32) as ps_c1, \
             nc.psum_tensor("ps_c2", [128, 1024], F32) as ps_c2, \
             nc.psum_tensor("ps_ra", [128, 512], F32) as ps_ra, \
             nc.psum_tensor("ps_rb", [128, 512], F32) as ps_rb:

            def p3_sync():
                e = EY
                x23 = d_x2.rearrange("(c p) t -> p c t", p=128)
                x03 = x_in.rearrange("(c p) t -> p c t", p=128)
                for j in range(NJ2):
                    if j >= 1:
                        e.wait(("p3x3all", j - 1))
                    e.dma(x2t[:, :].rearrange("p (c t) -> p c t", t=TJ),
                          x23[:, :, TJ * j:TJ * (j + 1)], sLD)
                    e.mark(sLD, ("p3x2ld", j))
                    if j >= 1:
                        e.wait(("p3y1sp", j - 1))
                    e.dma(y1b[:, :].rearrange("p (c t) -> p c t", t=TJ),
                          x03[:, :, TJ * j:TJ * (j + 1)], sL2)
                    e.mark(sL2, ("p3x0", j))

            def p3_scalar():
                e = EC
                for j in range(NJ2):
                    e.wait(("p3x2ld", j))
                    if j >= 1:
                        e.wait(("p3xn2", j - 1))
                    i = e.activation(sh3[:, :], x2t[:, :], AF.Square)
                    e.inc(i, sEV, 1)
                    e.mark(sEV, ("p3sq", j))
                    e.wait(("p3ss", j))
                    i = e.activation(row(R_SQ, TJ), ps_ra[0:1, 0:TJ], AF.Sqrt)
                    e.inc(i, sEV, 1)
                    e.mark(sEV, ("p3sqrt", j))
                    for ho in range(16):
                        e.wait(("p3mc1", j, ho))
                        hh = j * 16 + ho
                        if hh >= 2:
                            e.wait(("p3h1lo", (hh - 2) // 16, (hh - 2) % 16))
                        pc = ps_c1[:, (ho % 2) * 512:(ho % 2) * 512 + TJ]
                        e.activation(h1b[:, (ho % 2) * TJ:(ho % 2) * TJ + TJ], pc,
                                     AF.Gelu, bias=c1b_sb[:, ho:ho + 1])
                        i = e.copy(h1h[:, ho * TJ:(ho + 1) * TJ],
                                   h1b[:, (ho % 2) * TJ:(ho % 2) * TJ + TJ])
                        e.inc(i, sEV, 1)
                        e.mark(sEV, ("p3gel", j, ho))
                    for oc in range(8):
                        e.wait(("p3mc2", j, oc))
                        qq = j * 8 + oc
                        if qq >= 2:
                            e.wait(("p3x3", (qq - 2) // 8, (qq - 2) % 8))
                        i = e.activation(pt3[:, (oc % 2) * TJ:(oc % 2) * TJ + TJ],
                                         ps_c2[:, (oc % 2) * 512:(oc % 2) * 512 + TJ],
                                         AF.Identity, scale=vsl(2, oc),
                                         bias=gpb_sb[:, 8 + oc:9 + oc])
                        e.inc(i, sEV, 1)
                        e.mark(sEV, ("p3ev2", j, oc))
                    e.wait(("p3x3all", j))
                    i = e.activation(sh3[:, :], x3b[:, :], AF.Square)
                    e.inc(i, sEV, 1)
                    e.mark(sEV, ("p3sq3", j))
                    e.wait(("p3ln1mm", j))
                    e.activation(row(R_M, TJ), ps_ra[0:1, 0:TJ], AF.Identity,
                                 scale=1.0 / 1024.0)
                    i = e.activation(row(R_EX, TJ), ps_rb[0:1, 0:TJ], AF.Identity,
                                     scale=1.0 / 1024.0)
                    e.inc(i, sEV, 1)
                    e.mark(sEV, ("p3rows", j))
                    e.wait(("p3var", j))
                    i = e.activation(row(R_SD, TJ), row(R_VAR, TJ), AF.Sqrt, bias=cbias_sb[0:1, 0:1])
                    e.inc(i, sEV, 1)
                    e.mark(sEV, ("p3sd", j))

            def p3_vector():
                e = EV
                e.memset(amaxc[:, 0:1], 0.0)
                for j in range(NJ2):
                    e.wait(("p3sqrt", j))
                    e.reciprocal(row(R_RC, TJ), row(R_SQ, TJ))
                    i = e.tensor_scalar(row(R_RK, TJ), row(R_RC, TJ), RMS_K, None,
                                        ALU.mult)
                    e.inc(i, sVE, 1)
                    e.mark(sVE, ("p3r2row", j))
                    e.wait(("p3r2bc", j))
                    i = e.tensor_copy(rbc3[:, 0:TJ], ps_ra[:, 0:TJ])
                    e.inc(i, sVE, 1)
                    e.mark(sVE, ("p3r2cp", j))
                    rb3 = rbc3[:, 0:TJ].unsqueeze(1).broadcast_to([128, 8, TJ])
                    x4 = x2t[:, :].rearrange("p (c t) -> p c t", t=TJ)
                    e.tensor_tensor(xn2h[:, :].rearrange("p (c t) -> p c t", t=TJ),
                                    x4, rb3, ALU.mult)
                    e.tensor_tensor(sh3[:, :].rearrange("p (c t) -> p c t", t=TJ),
                                    x4, rb3, ALU.mult)
                    i = e.tensor_tensor(xn2l[:, :], sh3[:, :], xn2h[:, :], ALU.subtract)
                    e.inc(i, sVE, 1)
                    e.mark(sVE, ("p3xn2", j))
                    for ho in range(16):
                        e.wait(("p3gel", j, ho))
                        i = e.tensor_tensor(h1l[:, ho * TJ:(ho + 1) * TJ],
                                            h1b[:, (ho % 2) * TJ:(ho % 2) * TJ + TJ],
                                            h1h[:, ho * TJ:(ho + 1) * TJ], ALU.subtract)
                        e.inc(i, sVE, 1)
                        e.mark(sVE, ("p3h1lo", j, ho))
                    for oc in range(8):
                        e.wait(("p3ev2", j, oc))
                        e.wait(("p3x0", j))
                        e.tensor_tensor(x3b[:, oc * TJ:(oc + 1) * TJ],
                                        pt3[:, (oc % 2) * TJ:(oc % 2) * TJ + TJ],
                                        x2t[:, oc * TJ:(oc + 1) * TJ], ALU.add)
                        i = e.tensor_tensor(x3b[:, oc * TJ:(oc + 1) * TJ],
                                            x3b[:, oc * TJ:(oc + 1) * TJ],
                                            y1b[:, oc * TJ:(oc + 1) * TJ], ALU.add)
                        e.inc(i, sVE, 1)
                        e.mark(sVE, ("p3x3", j, oc))
                    e.mark(sVE, ("p3x3all", j))
                    e.wait(("p3rows", j))
                    e.tensor_tensor(row(R_VAR, TJ), row(R_M, TJ), row(R_M, TJ), ALU.mult)
                    i = e.tensor_tensor(row(R_VAR, TJ), row(R_EX, TJ), row(R_VAR, TJ),
                                        ALU.subtract)
                    e.inc(i, sVE, 1)
                    e.mark(sVE, ("p3var", j))
                    e.wait(("p3sd", j))
                    e.reciprocal(row(R_R, TJ), row(R_SD, TJ))
                    i = e.tensor_tensor(row(R_MR, TJ), row(R_M, TJ), row(R_R, TJ), ALU.mult)
                    e.inc(i, sVE, 1)
                    e.mark(sVE, ("p3lnr", j))
                    e.wait(("p3lnbc", j))
                    if j >= 1:
                        e.wait(("p3y1sp", j - 1))
                    e.tensor_copy(rbc3[:, 0:TJ], ps_ra[:, 0:TJ])
                    i = e.tensor_copy(rbc3[:, TJ:2 * TJ], ps_rb[:, 0:TJ])
                    e.inc(i, sVE, 1)
                    e.mark(sVE, ("p3lncp", j))
                    rb = rbc3[:, 0:TJ].unsqueeze(1).broadcast_to([128, 8, TJ])
                    mrb = rbc3[:, TJ:2 * TJ].unsqueeze(1).broadcast_to([128, 8, TJ])
                    x5 = x3b[:, :].rearrange("p (c t) -> p c t", t=TJ)
                    y5 = y1b[:, :].rearrange("p (c t) -> p c t", t=TJ)
                    e.tensor_tensor(y5, x5, rb, ALU.mult)
                    e.tensor_tensor(y5, y5, mrb, ALU.subtract)
                    e.tensor_reduce(amaxc[:, 4:5], y1b[:, :], AX.X, ALU.max,
                                    apply_absolute_value=True)
                    i = e.tensor_tensor(amaxc[:, 0:1], amaxc[:, 0:1], amaxc[:, 4:5],
                                        ALU.max)
                    e.inc(i, sVE, 1)
                    e.mark(sVE, ("p3y1", j))

            def p3_tensor():
                e = ET
                for j in range(NJ2):
                    e.wait(("p3sq", j))
                    if j >= 1:
                        e.wait(("p3lncp", j - 1))
                    for k in range(8):
                        i = e.matmul(ps_ra[0:1, 0:TJ], onesc_sb[:, :],
                                     sh3[:, k * TJ:(k + 1) * TJ],
                                     start=(k == 0), stop=(k == 7))
                    e.inc(i, sMM, 1)
                    e.mark(sMM, ("p3ss", j))
                    e.wait(("p3sqrt", j))
                    e.wait(("p3r2row", j))
                    i = e.matmul(ps_ra[:, 0:TJ], onesr_sb[0:1, :],
                                 row(R_RK, TJ), start=True, stop=True)
                    e.inc(i, sMM, 1)
                    e.mark(sMM, ("p3r2bc", j))
                    e.wait(("p3xn2", j))
                    for ho in range(16):
                        hh = j * 16 + ho
                        if hh >= 2:
                            e.wait(("p3gel", (hh - 2) // 16, (hh - 2) % 16))
                        pc = ps_c1[:, (ho % 2) * 512:(ho % 2) * 512 + TJ]
                        n = 0
                        for (ah, bh) in ((1, 1), (1, 0), (0, 1)):
                            for k in range(8):
                                w = (wc1_h if ah else wc1_l)[:, k * HID + ho * 128:
                                                             k * HID + (ho + 1) * 128]
                                xs = (xn2h if bh else xn2l)[:, k * TJ:(k + 1) * TJ]
                                i = e.matmul(pc, w, xs, start=(n == 0), stop=(n == 23))
                                n += 1
                        e.inc(i, sMM, 1)
                        e.mark(sMM, ("p3mc1", j, ho))
                    for k in range(16):
                        e.wait(("p3h1lo", j, k))
                    for oc in range(8):
                        qq = j * 8 + oc
                        if qq >= 2:
                            e.wait(("p3ev2", (qq - 2) // 8, (qq - 2) % 8))
                        pc2 = ps_c2[:, (oc % 2) * 512:(oc % 2) * 512 + TJ]
                        n = 0
                        for (ah, bh) in ((1, 1), (1, 0), (0, 1)):
                            for k in range(16):
                                w = (wc2_h if ah else wc2_l)[:, k * C + oc * 128:
                                                             k * C + (oc + 1) * 128]
                                xs = (h1h if bh else h1l)[:, k * TJ:(k + 1) * TJ]
                                i = e.matmul(pc2, w, xs, start=(n == 0), stop=(n == 47))
                                n += 1
                        e.inc(i, sMM, 1)
                        e.mark(sMM, ("p3mc2", j, oc))
                    e.wait(("p3sq3", j))
                    e.wait(("p3r2cp", j))
                    for k in range(8):
                        e.matmul(ps_ra[0:1, 0:TJ], onesc_sb[:, :],
                                 x3b[:, k * TJ:(k + 1) * TJ], start=(k == 0), stop=(k == 7))
                        i = e.matmul(ps_rb[0:1, 0:TJ], onesc_sb[:, :],
                                     sh3[:, k * TJ:(k + 1) * TJ], start=(k == 0),
                                     stop=(k == 7))
                    e.inc(i, sMM, 1)
                    e.mark(sMM, ("p3ln1mm", j))
                    e.wait(("p3lnr", j))
                    e.matmul(ps_ra[:, 0:TJ], onesr_sb[0:1, :], row(R_R, TJ),
                             start=True, stop=True)
                    i = e.matmul(ps_rb[:, 0:TJ], onesr_sb[0:1, :], row(R_MR, TJ),
                                 start=True, stop=True)
                    e.inc(i, sMM, 1)
                    e.mark(sMM, ("p3lnbc", j))

            def p3_gpsimd():
                e = EG
                for j in range(NJ2):
                    e.wait(("p3y1", j))
                    e.dma(d_y1.rearrange("(c p) t -> p c t", p=128)[:, :, TJ * j:TJ * (j + 1)],
                          y1b[:, :].rearrange("p (c t) -> p c t", t=TJ), sSP)
                    e.mark(sSP, ("p3y1sp", j))

            run_phase([("sync", p3_sync), ("scalar", p3_scalar), ("vector", p3_vector),
                       ("tensor", p3_tensor), ("gpsimd", p3_gpsimd)], "p3")

        sc_c12.close()

        def emit_cc_finalize(tag, pieces, ccidx, comp_fn, bcasts, ps_cc):

            def t_fn():
                e = ET
                for n_, (acol, scol) in enumerate(pieces):
                    i = e.transpose(ps_cc[0:1, n_ * 128:n_ * 128 + 128],
                                    amaxc[:, acol:acol + 1], eye32_sb[:, :])
                    e.inc(i, sMM, 1)
                e.mark(sMM, (tag, "tr"))
                e.wait((tag, "comp"))
                for n_, (scol, bcol) in enumerate(bcasts):
                    i = e.matmul(ps_cc[:, n_ * 128:n_ * 128 + 1], onesr_sb[0:1, :],
                                 scal[0:1, scol:scol + 1], start=True, stop=True)
                    e.inc(i, sMM, 1)
                e.mark(sMM, (tag, "bc"))

            def v_fn():
                e = EV
                e.wait((tag, "tr"))
                for n_, (acol, scol) in enumerate(pieces):
                    i = e.tensor_reduce(scal[0:1, scol:scol + 1],
                                        ps_cc[0:1, n_ * 128:n_ * 128 + 128],
                                        AX.X, ALU.max, apply_absolute_value=True)
                if len(pieces) == 1:
                    i = e.memset(scal[0:1, pieces[0][1] + 1:pieces[0][1] + 2], 0.0)
                e.inc(i, sVE, 1)
                e.mark(sVE, (tag, "locmax"))
                e.wait((tag, "ccld"))
                i = comp_fn(e)
                e.inc(i, sVE, 1)
                e.mark(sVE, (tag, "comp"))
                e.wait((tag, "bc"))
                for n_, (scol, bcol) in enumerate(bcasts):
                    i = e.tensor_copy(scbc[:, bcol:bcol + 1], ps_cc[:, n_ * 128:n_ * 128 + 1])
                e.inc(i, sVE, 1)
                e.mark(sVE, (tag, "done"))

            def g_fn():
                e = EG
                e.wait((tag, "locmax"))
                src_col = pieces[0][1]
                e.dma(cc_in[ccidx][0:1, 0:2], scal[0:1, src_col:src_col + 2], sSP)
                e.wait_sem_now(sSP)
                e.cc(cc_in[ccidx][:, :], cc_out[ccidx][:, :], sGP)
                e.wait_sem_now(sGP)
                e.dma(scal[0:1, src_col + 2:src_col + 4], cc_out[ccidx][0:1, 0:2], sSP)
                e.mark(sSP, (tag, "ccld"))

            return [("tensor", t_fn), ("vector", v_fn), ("gpsimd", g_fn)]

        with nc.psum_tensor("ps_cc0", [128, 256], F32) as ps_cc0:
            def comp0(e):
                e.tensor_scalar(sc1(4), sc1(2), INV_QMAX, 1e-12, ALU.mult, ALU.max)
                return e.reciprocal(sc1(5), sc1(4))

            run_phase(emit_cc_finalize("cc0", [(0, 0)], 0, comp0, [(5, 0)], ps_cc0), "cc0")

        sc_mx = contextlib.ExitStack()
        mx_h = sc_mx.enter_context(nc.sbuf_tensor("mx_h", [128, 8 * T], F16))
        mx_l = sc_mx.enter_context(nc.sbuf_tensor("mx_l", [128, 8 * T], F16))

        def mx3(t_):
            return t_.rearrange("p (c t) -> p c t", t=T)

        mats5 = [(wup_d, wup_t, d_U, 2), (wgate_d, wgate_t, d_G, 3)]

        with nc.sbuf_tensor("y1t5", [128, 2 * 512], F32) as y1t5, \
             nc.sbuf_tensor("mxi", [128, 2 * 512], I32) as mxi, \
             nc.sbuf_tensor("wff", [128, 2 * C], F32) as wff, \
             nc.sbuf_tensor("wts", [128, 2 * 8 * 128], F32) as wts, \
             nc.sbuf_tensor("sdq", [128, 2 * 32], F32) as sdq, \
             nc.sbuf_tensor("sTb", [32, 2 * 128], F32) as sTb, \
             nc.sbuf_tensor("srep", [128, 2 * 128], F32) as srep, \
             nc.sbuf_tensor("rsrep", [128, 2 * 128], F32) as rsrep, \
             nc.sbuf_tensor("mwt", [128, 128], I32) as mwt, \
             nc.sbuf_tensor("w16t", [128, 128], F32) as w16t, \
             nc.sbuf_tensor("wqh", [128, 2 * 8 * 128], F16) as wqh, \
             nc.sbuf_tensor("wql", [128, 2 * 8 * 128], F16) as wql, \
             nc.sbuf_tensor("ub", [128, 4 * 512], F32) as ub, \
             nc.psum_tensor("ps_ug", [128, 2 * 512], F32) as ps_ug, \
             nc.psum_tensor("ps_s5", [32, 1024], F32) as ps_s5, \
             nc.psum_tensor("ps_sr", [128, 1024], F32) as ps_sr:

            ND = 48

            def p5_sync():
                e = EY
                y13 = d_y1.rearrange("(c p) t -> p c t", p=128)
                for j in range(NJ):
                    for c in range(8):
                        fc = j * 8 + c
                        if fc >= 2:
                            e.wait(("p5mxl", (fc - 2) // 8, (fc - 2) % 8))
                        e.dma(y1t5[:, (fc % 2) * 512:(fc % 2) * 512 + 512],
                              y13[:, c, 512 * j:512 * (j + 1)], sLD)
                        e.mark(sLD, ("p5y1", j, c))
                for nb in range(ND):
                    mi, oc = divmod(nb, 24)
                    if nb >= 2:
                        e.wait(("p5s", nb - 2))
                    e.dma(wff[:, (nb % 2) * C:(nb % 2) * C + C],
                          mats5[mi][0][oc * 128:(oc + 1) * 128, :], sLD)
                    e.mark(sLD, ("p5w", nb))
                    if nb >= 2:
                        e.wait(("p5dq", nb - 2, 7))
                    e.dma(wts[:, (nb % 2) * 1024:(nb % 2) * 1024 + 1024]
                          .rearrange("p (ci o) -> p ci o", o=128),
                          mats5[mi][1].rearrange("(ci p) o -> p ci o", p=128)
                          [:, :, oc * 128:(oc + 1) * 128], sL2)
                    e.mark(sL2, ("p5wt", nb))

            def p5_scalar_all():
                e = EC
                for j in range(NJ):
                    for c in range(8):
                        fc = j * 8 + c
                        e.wait(("p5y1", j, c))
                        if fc >= 2:
                            e.wait(("p5mxl", (fc - 2) // 8, (fc - 2) % 8))
                        sl = slice((fc % 2) * 512, (fc % 2) * 512 + 512)
                        e.activation(mxi[:, sl], y1t5[:, sl], AF.Identity,
                                     scale=scbc[:, 0:1])
                        i = e.copy(mx3(mx_h[:, :])[:, c, 512 * j:512 * (j + 1)],
                                   mxi[:, sl])
                        e.inc(i, sEV, 1)
                        e.mark(sEV, ("p5mxh", j, c))
                for nb0 in range(2):
                    e.wait(("p5strd", nb0))
                    i = e.copy(sTb[0:32, (nb0 % 2) * 128:(nb0 % 2) * 128 + 128],
                               ps_s5[0:32, (nb0 % 2) * 512:(nb0 % 2) * 512 + 128])
                    e.inc(i, sEV, 1)
                    e.mark(sEV, ("p5stcp", nb0))
                for nb in range(ND):
                    for j in range(NJ):
                        fb = nb * NJ + j
                        e.wait(("p5mm", nb, j))
                        if fb >= 4:
                            e.wait(("p5sp", fb - 4))
                            e.wait(("p5am", fb - 4))
                        i = e.copy(ub[:, (fb % 4) * 512:(fb % 4) * 512 + 512],
                                   ps_ug[:, (fb % 2) * 512:(fb % 2) * 512 + 512])
                        e.inc(i, sEV, 1)
                        e.mark(sEV, ("p5ev", fb))
                    if nb + 2 < ND:
                        e.wait(("p5strd", nb + 2))
                        e.wait(("p5sr", nb, 7))
                        i = e.copy(sTb[0:32, ((nb + 2) % 2) * 128:((nb + 2) % 2) * 128 + 128],
                                   ps_s5[0:32, ((nb + 2) % 2) * 512:((nb + 2) % 2) * 512 + 128])
                        e.inc(i, sEV, 1)
                        e.mark(sEV, ("p5stcp", nb + 2))

            def p5_dequant_head(e, nb):
                e.wait(("p5w", nb))
                if nb >= 2:
                    e.wait(("p5strd", nb - 2))
                stg = wff[:, (nb % 2) * C:(nb % 2) * C + C]
                w3 = stg.rearrange("p (b i) -> p b i", i=32)
                ss_ = sdq[:, (nb % 2) * 32:(nb % 2) * 32 + 32]
                e.tensor_reduce(ss_, w3, AX.X, ALU.max, apply_absolute_value=True)
                i = e.tensor_scalar(ss_, ss_, 1.0 / 7.0, 1e-12, ALU.mult, ALU.max)
                e.inc(i, sVE, 1)
                e.mark(sVE, ("p5s", nb))

            def p5_dequant_tail(e, nb):
                e.wait(("p5wt", nb))
                for ci in range(8):
                    e.wait(("p5sr", nb, ci))
                    sl2 = slice((ci % 2) * 128, (ci % 2) * 128 + 128)
                    e.tensor_copy(srep[:, sl2], ps_sr[:, (ci % 2) * 512:(ci % 2) * 512 + 128])
                    i = e.reciprocal(rsrep[:, sl2], ps_sr[:, (ci % 2) * 512:(ci % 2) * 512 + 128])
                    e.inc(i, sVE, 1)
                    e.mark(sVE, ("p5srcp", nb, ci))
                    wt_sl = wts[:, (nb % 2) * 1024 + ci * 128:(nb % 2) * 1024 + (ci + 1) * 128]
                    e.tensor_tensor(mwt[:, :], wt_sl, rsrep[:, sl2], ALU.mult)
                    sl = slice((nb % 2) * 1024 + ci * 128, (nb % 2) * 1024 + (ci + 1) * 128)
                    e.tensor_tensor(wqh[:, sl], mwt[:, :], srep[:, sl2], ALU.mult)
                    e.tensor_tensor(w16t[:, :], mwt[:, :], srep[:, sl2], ALU.mult)
                    i = e.tensor_tensor(wql[:, sl], w16t[:, :], wqh[:, sl], ALU.subtract)
                    e.inc(i, sVE, 1)
                    e.mark(sVE, ("p5dq", nb, ci))

            def p5_vector():
                e = EV
                e.memset(amaxc[:, 2:4], 0.0)
                i = e.memset(amaxc[:, 5:7], 0.0)
                e.inc(i, sVE, 1)
                e.mark(sVE, ("p5amz",))
                for j in range(NJ):
                    for c in range(8):
                        fc = j * 8 + c
                        e.wait(("p5mxh", j, c))
                        sl = slice((fc % 2) * 512, (fc % 2) * 512 + 512)
                        i = e.tensor_tensor(mx3(mx_l[:, :])[:, c, 512 * j:512 * (j + 1)],
                                            mxi[:, sl],
                                            mx3(mx_h[:, :])[:, c, 512 * j:512 * (j + 1)],
                                            ALU.subtract)
                        e.inc(i, sVE, 1)
                        e.mark(sVE, ("p5mxl", j, c))
                p5_dequant_head(e, 0)
                p5_dequant_tail(e, 0)
                p5_dequant_head(e, 1)
                p5_dequant_tail(e, 1)
                for nb in range(ND):
                    mi, oc = divmod(nb, 24)
                    for j in range(NJ):
                        fb = nb * NJ + j
                        e.wait(("p5ev", fb))
                        e.tensor_reduce(amaxc[:, 4:5],
                                        ub[:, (fb % 4) * 512:(fb % 4) * 512 + 512],
                                        AX.X, ALU.max, apply_absolute_value=True)
                        i = e.tensor_tensor(amaxc[:, 2 + mi:3 + mi], amaxc[:, 2 + mi:3 + mi],
                                            amaxc[:, 4:5], ALU.max)
                        e.inc(i, sVE, 1)
                        e.mark(sVE, ("p5am", fb))
                    if nb + 2 < ND:
                        p5_dequant_head(e, nb + 2)
                        p5_dequant_tail(e, nb + 2)

            def p5_gpsimd():
                e = EG
                for nb in range(ND):
                    mi, oc = divmod(nb, 24)
                    ddst = mats5[mi][2]
                    for j in range(NJ):
                        fb = nb * NJ + j
                        e.wait(("p5ev", fb))
                        e.dma(ddst[oc * 128:(oc + 1) * 128, 512 * j:512 * (j + 1)],
                              ub[:, (fb % 4) * 512:(fb % 4) * 512 + 512], sSP)
                        e.mark(sSP, ("p5sp", fb))

            def p5_tensor_dq(e, nb):
                e.wait(("p5s", nb))
                i = e.transpose(ps_s5[0:32, (nb % 2) * 512:(nb % 2) * 512 + 128],
                                sdq[:, (nb % 2) * 32:(nb % 2) * 32 + 32], eye32_sb[:, :])
                e.inc(i, sMM, 1)
                e.mark(sMM, ("p5strd", nb))
                e.wait(("p5stcp", nb))
                for ci in range(8):
                    cc_ = nb * 8 + ci
                    if cc_ >= 2:
                        e.wait(("p5srcp", (cc_ - 2) // 8, (cc_ - 2) % 8))
                    i = e.matmul(ps_sr[:, (ci % 2) * 512:(ci % 2) * 512 + 128],
                                 eexpA_sb[:, ci * 128:(ci + 1) * 128],
                                 sTb[:, (nb % 2) * 128:(nb % 2) * 128 + 128],
                                 start=True, stop=True)
                    e.inc(i, sMM, 1)
                    e.mark(sMM, ("p5sr", nb, ci))

            def p5_tensor():
                e = ET
                p5_tensor_dq(e, 0)
                p5_tensor_dq(e, 1)
                for nb in range(ND):
                    for ci in range(8):
                        e.wait(("p5dq", nb, ci))
                    for j in range(NJ):
                        fb = nb * NJ + j
                        for c in range(8):
                            e.wait(("p5mxl", j, c))
                        if fb >= 2:
                            e.wait(("p5ev", fb - 2))
                        pu = ps_ug[:, (fb % 2) * 512:(fb % 2) * 512 + 512]
                        n = 0
                        for (ah, bh) in ((1, 1), (1, 0), (0, 1)):
                            for k in range(8):
                                sl = slice((nb % 2) * 1024 + k * 128,
                                           (nb % 2) * 1024 + (k + 1) * 128)
                                w = (wqh if ah else wql)[:, sl]
                                xs = mx3((mx_h if bh else mx_l)[:, :])[:, k,
                                                                       512 * j:512 * (j + 1)]
                                i = e.matmul(pu, w, xs, start=(n == 0), stop=(n == 23))
                                n += 1
                        e.inc(i, sMM, 1)
                        e.mark(sMM, ("p5mm", nb, j))
                    if nb + 2 < ND:
                        p5_tensor_dq(e, nb + 2)

            run_phase([("sync", p5_sync), ("scalar", p5_scalar_all),
                       ("vector", p5_vector), ("gpsimd", p5_gpsimd),
                       ("tensor", p5_tensor)], "p5")

        with nc.psum_tensor("ps_cc1", [128, 384], F32) as ps_cc1:
            def comp1(e):
                e.tensor_tensor(sc1(11), sc1(8), sc1(4), ALU.mult)
                e.tensor_scalar(sc1(10), sc1(11), INV_QMAX, 1e-12, ALU.mult, ALU.max)
                e.reciprocal(sc1(11), sc1(10))
                e.tensor_tensor(sc1(12), sc1(4), sc1(11), ALU.mult)
                e.tensor_tensor(sc1(11), sc1(9), sc1(4), ALU.mult)
                e.tensor_scalar(sc1(13), sc1(11), INV_QMAX, 1e-12, ALU.mult, ALU.max)
                e.reciprocal(sc1(11), sc1(13))
                e.tensor_tensor(sc1(14), sc1(4), sc1(11), ALU.mult)
                return e.tensor_scalar(sc1(15), sc1(13), 1.0 / 65536.0, None, ALU.mult)

            run_phase(emit_cc_finalize("cc1", [(2, 6), (3, 7)], 1, comp1,
                                       [(12, 1), (14, 2), (13, 3)], ps_cc1), "cc1")
        sc_mx.close()

        with nc.sbuf_tensor("u6", [128, 2 * 2048], F32) as u6, \
             nc.sbuf_tensor("g6", [128, 2 * 2048], F32) as g6, \
             nc.sbuf_tensor("mu6", [128, 2 * 2048], I32) as mu6, \
             nc.sbuf_tensor("mg6", [128, 2 * 2048], I32) as mg6, \
             nc.sbuf_tensor("sg6", [128, 2 * 2048], F32) as sg6, \
             nc.sbuf_tensor("ms6", [128, 2 * 2048], I32) as ms6, \
             nc.sbuf_tensor("p6", [128, 2 * 2048], F32) as p6:

            NJ4 = T // 2048 if T >= 2048 else 1
            W6 = min(T, 2048)
            NB6 = 24 * NJ4

            def p6a_sync():
                e = EY
                for fb in range(NB6):
                    oc, j = divmod(fb, NJ4)
                    if fb >= 2:
                        e.wait(("p6musp", fb - 2))
                        e.wait(("p6psp", fb - 2))
                    e.dma(u6[:, (fb % 2) * W6:(fb % 2) * W6 + W6],
                          d_U[oc * 128:(oc + 1) * 128, W6 * j:W6 * (j + 1)], sLD)
                    e.dma(g6[:, (fb % 2) * W6:(fb % 2) * W6 + W6],
                          d_G[oc * 128:(oc + 1) * 128, W6 * j:W6 * (j + 1)], sL2)
                    e.mark(sLD, ("p6ldu", fb))
                    e.mark(sL2, ("p6ldg", fb))

            def p6a_scalar():
                e = EC
                for fb in range(NB6):
                    e.wait(("p6ldu", fb))
                    e.wait(("p6ldg", fb))
                    if fb >= 2:
                        e.wait(("p6p", fb - 2))
                        e.wait(("p6musp", fb - 2))
                    sl = slice((fb % 2) * W6, (fb % 2) * W6 + W6)
                    e.activation(mu6[:, sl], u6[:, sl], AF.Identity, scale=scbc[:, 1:2])
                    e.activation(mg6[:, sl], g6[:, sl], AF.Identity, scale=scbc[:, 2:3])
                    i = e.activation(sg6[:, sl], mg6[:, sl], AF.Sigmoid, scale=scbc[:, 3:4])
                    e.inc(i, sEV, 1)
                    e.mark(sEV, ("p6s", fb))

            def p6a_vector():
                e = EV
                for fb in range(NB6):
                    e.wait(("p6s", fb))
                    if fb >= 2:
                        e.wait(("p6psp", fb - 2))
                    sl = slice((fb % 2) * W6, (fb % 2) * W6 + W6)
                    e.tensor_scalar(ms6[:, sl], sg6[:, sl], 65536.0, 65535.0,
                                    ALU.mult, ALU.min)
                    e.tensor_tensor(p6[:, sl], mg6[:, sl], ms6[:, sl], ALU.mult)
                    e.tensor_reduce(amaxc[:, 4:5], p6[:, sl], AX.X, ALU.max,
                                    apply_absolute_value=True)
                    i = e.tensor_tensor(amaxc[:, 5:6], amaxc[:, 5:6], amaxc[:, 4:5],
                                        ALU.max)
                    e.inc(i, sVE, 1)
                    e.mark(sVE, ("p6p", fb))

            def p6a_gpsimd():
                e = EG
                for fb in range(NB6):
                    oc, j = divmod(fb, NJ4)
                    sl = slice((fb % 2) * W6, (fb % 2) * W6 + W6)
                    e.wait(("p6s", fb))
                    e.dma(d_mu[oc * 128:(oc + 1) * 128, W6 * j:W6 * (j + 1)],
                          mu6[:, sl], sSP)
                    e.mark(sSP, ("p6musp", fb))
                    e.wait(("p6p", fb))
                    e.dma(d_P[oc * 128:(oc + 1) * 128, W6 * j:W6 * (j + 1)],
                          p6[:, sl], sS2)
                    e.mark(sS2, ("p6psp", fb))

            run_phase([("sync", p6a_sync), ("scalar", p6a_scalar),
                       ("vector", p6a_vector), ("gpsimd", p6a_gpsimd)], "p6a")

        with nc.psum_tensor("ps_cc2", [128, 256], F32) as ps_cc2:
            def comp2(e):
                e.tensor_tensor(sc1(22), sc1(19), sc1(15), ALU.mult)
                e.tensor_scalar(sc1(21), sc1(22), INV_QMAX, 1e-12, ALU.mult, ALU.max)
                e.reciprocal(sc1(22), sc1(21))
                return e.tensor_tensor(sc1(23), sc1(15), sc1(22), ALU.mult)

            run_phase(emit_cc_finalize("cc2", [(5, 17)], 2, comp2, [(23, 4)], ps_cc2), "cc2")

        with nc.sbuf_tensor("p6b", [128, 2 * 2048], F32) as p6b, \
             nc.sbuf_tensor("mu6b", [128, 2 * 2048], I32) as mu6b, \
             nc.sbuf_tensor("mg2b", [128, 2 * 2048], I32) as mg2b, \
             nc.sbuf_tensor("q6", [128, 2 * 2048], F32) as q6:

            NJ4 = T // 2048 if T >= 2048 else 1
            W6 = min(T, 2048)
            NB6 = 24 * NJ4

            def p6b_sync():
                e = EY
                for fb in range(NB6):
                    oc, j = divmod(fb, NJ4)
                    if fb >= 2:
                        e.wait(("p6qsp", fb - 2))
                    e.dma(p6b[:, (fb % 2) * W6:(fb % 2) * W6 + W6],
                          d_P[oc * 128:(oc + 1) * 128, W6 * j:W6 * (j + 1)], sLD)
                    e.dma(mu6b[:, (fb % 2) * W6:(fb % 2) * W6 + W6],
                          d_mu[oc * 128:(oc + 1) * 128, W6 * j:W6 * (j + 1)], sL2)
                    e.mark(sLD, ("p6bldp", fb))
                    e.mark(sL2, ("p6bldm", fb))

            def p6b_scalar():
                e = EC
                for fb in range(NB6):
                    e.wait(("p6bldp", fb))
                    if fb >= 2:
                        e.wait(("p6q", fb - 2))
                    sl = slice((fb % 2) * W6, (fb % 2) * W6 + W6)
                    i = e.activation(mg2b[:, sl], p6b[:, sl], AF.Identity,
                                     scale=scbc[:, 4:5])
                    e.inc(i, sEV, 1)
                    e.mark(sEV, ("p6mg2", fb))

            def p6b_vector():
                e = EV
                for fb in range(NB6):
                    e.wait(("p6mg2", fb))
                    e.wait(("p6bldm", fb))
                    if fb >= 2:
                        e.wait(("p6qsp", fb - 2))
                    sl = slice((fb % 2) * W6, (fb % 2) * W6 + W6)
                    e.tensor_tensor(q6[:, sl], mg2b[:, sl], mu6b[:, sl], ALU.mult)
                    e.tensor_reduce(amaxc[:, 4:5], q6[:, sl], AX.X, ALU.max,
                                    apply_absolute_value=True)
                    i = e.tensor_tensor(amaxc[:, 6:7], amaxc[:, 6:7], amaxc[:, 4:5],
                                        ALU.max)
                    e.inc(i, sVE, 1)
                    e.mark(sVE, ("p6q", fb))

            def p6b_gpsimd():
                e = EG
                for fb in range(NB6):
                    oc, j = divmod(fb, NJ4)
                    e.wait(("p6q", fb))
                    e.dma(d_Q[oc * 128:(oc + 1) * 128, W6 * j:W6 * (j + 1)],
                          q6[:, (fb % 2) * W6:(fb % 2) * W6 + W6], sSP)
                    e.mark(sSP, ("p6qsp", fb))

            run_phase([("sync", p6b_sync), ("scalar", p6b_scalar),
                       ("vector", p6b_vector), ("gpsimd", p6b_gpsimd)], "p6b")

        with nc.psum_tensor("ps_cc3", [128, 256], F32) as ps_cc3:
            def comp3(e):
                e.tensor_tensor(sc1(28), sc1(21), sc1(10), ALU.mult)
                e.tensor_tensor(sc1(29), sc1(26), sc1(28), ALU.mult)
                e.tensor_scalar(sc1(30), sc1(29), INV_QMAX, 1e-12, ALU.mult, ALU.max)
                e.reciprocal(sc1(31), sc1(30))
                return e.tensor_tensor(sc1(29), sc1(28), sc1(31), ALU.mult)

            run_phase(emit_cc_finalize("cc3", [(6, 24)], 0, comp3,
                                       [(29, 5), (30, 6)], ps_cc3), "cc3")

        sc_wd = contextlib.ExitStack()
        wdq_h = sc_wd.enter_context(nc.sbuf_tensor("wdq_h", [128, 8 * FF], F16))
        wdq_l = sc_wd.enter_context(nc.sbuf_tensor("wdq_l", [128, 8 * FF], F16))

        with nc.sbuf_tensor("eexpB_sb", [96, 24 * 128], F32) as eexpB_sb, \
             nc.sbuf_tensor("wff7", [128, 2 * FF], F32) as wff7, \
             nc.sbuf_tensor("wts7", [128, 2 * 24 * 128], F32) as wts7, \
             nc.sbuf_tensor("sdq7", [128, 2 * 96], F32) as sdq7, \
             nc.sbuf_tensor("sTb7", [96, 2 * 128], F32) as sTb7, \
             nc.sbuf_tensor("srep7", [128, 2 * 128], F32) as srep7, \
             nc.sbuf_tensor("rsrep7", [128, 2 * 128], F32) as rsrep7, \
             nc.sbuf_tensor("mwt7", [128, 128], I32) as mwt7, \
             nc.sbuf_tensor("w16t7", [128, 128], F32) as w16t7, \
             nc.psum_tensor("ps_s57", [96, 1024], F32) as ps_s57, \
             nc.psum_tensor("ps_sr7", [128, 1024], F32) as ps_sr7:

            def p7a_sync():
                e = EY
                e.dma(eexpB_sb[:, :], eexpB_c[:, :], sLD)
                for nb in range(8):
                    if nb >= 2:
                        e.wait(("p7s", nb - 2))
                    e.dma(wff7[:, (nb % 2) * FF:(nb % 2) * FF + FF],
                          wdown_d[nb * 128:(nb + 1) * 128, :], sLD)
                    e.mark(sLD, ("p7w", nb))
                    if nb >= 2:
                        e.wait(("p7dq", nb - 2, 23))
                    e.dma(wts7[:, (nb % 2) * 3072:(nb % 2) * 3072 + 3072]
                          .rearrange("p (ci o) -> p ci o", o=128),
                          wdown_t.rearrange("(ci p) o -> p ci o", p=128)
                          [:, :, nb * 128:(nb + 1) * 128], sL2)
                    e.mark(sL2, ("p7wt_ld", nb))

            def p7a_vector():
                e = EV
                for nb in range(8):
                    e.wait(("p7w", nb))
                    if nb >= 2:
                        e.wait(("p7strd", nb - 2))
                    stg = wff7[:, (nb % 2) * FF:(nb % 2) * FF + FF]
                    w3 = stg.rearrange("p (b i) -> p b i", i=32)
                    ss_ = sdq7[:, (nb % 2) * 96:(nb % 2) * 96 + 96]
                    e.tensor_reduce(ss_, w3, AX.X, ALU.max, apply_absolute_value=True)
                    i = e.tensor_scalar(ss_, ss_, 1.0 / 7.0, 1e-12, ALU.mult, ALU.max)
                    e.inc(i, sVE, 1)
                    e.mark(sVE, ("p7s", nb))
                    e.wait(("p7wt_ld", nb))
                    for ci in range(24):
                        e.wait(("p7sr", nb, ci))
                        sl2 = slice((ci % 2) * 128, (ci % 2) * 128 + 128)
                        e.tensor_copy(srep7[:, sl2],
                                      ps_sr7[:, (ci % 2) * 512:(ci % 2) * 512 + 128])
                        i = e.reciprocal(rsrep7[:, sl2],
                                         ps_sr7[:, (ci % 2) * 512:(ci % 2) * 512 + 128])
                        e.inc(i, sVE, 1)
                        e.mark(sVE, ("p7srcp", nb, ci))
                        wt_sl = wts7[:, (nb % 2) * 3072 + ci * 128:
                                     (nb % 2) * 3072 + (ci + 1) * 128]
                        e.tensor_tensor(mwt7[:, :], wt_sl, rsrep7[:, sl2], ALU.mult)
                        sl = slice(nb * FF + ci * 128, nb * FF + (ci + 1) * 128)
                        e.tensor_tensor(wdq_h[:, sl], mwt7[:, :], srep7[:, sl2], ALU.mult)
                        e.tensor_tensor(w16t7[:, :], mwt7[:, :], srep7[:, sl2], ALU.mult)
                        i = e.tensor_tensor(wdq_l[:, sl], w16t7[:, :], wdq_h[:, sl],
                                            ALU.subtract)
                        e.inc(i, sVE, 1)
                        e.mark(sVE, ("p7dq", nb, ci))

            def p7a_scalar():
                e = EC
                for nb in range(8):
                    e.wait(("p7strd", nb))
                    if nb >= 2:
                        e.wait(("p7sr", nb - 2, 23))
                    i = e.copy(sTb7[0:96, (nb % 2) * 128:(nb % 2) * 128 + 128],
                               ps_s57[0:96, (nb % 2) * 512:(nb % 2) * 512 + 128])
                    e.inc(i, sEV, 1)
                    e.mark(sEV, ("p7stcp", nb))

            def p7a_tensor():
                e = ET
                for nb in range(8):
                    e.wait(("p7s", nb))
                    i = e.transpose(ps_s57[0:96, (nb % 2) * 512:(nb % 2) * 512 + 128],
                                    sdq7[:, (nb % 2) * 96:(nb % 2) * 96 + 96],
                                    eye32_sb[:, :])
                    e.inc(i, sMM, 1)
                    e.mark(sMM, ("p7strd", nb))
                    e.wait(("p7stcp", nb))
                    for ci in range(24):
                        cc_ = nb * 24 + ci
                        if cc_ >= 2:
                            e.wait(("p7srcp", (cc_ - 2) // 24, (cc_ - 2) % 24))
                        i = e.matmul(ps_sr7[:, (ci % 2) * 512:(ci % 2) * 512 + 128],
                                     eexpB_sb[:, ci * 128:(ci + 1) * 128],
                                     sTb7[:, (nb % 2) * 128:(nb % 2) * 128 + 128],
                                     start=True, stop=True)
                        e.inc(i, sMM, 1)
                        e.mark(sMM, ("p7sr", nb, ci))

            run_phase([("sync", p7a_sync), ("vector", p7a_vector), ("scalar", p7a_scalar),
                       ("tensor", p7a_tensor)], "p7a")

        with nc.sbuf_tensor("q7", [128, 2 * TJ], F32) as q7, \
             nc.sbuf_tensor("moi", [128, 2 * TJ], I32) as moi, \
             nc.sbuf_tensor("mo_h", [128, 2 * 24 * TJ], F16) as mo_h, \
             nc.sbuf_tensor("mo_l", [128, 2 * 24 * TJ], F16) as mo_l, \
             nc.sbuf_tensor("y1t7", [128, 8 * TJ], F32) as y1t7, \
             nc.sbuf_tensor("pt7", [128, 2 * TJ], F32) as pt7, \
             nc.sbuf_tensor("x4b", [128, 8 * TJ], F32) as x4b, \
             nc.sbuf_tensor("sh7", [128, 8 * TJ], F32) as sh7, \
             nc.sbuf_tensor("rbc7", [128, 2 * TJ], F32) as rbc7, \
             nc.sbuf_tensor("yb", [128, 8 * TJ], F32) as yb, \
             nc.psum_tensor("ps_dn", [128, 1024], F32) as ps_dn, \
             nc.psum_tensor("ps_r7a", [128, 512], F32) as ps_r7a, \
             nc.psum_tensor("ps_r7b", [128, 512], F32) as ps_r7b:

            def p7b_sync():
                e = EY
                q3 = d_Q.rearrange("(c p) t -> p c t", p=128)
                y13 = d_y1.rearrange("(c p) t -> p c t", p=128)
                for j in range(NJ2):
                    for ci in range(24):
                        fb = j * 24 + ci
                        if fb >= 2:
                            pf = fb - 2
                            e.wait(("p7mol", pf // 24, pf % 24))
                        e.dma(q7[:, (ci % 2) * TJ:(ci % 2) * TJ + TJ],
                              q3[:, ci, TJ * j:TJ * (j + 1)], sLD)
                        e.mark(sLD, ("p7q", j, ci))
                    if j >= 1:
                        e.wait(("p7x4all", j - 1))
                    e.dma(y1t7[:, :].rearrange("p (c t) -> p c t", t=TJ),
                          y13[:, :, TJ * j:TJ * (j + 1)], sLD)
                    e.mark(sLD, ("p7y1", j))

            def p7b_scalar():
                e = EC
                for j in range(NJ2 + 1):
                    if j < NJ2:
                        for ci in range(24):
                            e.wait(("p7q", j, ci))
                            if j >= 2 and ci == 0:
                                e.wait(("p7dn", j - 2, 7))
                            if j * 24 + ci >= 2:
                                pf = j * 24 + ci - 2
                                e.wait(("p7mol", pf // 24, pf % 24))
                            sl = slice((ci % 2) * TJ, (ci % 2) * TJ + TJ)
                            off = (j % 2) * 24 * TJ
                            e.activation(moi[:, sl], q7[:, sl], AF.Identity,
                                         scale=scbc[:, 5:6])
                            i = e.copy(mo_h[:, off + ci * TJ:off + (ci + 1) * TJ],
                                       moi[:, sl])
                            e.inc(i, sEV, 1)
                            e.mark(sEV, ("p7moh", j, ci))
                    if j < 1:
                        continue
                    jj = j - 1
                    for oc in range(8):
                        e.wait(("p7dn", jj, oc))
                        qq = jj * 8 + oc
                        if qq >= 2:
                            e.wait(("p7x4", (qq - 2) // 8, (qq - 2) % 8))
                        i = e.activation(pt7[:, (oc % 2) * TJ:(oc % 2) * TJ + TJ],
                                         ps_dn[:, (oc % 2) * 512:(oc % 2) * 512 + TJ],
                                         AF.Identity, scale=scbc[:, 6:7])
                        e.inc(i, sEV, 1)
                        e.mark(sEV, ("p7ev", jj, oc))
                    e.wait(("p7x4all", jj))
                    i = e.activation(sh7[:, :], x4b[:, :], AF.Square)
                    e.inc(i, sEV, 1)
                    e.mark(sEV, ("p7sq", jj))
                    e.wait(("p7lnmm", jj))
                    e.activation(row(R_M, TJ), ps_r7a[0:1, 0:TJ], AF.Identity,
                                 scale=1.0 / 1024.0)
                    i = e.activation(row(R_EX, TJ), ps_r7b[0:1, 0:TJ], AF.Identity,
                                     scale=1.0 / 1024.0)
                    e.inc(i, sEV, 1)
                    e.mark(sEV, ("p7rows", jj))
                    e.wait(("p7var", jj))
                    i = e.activation(row(R_SD, TJ), row(R_VAR, TJ), AF.Sqrt, bias=cbias_sb[0:1, 0:1])
                    e.inc(i, sEV, 1)
                    e.mark(sEV, ("p7sd", jj))

            def p7b_vector():
                e = EV
                for j in range(NJ2 + 1):
                    if j < NJ2:
                        for ci in range(24):
                            e.wait(("p7moh", j, ci))
                            sl = slice((ci % 2) * TJ, (ci % 2) * TJ + TJ)
                            off = (j % 2) * 24 * TJ
                            i = e.tensor_tensor(mo_l[:, off + ci * TJ:off + (ci + 1) * TJ],
                                                moi[:, sl],
                                                mo_h[:, off + ci * TJ:off + (ci + 1) * TJ],
                                                ALU.subtract)
                            e.inc(i, sVE, 1)
                            e.mark(sVE, ("p7mol", j, ci))
                    if j < 1:
                        continue
                    j = j - 1
                    for oc in range(8):
                        e.wait(("p7ev", j, oc))
                        e.wait(("p7y1", j))
                        i = e.tensor_tensor(x4b[:, oc * TJ:(oc + 1) * TJ],
                                            pt7[:, (oc % 2) * TJ:(oc % 2) * TJ + TJ],
                                            y1t7[:, oc * TJ:(oc + 1) * TJ], ALU.add)
                        e.inc(i, sVE, 1)
                        e.mark(sVE, ("p7x4", j, oc))
                    e.mark(sVE, ("p7x4all", j))
                    e.wait(("p7rows", j))
                    e.tensor_tensor(row(R_VAR, TJ), row(R_M, TJ), row(R_M, TJ), ALU.mult)
                    i = e.tensor_tensor(row(R_VAR, TJ), row(R_EX, TJ), row(R_VAR, TJ),
                                        ALU.subtract)
                    e.inc(i, sVE, 1)
                    e.mark(sVE, ("p7var", j))
                    e.wait(("p7sd", j))
                    e.reciprocal(row(R_R, TJ), row(R_SD, TJ))
                    i = e.tensor_tensor(row(R_MR, TJ), row(R_M, TJ), row(R_R, TJ),
                                        ALU.mult)
                    e.inc(i, sVE, 1)
                    e.mark(sVE, ("p7lnr", j))
                    e.wait(("p7lnbc", j))
                    e.tensor_copy(rbc7[:, 0:TJ], ps_r7a[:, 0:TJ])
                    i = e.tensor_copy(rbc7[:, TJ:2 * TJ], ps_r7b[:, 0:TJ])
                    e.inc(i, sVE, 1)
                    e.mark(sVE, ("p7lncp", j))
                    if j >= 1:
                        e.wait(("p7ysp", j - 1))
                    rb = rbc7[:, 0:TJ].unsqueeze(1).broadcast_to([128, 8, TJ])
                    mrb = rbc7[:, TJ:2 * TJ].unsqueeze(1).broadcast_to([128, 8, TJ])
                    x5 = x4b[:, :].rearrange("p (c t) -> p c t", t=TJ)
                    y5 = yb[:, :].rearrange("p (c t) -> p c t", t=TJ)
                    e.tensor_tensor(y5, x5, rb, ALU.mult)
                    i = e.tensor_tensor(y5, y5, mrb, ALU.subtract)
                    e.inc(i, sVE, 1)
                    e.mark(sVE, ("p7y", j))

            def p7b_tensor():
                e = ET
                for j in range(NJ2):
                    for ci in range(24):
                        e.wait(("p7mol", j, ci))
                    for oc in range(8):
                        qq = j * 8 + oc
                        if qq >= 2:
                            e.wait(("p7ev", (qq - 2) // 8, (qq - 2) % 8))
                        pd = ps_dn[:, (oc % 2) * 512:(oc % 2) * 512 + TJ]
                        off = (j % 2) * 24 * TJ
                        n = 0
                        for (ah, bh) in ((1, 1), (1, 0), (0, 1)):
                            for k in range(24):
                                w = (wdq_h if ah else wdq_l)[:, oc * FF + k * 128:
                                                             oc * FF + (k + 1) * 128]
                                xs = (mo_h if bh else mo_l)[:, off + k * TJ:
                                                            off + (k + 1) * TJ]
                                i = e.matmul(pd, w, xs, start=(n == 0), stop=(n == 71))
                                n += 1
                        e.inc(i, sMM, 1)
                        e.mark(sMM, ("p7dn", j, oc))
                    e.wait(("p7sq", j))
                    if j >= 1:
                        e.wait(("p7lncp", j - 1))
                    e.wait(("p7x4all", j))
                    for k in range(8):
                        e.matmul(ps_r7a[0:1, 0:TJ], onesc_sb[:, :],
                                 x4b[:, k * TJ:(k + 1) * TJ], start=(k == 0), stop=(k == 7))
                        i = e.matmul(ps_r7b[0:1, 0:TJ], onesc_sb[:, :],
                                     sh7[:, k * TJ:(k + 1) * TJ], start=(k == 0),
                                     stop=(k == 7))
                    e.inc(i, sMM, 1)
                    e.mark(sMM, ("p7lnmm", j))
                    e.wait(("p7lnr", j))
                    e.matmul(ps_r7a[:, 0:TJ], onesr_sb[0:1, :], row(R_R, TJ),
                             start=True, stop=True)
                    i = e.matmul(ps_r7b[:, 0:TJ], onesr_sb[0:1, :], row(R_MR, TJ),
                                 start=True, stop=True)
                    e.inc(i, sMM, 1)
                    e.mark(sMM, ("p7lnbc", j))

            def p7b_gpsimd():
                e = EG
                for j in range(NJ2):
                    e.wait(("p7y", j))
                    e.dma(y_out.rearrange("(c p) t -> p c t", p=128)
                          [:, :, TJ * j:TJ * (j + 1)],
                          yb[:, :].rearrange("p (c t) -> p c t", t=TJ), sSP)
                    e.mark(sSP, ("p7ysp", j))
                e.wait_sem_now(sSP)

            run_phase([("sync", p7b_sync), ("scalar", p7b_scalar), ("vector", p7b_vector),
                       ("tensor", p7b_tensor), ("gpsimd", p7b_gpsimd)], "p7b")

        sc_wd.close()

    return nc


_PROG_CACHE = {}


def _get_prog(T):
    if T not in _PROG_CACHE:
        _PROG_CACHE[T] = build_program(T)
    return _PROG_CACHE[T]


def make_inmaps(inputs, T):
    x = np.ascontiguousarray(np.asarray(inputs["L0_attn_x"]))
    B = x.shape[0]
    wqkv = np.ascontiguousarray(np.asarray(inputs["L0_attn_qkv_w"]).T)
    wproj = np.ascontiguousarray(np.asarray(inputs["L0_attn_proj_w"]).T)
    wc1 = np.ascontiguousarray(np.asarray(inputs["L0_attn_conv1_w"]).T)
    wc2 = np.ascontiguousarray(np.asarray(inputs["L0_attn_conv2_w"]).T)
    wup = np.ascontiguousarray(np.asarray(inputs["L0_ffn_w_gate" if False else "L0_ffn_w_up"]))
    wgate = np.ascontiguousarray(np.asarray(inputs["L0_ffn_w_gate"]))
    wdown = np.ascontiguousarray(np.asarray(inputs["L0_ffn_w_down"]))
    vecs = np.zeros((6, C), np.float32)
    vecs[0] = np.asarray(inputs["L0_attn_scale1"])
    vecs[1] = np.asarray(inputs["L0_attn_scale2"])
    vecs[2] = np.asarray(inputs["L0_attn_gamma"])
    vecs[3] = np.asarray(inputs["L0_attn_proj_b"])
    vecs[4] = np.asarray(inputs["L0_attn_conv2_b"])
    vecs = np.ascontiguousarray(vecs.reshape(6, 8, 128).transpose(2, 0, 1).reshape(128, 48))
    c1b = np.asarray(inputs["L0_attn_conv1_b"], dtype=np.float32)
    c1b = np.ascontiguousarray(c1b.reshape(16, 128).T)
    shared = dict(wqkv=wqkv, wproj=wproj, wc1=wc1, wc2=wc2, wup=wup,
                  wgate=wgate, wdown=wdown, vecs=vecs, c1b=c1b,
                  wup_t=np.ascontiguousarray(wup.T),
                  wgate_t=np.ascontiguousarray(wgate.T),
                  wdown_t=np.ascontiguousarray(wdown.T))
    in_maps = []
    for b in range(B):
        m = dict(shared)
        m["x"] = np.ascontiguousarray(x[b].T)
        in_maps.append(m)
    return in_maps


def kernel(**inputs):
    T = int(np.asarray(inputs["L0_attn_x"]).shape[1])
    nc = _get_prog(T)
    in_maps = make_inmaps(inputs, T)
    res = run_bass_kernel_spmd(nc, in_maps, core_ids=list(range(8)))
    out = np.stack([np.asarray(r["y"]).T for r in res.results], axis=0)
    return out.astype(np.float32)
